# version 03
# baseline (speedup 1.0000x reference)
import contextlib
import numpy as np
import bass_rust
import concourse.bass as bass
import concourse.mybir as mybir
from concourse.bass_utils import run_bass_kernel_spmd

F32 = mybir.dt.float32
BF16 = mybir.dt.bfloat16
AF = mybir.ActivationFunctionType
ALU = mybir.AluOpType

B, N, D = 16, 4096, 1024
NCORE = 8
SPC = 2
TT = 512
NTILE = N // TT
NH, MF, DH = 16, 256, 64
LN_EPS = 1e-5
LN7 = -16.11809565095832
SEM_SHIFT = 0
C_TRIPS = (SPC, 4, 8, 4)
C_STAGE = 3
ENGS = ("sync", "tensor", "vector", "scalar", "gpsimd")


class LoopVar:
    __slots__ = ("lvl", "var")

    def __init__(self, lvl, var):
        self.lvl = lvl
        self.var = var


class Tok:
    __slots__ = ("c", "terms")

    def __init__(self, c, terms=()):
        self.c = c
        self.terms = tuple(terms)

    def __add__(self, k):
        return Tok(self.c + k, self.terms)


class Sem:
    def __init__(self, p, name, owner, step=1):
        self.p, self.name, self.owner, self.step = p, name, owner, step
        self.c = SEM_SHIFT
        self.stack = []
        if not p.dry:
            self.h = p.ctx.sem(name)
            if SEM_SHIFT:
                getattr(p.nc, owner).sem_inc(self.h, SEM_SHIFT)

    def tok(self, delta=0):
        return Tok(self.c + delta, [(lvl, pi) for (lvl, _s, pi) in self.stack if pi])


class P:
    MAXDEPTH = 4

    def __init__(self, dry, totals=None):
        self.dry = dry
        self.totals = totals if totals is not None else {}
        self.loop_seq = 0
        self.depth = 0
        self.sems = []
        self.r = NS()
        self.io = NS()
        self.regs = {}
        if not dry:
            self.ctx = Ctx()
            self.nc = self.ctx.nc
            for eng in ENGS:
                E = getattr(self.nc, eng)
                self.regs[eng] = {
                    "ri": [E.alloc_register(f"{eng}_ri{l}") for l in range(self.MAXDEPTH)],
                    "t0": E.alloc_register(f"{eng}_t0"),
                    "t1": E.alloc_register(f"{eng}_t1"),
                }

    def sem(self, owner, name, step=1):
        s = Sem(self, name, owner, step)
        self.sems.append(s)
        return s

    @contextlib.contextmanager
    def loop(self, n):
        idx = self.loop_seq
        self.loop_seq += 1
        lvl = self.depth
        self.depth += 1

        if self.dry:
            snap = [(s, s.c) for s in self.sems]
            for s in self.sems:
                s.stack.append([lvl, s.c, 0])
            yield LoopVar(lvl, 0)
            pis = {}
            for s, c0 in snap:
                pi = s.c - c0
                pis[s.name] = pi
                s.stack.pop()
                s.c = c0 + n * pi
            self.totals[idx] = pis
        else:
            pis = self.totals[idx]
            nc = self.nc
            loop_start = f"ploop{idx}_loop"
            loop_end = f"ploop{idx}_end"
            ris = bass_rust.RegisterHandles(
                tuple(self.regs[eng]["ri"][lvl] for eng in ENGS))
            for eng in ENGS:
                E = getattr(nc, eng)
                E.reg_mov(self.regs[eng]["ri"][lvl], -1)
            nc.br(loop_start)
            with nc.body(loop_start):
                for eng in ENGS:
                    E = getattr(nc, eng)
                    E.reg_add(self.regs[eng]["ri"][lvl],
                              self.regs[eng]["ri"][lvl], 1)
                snap = [(s, s.c) for s in self.sems]
                for s in self.sems:
                    s.stack.append([lvl, s.c, pis.get(s.name, 0)])
                yield LoopVar(lvl, None)
                for s, c0 in snap:
                    got = s.c - c0
                    assert got == pis.get(s.name, 0), \
                        f"loop {idx} sem {s.name}: emitted {got} != dry {pis.get(s.name, 0)}"
                    s.stack.pop()
                    s.c = c0 + n * got
                nc.br_lt(ris, n - 1, on_true=loop_start, on_false=loop_end)
            nc.switch_bb(loop_end)
        self.depth -= 1

    def _regval(self, eng, const, terms):
        E = getattr(self.nc, eng)
        rg = self.regs[eng]
        first = True
        for lvl, coef in terms:
            if coef == 0:
                continue
            dst = rg["t0"] if first else rg["t1"]
            E.reg_mul(dst, rg["ri"][lvl], coef)
            if not first:
                E.reg_add(rg["t0"], rg["t0"], rg["t1"])
            first = False
        assert not first
        E.reg_add(rg["t0"], rg["t0"], const)
        return rg["t0"]

    def wait(self, eng, sem, tok):
        if self.dry:
            return
        terms = [(l, c) for (l, c) in tok.terms if c]
        if not terms:
            assert tok.c >= 0
            getattr(self.nc, eng).wait_ge(sem.h, tok.c)
        else:
            reg = self._regval(eng, tok.c, terms)
            getattr(self.nc, eng).wait_ge(sem.h, reg)

    def off(self, eng, const, terms):
        E = getattr(self.nc, eng)
        rg = self.regs[eng]
        terms = [(lv, c) for (lv, c) in terms if c]
        if not terms:
            return const
        o = E.alloc_register()
        first = True
        for lv, coef in terms:
            if first:
                E.reg_mul(o, rg["ri"][lv.lvl], coef)
            else:
                E.reg_mul(rg["t1"], rg["ri"][lv.lvl], coef)
                E.reg_add(o, o, rg["t1"])
            first = False
        E.reg_add(o, o, const)
        return E.snap(o, donate=True)

    def op(self, eng, sem, fn, k=1):
        if not self.dry:
            ins = fn(getattr(self.nc, eng))
            ins.then_inc(sem.h, sem.step * k)
        sem.c += sem.step * k
        return sem.tok()

    def raw(self, eng, fn):
        if not self.dry:
            fn(getattr(self.nc, eng))

    def vchain(self, eng, sem, fns):
        t = None
        for fn in fns:
            t = self.op(eng, sem, fn)
            self.wait(eng, sem, t)
        return t

    def finish(self):
        for s in self.sems:
            if s.step == 16:
                self.wait(s.owner, s, s.tok())

    def barrier(self, sems=None):
        sems = sems if sems is not None else self.sems
        toks = [(s, s.tok()) for s in sems if s.step == 1]
        for eng in ENGS:
            for s, t in toks:
                if s.owner != eng:
                    self.wait(eng, s, t)
        return {s.name: t for s, t in toks}


class Ctx:
    def __init__(self):
        self.nc = bass.Bass("TRN2", target_bir_lowering=False, detect_race_conditions=False)
        self.st = contextlib.ExitStack()

    def sem(self, name):
        return self.st.enter_context(self.nc.semaphore(name))

    def sbuf(self, name, shape, dtype):
        return self.st.enter_context(self.nc.sbuf_tensor(name, shape, dtype))

    def psum(self, name, shape, dtype=F32):
        return self.st.enter_context(self.nc.psum_tensor(name, shape, dtype))


def build(program_fn, *args):
    p1 = P(dry=True)
    program_fn(p1, *args)
    p2 = P(dry=False, totals=p1.totals)
    program_fn(p2, *args)
    return p2


class NS(dict):
    __getattr__ = dict.__getitem__
    __setattr__ = dict.__setitem__


class Arena:

    def __init__(self, nc, st):
        self.nc, self.st = nc, st
        self.addr = 0
        self.size = 0
        self.tensors = []

    def sbuf(self, name, shape, dtype):
        per = 1
        for s in shape[1:]:
            per *= s
        per *= mybir.dt.size(dtype)
        a = self.addr
        self.addr = (a + per + 31) & ~31
        self.size = max(self.size, self.addr)
        h = self.nc.virtual_sbuf_tensor(name, shape, dtype)
        self.tensors.append((h, a))
        return h

    def reset(self):
        self.addr = 0

    def allocate(self):
        handle = self.st.enter_context(
            self.nc.sbuf_tensor("arena", [128, self.size], mybir.dt.uint8))
        off = self.nc.lookup_mloc(handle).addr
        for h, a in self.tensors:
            self.nc._tensor(h.name, h.shape, h.dtype, type="SB")
            m = self.nc.lookup_mloc(h)
            m.allocated = True
            m.addr = off + a


def set_stage(p, k):
    if not p.dry:
        p.r = p.sr[k]


def stage_barrier(p, S):
    for s in p.sems:
        if s.step == 16 and s.name not in ("gP0", "gP1"):
            p.wait(s.owner, s, s.tok())
    R = lambda n: p.r[n] if not p.dry else None
    io = p.io
    incs = []
    t = p.op("tensor", S.tQ, lambda e: e.matmul(
        R("pA")[0:1, 0:1], R("onesbf")[:, 0:1], R("onesbf")[:, 0:1],
        start=True, stop=True))
    incs.append(("tensor", S.tQ, t))
    t = p.op("vector", S.cV, lambda e: e.memset(R("lnrow")[0:1, 0:1], 0.0))
    incs.append(("vector", S.cV, t))
    t = p.op("scalar", S.cS, lambda e: e.copy(R("lnrow")[0:1, 2:3], R("lnrow")[0:1, 3:4]))
    incs.append(("scalar", S.cS, t))
    t = p.op("sync", S.oX, lambda e: e.dma_start(io.bsc[0:1, :], io.bsc[1:2, :]))
    incs.append(("sync", S.oX, t))
    t = p.op("gpsimd", S.gX, lambda e: e.dma_start(io.bsc[2:3, :], io.bsc[3:4, :]))
    incs.append(("gpsimd", S.gX, t))
    for eng in ENGS:
        for o, sem, tk in incs:
            if o != eng:
                p.wait(eng, sem, tk)


def emit_ln_store(p, sems, y0_toks, store_fn, store_sem, yb=("stk", 0), y2b=("stk", 4096)):
    tQ, cS, cV = sems["tQ"], sems["cS"], sems["cV"]
    R = lambda n: p.r[n] if not p.dry else None
    ybn, ybo = yb
    y2n, y2o = y2b
    cast_toks, sq_toks = [], []
    for jc in range(8):
        ct = p.op("vector", cV, lambda e, jc=jc: e.tensor_copy(
            R(ybn)[:, ybo + jc * 512: ybo + (jc + 1) * 512], R("y0")[:, jc * 512:(jc + 1) * 512]))
        p.wait("scalar", cV, y0_toks[jc])
        st = p.op("scalar", cS, lambda e, jc=jc: e.activation(
            R(y2n)[:, y2o + jc * 512: y2o + (jc + 1) * 512],
            R("y0")[:, jc * 512:(jc + 1) * 512], AF.Square))
        cast_toks.append(ct)
        sq_toks.append(st)
    p.wait("tensor", cV, cast_toks[-1])
    p.wait("tensor", cS, sq_toks[-1])
    p.op("tensor", tQ, lambda e: [
        e.matmul(R("pC")[0:1, :], R("onesbf")[:, :],
                 R(ybn)[:, ybo + kc * 512: ybo + (kc + 1) * 512],
                 start=(kc == 0), stop=(kc == 7)) for kc in range(8)][-1])
    t2 = p.op("tensor", tQ, lambda e: [
        e.matmul(R("pD")[0:1, :], R("onesbf")[:, :],
                 R(y2n)[:, y2o + kc * 512: y2o + (kc + 1) * 512],
                 start=(kc == 0), stop=(kc == 7)) for kc in range(8)][-1])
    p.wait("scalar", tQ, t2)
    p.op("scalar", cS, lambda e: e.mul(R("lnrow")[0:1, 0:512], R("pC")[0:1, :], 1.0 / 1024))
    p.op("scalar", cS, lambda e: e.mul(R("lnrow")[0:1, 512:1024], R("pD")[0:1, :], 1.0 / 1024))
    mt = p.op("scalar", cS, lambda e: e.activation(
        R("lnrow")[0:1, 1024:1536], R("lnrow")[0:1, 0:512], AF.Square))
    p.wait("vector", cS, mt)
    vt = p.op("vector", cV, lambda e: [
        e.tensor_sub(R("lnrow")[0:1, 1536:2048], R("lnrow")[0:1, 512:1024],
                     R("lnrow")[0:1, 1024:1536]),
        e.tensor_scalar(R("lnrow")[0:1, 1536:2048], R("lnrow")[0:1, 1536:2048],
                        LN_EPS, None, ALU.add)][-1])
    p.wait("scalar", cV, vt)
    sdt = p.op("scalar", cS, lambda e: e.activation(
        R("lnrow")[0:1, 2048:2560], R("lnrow")[0:1, 1536:2048], AF.Sqrt))
    p.wait("vector", cS, sdt)
    rst = p.op("vector", cV, lambda e: e.reciprocal(
        R("lnrow")[0:1, 2560:3072], R("lnrow")[0:1, 2048:2560]))
    p.wait("tensor", cV, rst)
    bt = p.op("tensor", tQ, lambda e: [
        e.matmul(R("pE")[:, :], R("ones128")[0:1, :], R("lnrow")[0:1, 0:512], start=True, stop=True),
        e.matmul(R("pF")[:, :], R("ones128")[0:1, :], R("lnrow")[0:1, 2560:3072], start=True, stop=True)][-1])
    p.wait("scalar", tQ, bt)
    cbt = p.op("scalar", cS, lambda e: [
        e.copy(R("mbr")[:, :], R("pE")[:, :]),
        e.copy(R("rbr")[:, :], R("pF")[:, :])][-1])
    p.wait("vector", cS, cbt)
    xt_ = None
    for jc in range(8):
        xt_ = p.op("vector", cV, lambda e, jc=jc: [
            e.tensor_sub(R("xout")[:, jc * 512:(jc + 1) * 512],
                         R("y0")[:, jc * 512:(jc + 1) * 512], R("mbr")[:, :]),
            e.tensor_mul(R("xout")[:, jc * 512:(jc + 1) * 512],
                         R("xout")[:, jc * 512:(jc + 1) * 512], R("rbr")[:, :])][-1])
    p.wait("scalar", cV, xt_)
    p.op("scalar", store_sem, lambda e: store_fn(e))


def emit_l0attn(p, S):
    gX, sX, oX, vX = S.gX, S.sX, S.oX, S.vX
    tQ, tV, cS, cV = S.tQ, S.tV, S.cS, S.cV
    sems = dict(tQ=tQ, cS=cS, cV=cV)

    R = lambda n: p.r[n] if not p.dry else None
    io = p.io

    for kc in range(8):
        p.op("gpsimd", gX, lambda e, kc=kc: e.dma_start(
            R("wqkvb")[:, kc * 3072:(kc + 1) * 3072], io.wqkv[kc * 128:(kc + 1) * 128, :]))
        p.op("gpsimd", gX, lambda e, kc=kc: e.dma_start(
            R("woutb")[:, kc * 1024:(kc + 1) * 1024], io.wout[kc * 128:(kc + 1) * 128, :]))
    p.op("sync", sX, lambda e: e.dma_start(R("bqkv")[:, :], io.bqkv[:, :]))
    p.op("sync", sX, lambda e: e.dma_start(R("bout")[:, :], io.bout[:, :]))
    p.op("sync", sX, lambda e: e.dma_start(R("bqr")[0:1, :], io.bqv[0:1, :]))
    p.raw("gpsimd", lambda e: e.memset(R("omega")[:, :], -0.5))
    p.op("gpsimd", gX, lambda e: e.dma_start(R("omega")[0:64, :], io.omega[:, :]))
    p.raw("gpsimd", lambda e: e.memset(R("vsb")[:, :], 1.0))
    ones_tok = p.op("vector", cV, lambda e: [
        e.memset(R("ones128")[:, :], 1.0),
        e.memset(R("onesbf")[:, :], 1.0)][-1])
    p.wait("tensor", gX, gX.tok())
    p.wait("tensor", sX, sX.tok())
    p.wait("tensor", cV, ones_tok)
    bt = p.op("tensor", tQ, lambda e: [
        e.matmul(R("pA")[:, :], R("ones128")[0:1, :], R("bqr")[0:1, 0:512], start=True, stop=True),
        e.matmul(R("pB")[:, :], R("ones128")[0:1, :], R("bqr")[0:1, 512:1024], start=True, stop=True)][-1])
    p.wait("scalar", tQ, bt)
    p.op("scalar", cS, lambda e: [
        e.copy(R("vb")[:, 0:512], R("pA")[:, :]),
        e.copy(R("vb")[:, 512:1024], R("pB")[:, :])][-1])

    with p.loop(SPC) as s_lv:
        p.barrier()
        p.op("vector", cV, lambda e: e.memset(R("kvacc")[:, :], 0.0))

        with p.loop(NTILE) as ti:
            p.barrier()
            p.op("gpsimd", gX, lambda e: e.dma_start(
                bass.AP(R("xbt"), 0, [[4096, 128], [512, 8], [1, TT]]),
                bass.AP(io.x, p.off("gpsimd", 0, [(s_lv, 1024 * 4096), (ti, TT)]),
                        [[4096, 128], [128 * 4096, 8], [1, TT]])))
            p.wait("tensor", gX, gX.tok())

            ev = []
            for j in range(8):
                ps = "pA" if j % 2 == 0 else "pB"
                if j >= 2:
                    p.wait("tensor", cS, ev[j - 2])
                mt = p.op("tensor", tQ, lambda e, j=j, ps=ps: [
                    e.matmul(R(ps)[:, :],
                             R("wqkvb")[:, kc * 3072 + 1024 + j * 128: kc * 3072 + 1024 + (j + 1) * 128],
                             R("xbt")[:, kc * 512:(kc + 1) * 512],
                             start=(kc == 0), stop=(kc == 7)) for kc in range(8)][-1])
                p.wait("scalar", tQ, mt)
                h0, h1 = 2 * j, 2 * j + 1
                et = p.op("scalar", cS, lambda e, j=j, ps=ps, h0=h0, h1=h1: [
                    e.activation(R("stk")[0:64, h0 * 512:(h0 + 1) * 512], R(ps)[0:64, :],
                                 AF.Identity, bias=R("bqkv")[0:64, 8 + j: 9 + j]),
                    e.activation(R("stk")[64:128, h0 * 512:(h0 + 1) * 512], R(ps)[0:64, :],
                                 AF.Square, bias=R("bqkv")[0:64, 8 + j: 9 + j]),
                    e.activation(R("stk")[0:64, h1 * 512:(h1 + 1) * 512], R(ps)[64:128, :],
                                 AF.Identity, bias=R("bqkv")[64:128, 8 + j: 9 + j]),
                    e.activation(R("stk")[64:128, h1 * 512:(h1 + 1) * 512], R(ps)[64:128, :],
                                 AF.Square, bias=R("bqkv")[64:128, 8 + j: 9 + j])][-1])
                ev.append(et)

            vev = []
            for idx in range(8):
                sc, grp = idx // 2, idx % 2
                ps = "pC" if idx % 2 == 0 else "pD"
                if idx >= 2:
                    p.wait("tensor", cV, vev[idx - 2])
                mt = p.op("tensor", tQ, lambda e, sc=sc, grp=grp, ps=ps: [
                    e.matmul(R(ps)[:, :],
                             R("xbt")[:, kc * 512 + sc * 128: kc * 512 + (sc + 1) * 128],
                             R("wqkvb")[:, kc * 3072 + 2048 + grp * 512: kc * 3072 + 2048 + (grp + 1) * 512],
                             start=(kc == 0), stop=(kc == 7)) for kc in range(8)][-1])
                p.wait("vector", tQ, mt)

                def vevac(e, sc=sc, grp=grp, ps=ps):
                    dst = bass.AP(R("vsb"), (sc * 16 + grp * 8) * 65,
                                  [[4 * 16 * 65, 128], [65, 8], [1, 64]])
                    src = bass.AP(R(ps), 0, [[512, 128], [64, 8], [1, 64]])
                    bia = bass.AP(R("vb"), grp * 512, [[1024, 128], [64, 8], [1, 64]])
                    return e.tensor_add(dst, src, bia)
                vev.append(p.op("vector", cV, vevac))
            p.wait("tensor", cV, vev[-1])

            kvadd = []
            exp_toks = {}
            for h in range(16):
                if h >= 1:
                    p.wait("tensor", cV, kvadd[h - 1])
                p.wait("tensor", cS, ev[h // 2])
                kt = None
                for sc in range(4):
                    pf = "pE" if sc % 2 == 0 else "pF"
                    if sc >= 2:
                        p.wait("tensor", cS, exp_toks[(h, sc - 2)])
                    elif h >= 1:
                        p.wait("tensor", cS, exp_toks[(h - 1, sc + 2)])
                    ft = p.op("tensor", tQ, lambda e, h=h, sc=sc, pf=pf: e.matmul(
                        R(pf)[:, 0:256], R("stk")[:, h * 512 + sc * 128: h * 512 + (sc + 1) * 128],
                        R("omega")[:, :], start=True, stop=True))
                    p.wait("scalar", tQ, ft)
                    p.wait("scalar", tV, tV.tok())
                    et = p.op("scalar", cS, lambda e, sc=sc, pf=pf: e.activation(
                        R("kpsb")[:, (sc % 2) * 512: (sc % 2) * 512 + 256], R(pf)[:, 0:256], AF.Exp))
                    exp_toks[(h, sc)] = et
                    p.wait("tensor", cS, et)
                    kt = p.op("tensor", tV, lambda e, h=h, sc=sc: [
                        e.matmul(R(("pG", "pH")[mc])[:, 0:65],
                                 R("kpsb")[:, (sc % 2) * 512 + mc * 128: (sc % 2) * 512 + mc * 128 + 128],
                                 R("vsb")[:, (sc * 16 + h) * 65: (sc * 16 + h + 1) * 65],
                                 start=(sc == 0), stop=(sc == 3)) for mc in range(2)][-1])
                p.wait("vector", tV, kt)
                kvadd.append(p.op("vector", cV, lambda e, h=h: [
                    e.tensor_add(R("kvacc")[:, h * 130: h * 130 + 65],
                                 R("kvacc")[:, h * 130: h * 130 + 65], R("pG")[:, 0:65]),
                    e.tensor_add(R("kvacc")[:, h * 130 + 65: h * 130 + 130],
                                 R("kvacc")[:, h * 130 + 65: h * 130 + 130], R("pH")[:, 0:65])][-1]))

        kvcast = p.op("vector", cV, lambda e: e.tensor_copy(R("kvbf")[:, :], R("kvacc")[:, :]))

        with p.loop(NTILE) as ti:
            p.barrier()
            p.wait("vector", oX, oX.tok())
            p.op("gpsimd", gX, lambda e: e.dma_start(
                bass.AP(R("xbt"), 0, [[4096, 128], [512, 8], [1, TT]]),
                bass.AP(io.x, p.off("gpsimd", 0, [(s_lv, 1024 * 4096), (ti, TT)]),
                        [[4096, 128], [128 * 4096, 8], [1, TT]])))
            p.op("scalar", vX, lambda e: e.dma_start(
                bass.AP(R("xt"), 0, [[4096, 128], [512, 8], [1, TT]]),
                bass.AP(io.x, p.off("scalar", 0, [(s_lv, 1024 * 4096), (ti, TT)]),
                        [[4096, 128], [128 * 4096, 8], [1, TT]])))
            xload32 = vX.tok()
            p.wait("tensor", gX, gX.tok())

            ev = []
            for j in range(8):
                ps = "pA" if j % 2 == 0 else "pB"
                if j >= 2:
                    p.wait("tensor", cS, ev[j - 2])
                mt = p.op("tensor", tQ, lambda e, j=j, ps=ps: [
                    e.matmul(R(ps)[:, :],
                             R("wqkvb")[:, kc * 3072 + j * 128: kc * 3072 + (j + 1) * 128],
                             R("xbt")[:, kc * 512:(kc + 1) * 512],
                             start=(kc == 0), stop=(kc == 7)) for kc in range(8)][-1])
                p.wait("scalar", tQ, mt)
                h0, h1 = 2 * j, 2 * j + 1
                et = p.op("scalar", cS, lambda e, j=j, ps=ps, h0=h0, h1=h1: [
                    e.activation(R("stk")[0:64, h0 * 512:(h0 + 1) * 512], R(ps)[0:64, :],
                                 AF.Identity, bias=R("bqkv")[0:64, j: j + 1]),
                    e.activation(R("stk")[64:128, h0 * 512:(h0 + 1) * 512], R(ps)[0:64, :],
                                 AF.Square, bias=R("bqkv")[0:64, j: j + 1]),
                    e.activation(R("stk")[0:64, h1 * 512:(h1 + 1) * 512], R(ps)[64:128, :],
                                 AF.Identity, bias=R("bqkv")[64:128, j: j + 1]),
                    e.activation(R("stk")[64:128, h1 * 512:(h1 + 1) * 512], R(ps)[64:128, :],
                                 AF.Square, bias=R("bqkv")[64:128, j: j + 1])][-1])
                ev.append(et)

            ho_toks, nd_toks, rt_toks = [], [], []
            expb = {}
            for h in range(16):
                if h >= 1:
                    p.wait("tensor", cS, expb[h - 1])
                p.wait("tensor", cS, ev[h // 2])
                qt = p.op("tensor", tQ, lambda e, h=h: [
                    e.matmul(R(("pC", "pD")[mc])[:, :], R("omega")[:, mc * 128:(mc + 1) * 128],
                             R("stk")[:, h * 512:(h + 1) * 512], start=True, stop=True)
                    for mc in range(2)][-1])
                p.wait("scalar", tQ, qt)
                if h >= 1:
                    p.wait("scalar", tV, nd_toks[h - 1])
                et = p.op("scalar", cS, lambda e: [
                    e.activation(R("kpsb")[:, mc * 512:(mc + 1) * 512], R(("pC", "pD")[mc])[:, :],
                                 AF.Exp) for mc in range(2)][-1])
                expb[h] = et
                p.wait("tensor", cS, et)
                if h == 0:
                    p.wait("tensor", cV, kvcast)
                if h >= 1:
                    p.wait("tensor", cV, ho_toks[h - 1])
                nt = p.op("tensor", tV, lambda e, h=h: [
                    e.matmul(R("pE")[0:65, :], R("kvbf")[:, h * 130 + mc * 65: h * 130 + (mc + 1) * 65],
                             R("kpsb")[:, mc * 512:(mc + 1) * 512],
                             start=(mc == 0), stop=(mc == 1)) for mc in range(2)][-1])
                nd_toks.append(nt)
                p.wait("vector", tV, nt)
                dt = p.op("vector", cV, lambda e: [
                    e.tensor_scalar(R("lnrow")[0:1, 0:512], R("pE")[64:65, :], 1e-6, None, ALU.add),
                    e.reciprocal(R("lnrow")[0:1, 512:1024], R("lnrow")[0:1, 0:512])][-1])
                p.wait("tensor", cV, dt)
                if h >= 1:
                    p.wait("tensor", cS, rt_toks[h - 1])
                bt2 = p.op("tensor", tQ, lambda e: e.matmul(
                    R("pF")[0:64, :], R("ones128")[0:1, 0:64], R("lnrow")[0:1, 512:1024],
                    start=True, stop=True))
                p.wait("scalar", tQ, bt2)
                if h >= 1:
                    p.wait("scalar", cV, ho_toks[h - 1])
                rt = p.op("scalar", cS, lambda e: e.copy(R("rbsb")[:, :], R("pF")[0:64, :]))
                rt_toks.append(rt)
                p.wait("vector", cS, rt)
                ho_toks.append(p.op("vector", cV, lambda e, h=h: e.tensor_mul(
                    R("hosb")[(h % 2) * 64:(h % 2) * 64 + 64, (h // 2) * 512:(h // 2 + 1) * 512],
                    R("pE")[0:64, :], R("rbsb")[:, :])))

            p.wait("tensor", cV, ho_toks[-1])
            y0_toks, wtoks = [], []
            for j in range(8):
                ps = "pG" if j % 2 == 0 else "pH"
                wo = "wo0" if j % 2 == 0 else "wo1"
                if j >= 2:
                    p.wait("tensor", cS, wtoks[j - 2])
                mt = p.op("tensor", tQ, lambda e, j=j, ps=ps: [
                    e.matmul(R(ps)[:, :],
                             R("woutb")[:, kc * 1024 + j * 128: kc * 1024 + (j + 1) * 128],
                             R("hosb")[:, kc * 512:(kc + 1) * 512],
                             start=(kc == 0), stop=(kc == 7)) for kc in range(8)][-1])
                p.wait("scalar", tQ, mt)
                wt = p.op("scalar", cS, lambda e, j=j, ps=ps, wo=wo: e.activation(
                    R(wo)[:, :], R(ps)[:, :], AF.Identity, bias=R("bout")[:, j: j + 1]))
                wtoks.append(wt)
                p.wait("vector", cS, wt)
                if j == 0:
                    p.wait("vector", vX, xload32)
                y0_toks.append(p.op("vector", cV, lambda e, j=j, wo=wo: e.tensor_add(
                    R("y0")[:, j * 512:(j + 1) * 512], R(wo)[:, :], R("xt")[:, j * 512:(j + 1) * 512])))

            def store_fn(e, s_lv=s_lv, ti=ti):
                return e.dma_start(
                    bass.AP(io.x1, p.off("scalar", 0, [(s_lv, 1024 * 4096), (ti, TT)]),
                            [[4096, 128], [128 * 4096, 8], [1, TT]]),
                    bass.AP(R("xout"), 0, [[4096, 128], [512, 8], [1, TT]]))

            emit_ln_store(p, sems, y0_toks, store_fn, oX)


def emit_l0ffn(p, S):
    gX, gC, sW, sX, oX, vX = S.gX, S.gC, S.sW, S.sX, S.oX, S.vX
    gP0 = S.gP0
    tQ, cS, cV = S.tQ, S.cS, S.cV
    sems = dict(tQ=tQ, cS=cS, cV=cV)
    CSZ = 2048 * 512
    TR_S, TR_TG, TR_E, TR_NC = C_TRIPS

    R = lambda n: p.r[n] if not p.dry else None
    io = p.io

    for c in range(16):
        p.op("gpsimd", gP0, lambda e, c=c: e.dma_start(
            io.wbf0[c * 8192:(c + 1) * 8192, :], io.w1w2_0[c * 8192:(c + 1) * 8192, :]))
    pc0 = gP0.tok()

    p.op("gpsimd", gC, lambda e: e.iota(R("it1")[:, :], [[1, 128]], channel_multiplier=0,
                                        allow_small_or_imprecise_dtypes=True))
    it = p.op("gpsimd", gC, lambda e: e.iota(R("it2")[:, :], [[0, 128]], channel_multiplier=1,
                                             allow_small_or_imprecise_dtypes=True))
    p.wait("vector", gC, it)
    ones_tok = p.op("vector", cV, lambda e: [
        e.tensor_tensor(R("identf")[:, :], R("it1")[:, :], R("it2")[:, :], ALU.is_equal),
        e.memset(R("ones128")[:, :], 1.0),
        e.memset(R("onesbf")[:, :], 1.0)][-1])
    p.op("gpsimd", gX, lambda e: e.dma_start(R("wrb")[:, :], io.wr0[:, :]))

    with p.loop(TR_S) as s_lv:
        with p.loop(TR_TG) as tg_lv:
            p.barrier()
            p.wait("vector", oX, oX.tok())
            p.wait("scalar", oX, oX.tok())
            p.wait("sync", gP0, pc0)
            p.wait("sync", sW, sW.tok())
            for par in range(2):
                p.op("sync", sW, lambda e, par=par: e.dma_start(
                    bass.AP(R("wb1_" + str(par)), 0, [[4096, 128], [512, 8], [1, 512]]),
                    bass.AP(io.wbf0, par * CSZ, [[512, 128], [128 * 512, 8], [1, 512]])))
                p.op("sync", sW, lambda e, par=par: e.dma_start(
                    bass.AP(R("wb2_" + str(par)), 0, [[4096, 128], [1024, 4], [1, 1024]]),
                    bass.AP(io.wbf0, par * CSZ + 1024 * 512, [[1024, 128], [128 * 1024, 4], [1, 1024]])))
            p.op("gpsimd", gX, lambda e: e.dma_start(
                bass.AP(R("xbf"), 0, [[8192, 128], [1024, 8], [1, 1024]]),
                bass.AP(io.x1, p.off("gpsimd", 0, [(s_lv, 1024 * 4096), (tg_lv, 1024)]),
                        [[4096, 128], [128 * 4096, 8], [1, 1024]])))
            xbf_tok = gX.tok()
            p.op("vector", cV, lambda e: e.memset(R("outacc")[:, :], 0.0))

            ROUTER = C_STAGE >= 1
            EXPERTS = C_STAGE >= 2
            p.wait("tensor", gX, xbf_tok)
            p.wait("tensor", cV, ones_tok)
            lg_toks = []
            for t2 in range(2 if ROUTER else 0):
                plg = ("pA", "pB")[t2]
                mt = p.op("tensor", tQ, lambda e, t2=t2, plg=plg: [
                    e.matmul(R(plg)[0:8, :], R("wrb")[:, kc * 8:(kc + 1) * 8],
                             R("xbf")[:, kc * 1024 + t2 * 512: kc * 1024 + t2 * 512 + 512],
                             start=(kc == 0), stop=(kc == 7)) for kc in range(8)][-1])
                p.wait("scalar", tQ, mt)
                lg_toks.append(p.op("scalar", cS, lambda e, t2=t2, plg=plg: e.copy(
                    R("lg")[0:8, t2 * 512:(t2 + 1) * 512], R(plg)[0:8, :])))
            c_toks = []
            rtm_toks = []
            for sub in range(8 if ROUTER else 0):
                p.wait("tensor", cS, lg_toks[1])
                if sub >= 1:
                    p.wait("tensor", cS, rtm_toks[sub - 1])
                tt = p.op("tensor", tQ, lambda e, sub=sub: e.transpose(
                    R("pA")[0:128, 0:8], R("lg")[0:8, sub * 128:(sub + 1) * 128],
                    R("identf")[0:8, 0:8]))
                p.wait("scalar", tQ, tt)
                if sub >= 1:
                    p.wait("scalar", cV, c_toks[sub - 1])
                rt = p.op("scalar", cS, lambda e: e.copy(R("rtm")[:, :], R("pA")[0:128, 0:8]))
                rtm_toks.append(rt)
                p.wait("vector", cS, rt)
                p.wait("scalar" if False else "vector", cS, rt)
                t = p.op("vector", cV, lambda e: e.reduce_max(
                    R("srow")[:, 0:1], R("rtm")[:, :], mybir.AxisListType.X))
                p.wait("vector", cV, t)
                ct = p.op("vector", cV, lambda e, sub=sub: e.tensor_scalar(
                    R("ctm")[:, sub * 8:(sub + 1) * 8], R("rtm")[:, :],
                    R("srow")[:, 0:1], None, ALU.subtract))
                p.wait("scalar", cV, ct)
                et = p.op("scalar", cS, lambda e, sub=sub: e.activation(
                    R("ctm")[:, sub * 8:(sub + 1) * 8], R("ctm")[:, sub * 8:(sub + 1) * 8], AF.Exp))
                p.wait("vector", cS, et)
                c_toks.append(p.vchain("vector", cV, [
                    lambda e, sub=sub: e.reduce_max(R("srow")[:, 3:4],
                                                    R("ctm")[:, sub * 8:(sub + 1) * 8],
                                                    mybir.AxisListType.X),
                    lambda e, sub=sub: e.tensor_scalar(R("rtm")[:, :],
                                                       R("ctm")[:, sub * 8:(sub + 1) * 8],
                                                       R("srow")[:, 3:4], None, ALU.is_lt),
                    lambda e, sub=sub: e.tensor_mul(R("rtm")[:, :], R("rtm")[:, :],
                                                    R("ctm")[:, sub * 8:(sub + 1) * 8]),
                    lambda e: e.reduce_max(R("srow")[:, 4:5], R("rtm")[:, :],
                                           mybir.AxisListType.X),
                    lambda e: e.tensor_add(R("srow")[:, 5:6], R("srow")[:, 3:4],
                                           R("srow")[:, 4:5]),
                    lambda e: e.reciprocal(R("srow")[:, 6:7], R("srow")[:, 5:6]),
                    lambda e, sub=sub: e.tensor_scalar(R("rtm")[:, :],
                                                       R("ctm")[:, sub * 8:(sub + 1) * 8],
                                                       R("srow")[:, 4:5], None, ALU.is_ge),
                    lambda e, sub=sub: e.tensor_mul(R("rtm")[:, :], R("rtm")[:, :],
                                                    R("ctm")[:, sub * 8:(sub + 1) * 8]),
                    lambda e, sub=sub: e.tensor_scalar(R("ctm")[:, sub * 8:(sub + 1) * 8],
                                                       R("rtm")[:, :],
                                                       R("srow")[:, 6:7], None, ALU.mult),
                ]))
            cf_toks = []
            for sub in range(8 if ROUTER else 0):
                p.wait("tensor", cV, c_toks[sub])
                if sub >= 1:
                    p.wait("tensor", cS, cf_toks[sub - 1])
                tt = p.op("tensor", tQ, lambda e, sub=sub: e.transpose(
                    R("pB")[0:8, 0:128], R("ctm")[:, sub * 8:(sub + 1) * 8], R("identf")[:, :]))
                p.wait("scalar", tQ, tt)
                cf_toks.append(p.op("scalar", cS, lambda e, sub=sub: e.copy(
                    R("cfm")[0:8, sub * 128:(sub + 1) * 128], R("pB")[0:8, 0:128])))
            if ROUTER:
                p.wait("sync", cS, cf_toks[-1])
                cdr_tok = p.op("sync", oX, lambda e: e.dma_start(io.cdr0[:, :], R("cfm")[:, :]))
            else:
                cdr_tok = oX.tok()

            if EXPERTS:
                with p.loop(TR_E) as e_lv:
                    cv_start = cV.tok()
                    cs_start = cS.tok()
                    sw_start = sW.tok()
                    p.wait("gpsimd", oX, cdr_tok)
                    p.wait("gpsimd", cV, cv_start)
                    ceb_tok = p.op("gpsimd", gX, lambda e: e.dma_start(
                        R("ceb")[:, :],
                        bass.AP(io.cdr0, p.off("gpsimd", 0, [(e_lv, 1024)]), [[0, 128], [1, 1024]])))
                    p.wait("vector", gX, ceb_tok)
                    xs_toks = []
                    for kc in range(8):
                        for t2 in range(2):
                            xs_toks.append(p.op("vector", cV, lambda e, kc=kc, t2=t2: e.tensor_mul(
                                R("xs")[:, kc * 1024 + t2 * 512: kc * 1024 + t2 * 512 + 512],
                                R("xbf")[:, kc * 1024 + t2 * 512: kc * 1024 + t2 * 512 + 512],
                                R("ceb")[:, t2 * 512:(t2 + 1) * 512])))
                    if C_STAGE >= 3:
                        with p.loop(TR_NC) as nc_lv:
                            sw_it = sW.tok()
                            cs_it = cS.tok()
                            cv_it = cV.tok()
                            for par in range(2):
                                wb1 = "wb1_" + str(par)
                                wb2 = "wb2_" + str(par)
                                p.wait("tensor", sW, sw_it)
                                p.wait("tensor", cV, xs_toks[-1])
                                relu_toks, add_toks = [], []
                                mm1_tok = mm2_tok = None
                                for t2 in range(2):
                                    p.wait("tensor", cS, cs_it if t2 == 0 else relu_toks[-1])
                                    mm1_tok = p.op("tensor", tQ, lambda e, wb1=wb1, t2=t2: [
                                        e.matmul(R(("pC", "pD", "pE", "pF")[oc])[:, :],
                                                 R(wb1)[:, kc * 512 + oc * 128: kc * 512 + (oc + 1) * 128],
                                                 R("xs")[:, kc * 1024 + t2 * 512: kc * 1024 + t2 * 512 + 512],
                                                 start=(kc == 0), stop=(kc == 7))
                                        for oc in range(4) for kc in range(8)][-1])
                                    p.wait("scalar", tQ, mm1_tok)
                                    rt = p.op("scalar", cS, lambda e: [
                                        e.activation(R("hid")[:, oc * 512:(oc + 1) * 512],
                                                     R(("pC", "pD", "pE", "pF")[oc])[:, :], AF.Relu)
                                        for oc in range(4)][-1])
                                    relu_toks.append(rt)
                                    p.wait("tensor", cS, rt)
                                    for jc in range(8):
                                        pw = ("pG", "pH")[jc % 2]
                                        a_idx = t2 * 8 + jc
                                        if a_idx >= 2:
                                            p.wait("tensor", cV, add_toks[a_idx - 2])
                                        else:
                                            p.wait("tensor", cV, cv_it)
                                        mm2_tok = p.op("tensor", tQ, lambda e, wb2=wb2, jc=jc, pw=pw: [
                                            e.matmul(R(pw)[:, :],
                                                     R(wb2)[:, kc2 * 1024 + jc * 128: kc2 * 1024 + (jc + 1) * 128],
                                                     R("hid")[:, kc2 * 512:(kc2 + 1) * 512],
                                                     start=(kc2 == 0), stop=(kc2 == 3)) for kc2 in range(4)][-1])
                                        p.wait("vector", tQ, mm2_tok)
                                        at = p.op("vector", cV, lambda e, jc=jc, t2=t2, pw=pw: e.tensor_add(
                                            R("outacc")[:, jc * 1024 + t2 * 512: jc * 1024 + t2 * 512 + 512],
                                            R("outacc")[:, jc * 1024 + t2 * 512: jc * 1024 + t2 * 512 + 512],
                                            R(pw)[:, :]))
                                        add_toks = add_toks + [at] if False else add_toks
                                        add_toks.append(at)
                                p.wait("sync", tQ, mm1_tok)
                                p.wait("sync", tQ, mm2_tok)
                                p.wait("sync", gP0, pc0)
                                p.op("sync", sW, lambda e, par=par, wb1=wb1: e.dma_start(
                                    bass.AP(R(wb1), 0, [[4096, 128], [512, 8], [1, 512]]),
                                    bass.AP(io.wbf0, p.off("sync", (par + 2) * CSZ,
                                                           [(e_lv, 8 * CSZ), (nc_lv, 2 * CSZ)]),
                                            [[512, 128], [128 * 512, 8], [1, 512]])))
                                p.op("sync", sW, lambda e, par=par, wb2=wb2: e.dma_start(
                                    bass.AP(R(wb2), 0, [[4096, 128], [1024, 4], [1, 1024]]),
                                    bass.AP(io.wbf0, p.off("sync", (par + 2) * CSZ + 1024 * 512,
                                                           [(e_lv, 8 * CSZ), (nc_lv, 2 * CSZ)]),
                                            [[1024, 128], [128 * 1024, 4], [1, 1024]])))

            for t2 in range(2):
                p.wait("vector", cV, cV.tok())
                xt_tok = p.op("scalar", vX, lambda e, t2=t2: e.dma_start(
                    bass.AP(R("xt"), 0, [[4096, 128], [512, 8], [1, 512]]),
                    bass.AP(io.x1, p.off("scalar", t2 * 512, [(s_lv, 1024 * 4096), (tg_lv, 1024)]),
                            [[4096, 128], [128 * 4096, 8], [1, 512]])))
                p.wait("vector", vX, xt_tok)
                y0_toks = []
                for jc in range(8):
                    y0_toks.append(p.op("vector", cV, lambda e, jc=jc, t2=t2: e.tensor_add(
                        R("y0")[:, jc * 512:(jc + 1) * 512],
                        R("outacc")[:, jc * 1024 + t2 * 512: jc * 1024 + t2 * 512 + 512],
                        R("xt")[:, jc * 512:(jc + 1) * 512])))

                def store_fn(e, t2=t2):
                    return e.dma_start(
                        bass.AP(io.x2, p.off("scalar", t2 * 512, [(s_lv, 1024 * 4096), (tg_lv, 1024)]),
                                [[4096, 128], [128 * 4096, 8], [1, 512]]),
                        bass.AP(R("xout"), 0, [[4096, 128], [512, 8], [1, 512]]))

                emit_ln_store(p, sems, y0_toks, store_fn, oX)


def emit_rglru(p, S):
    gX, sX, oX, vX = S.gX, S.sX, S.oX, S.vX
    tQ, cS, cV = S.tQ, S.cS, S.cV
    sems = dict(tQ=tQ, cS=cS, cV=cV)

    R = lambda n: p.r[n] if not p.dry else None
    io = p.io

    p.op("gpsimd", gX, lambda e: e.dma_start(
        bass.AP(R("igwb"), 0, [[1024, 128], [128, 8], [1, 128]]),
        bass.AP(io.igw, 0, [[128, 128], [128 * 128, 8], [1, 128]])))
    p.op("gpsimd", gX, lambda e: e.dma_start(
        bass.AP(R("agwb"), 0, [[1024, 128], [128, 8], [1, 128]]),
        bass.AP(io.agw, 0, [[128, 128], [128 * 128, 8], [1, 128]])))
    p.op("sync", sX, lambda e: e.dma_start(R("igb")[:, :], io.igb[:, :]))
    p.op("sync", sX, lambda e: e.dma_start(R("agb")[:, :], io.agb[:, :]))
    apt = p.op("sync", sX, lambda e: e.dma_start(R("apv")[:, :], io.ap[:, :]))
    ones_tok = p.op("vector", cV, lambda e: [
        e.memset(R("ones128")[:, :], 1.0),
        e.memset(R("onesbf")[:, :], 1.0),
        e.memset(R("onesf")[:, :], 1.0)][-1])
    p.wait("scalar", sX, apt)
    t = p.op("scalar", cS, lambda e: e.activation(R("sp8")[:, 0:8], R("apv")[:, :], AF.Abs))
    p.wait("scalar", cS, t)
    t = p.op("scalar", cS, lambda e: e.activation(R("sp8")[:, 8:16], R("sp8")[:, 0:8], AF.Exp, scale=-1.0))
    p.wait("vector", cS, t)
    t = p.vchain("vector", cV, [
        lambda e: e.tensor_scalar(R("sp8")[:, 16:24], R("sp8")[:, 8:16], 1.0, None, ALU.add),
    ])
    p.wait("scalar", cV, t)
    t = p.op("scalar", cS, lambda e: e.activation(R("sp8")[:, 24:32], R("sp8")[:, 16:24], AF.Ln))
    p.wait("vector", cS, t)
    p.vchain("vector", cV, [
        lambda e: e.tensor_scalar(R("cvec")[:, :], R("apv")[:, :], 0.0, None, ALU.max),
        lambda e: e.tensor_add(R("cvec")[:, :], R("cvec")[:, :], R("sp8")[:, 24:32]),
        lambda e: e.tensor_scalar(R("cvec")[:, :], R("cvec")[:, :], -8.0, None, ALU.mult),
    ])

    with p.loop(SPC) as s_lv:
        p.barrier()
        p.op("vector", cV, lambda e: [
            e.memset(R("lc")[:, :], 0.0),
            e.memset(R("sc")[:, :], 0.0)][-1])
        with p.loop(NTILE) as ti:
            p.barrier()
            p.wait("vector", oX, oX.tok())
            p.op("gpsimd", gX, lambda e: e.dma_start(
                bass.AP(R("xbt"), 0, [[4096, 128], [512, 8], [1, TT]]),
                bass.AP(io.x2, p.off("gpsimd", 0, [(s_lv, 1024 * 4096), (ti, TT)]),
                        [[4096, 128], [128 * 4096, 8], [1, TT]])))
            xbt_tok = gX.tok()
            p.op("scalar", vX, lambda e: e.dma_start(
                bass.AP(R("xt"), 0, [[4096, 128], [512, 8], [1, TT]]),
                bass.AP(io.x2, p.off("scalar", 0, [(s_lv, 1024 * 4096), (ti, TT)]),
                        [[4096, 128], [128 * 4096, 8], [1, TT]])))
            xt_tok = vX.tok()
            p.wait("tensor", gX, xbt_tok)
            p.wait("tensor", cV, ones_tok)

            y0_toks = []
            sig_toks = []
            for h in range(8):
                pig = ("pA", "pB")[h % 2]
                pag = ("pC", "pD")[h % 2]
                if h >= 2:
                    p.wait("tensor", cS, sig_toks[h - 2])
                mt = p.op("tensor", tQ, lambda e, h=h, pig=pig, pag=pag: [
                    e.matmul(R(pig)[:, :], R("igwb")[:, h * 128:(h + 1) * 128],
                             R("xbt")[:, h * 512:(h + 1) * 512], start=True, stop=True),
                    e.matmul(R(pag)[:, :], R("agwb")[:, h * 128:(h + 1) * 128],
                             R("xbt")[:, h * 512:(h + 1) * 512], start=True, stop=True)][-1])
                p.wait("scalar", tQ, mt)
                if h >= 1:
                    p.wait("scalar", cV, y0_toks[h - 1])
                st = p.op("scalar", cS, lambda e, h=h, pig=pig, pag=pag: [
                    e.activation(R("gx")[:, :], R(pig)[:, :], AF.Sigmoid,
                                 bias=R("igb")[:, h: h + 1]),
                    e.activation(R("ga")[:, :], R(pag)[:, :], AF.Sigmoid,
                                 bias=R("agb")[:, h: h + 1])][-1])
                sig_toks.append(st)
                p.wait("vector", cS, st)
                t = p.op("vector", cV, lambda e, h=h: e.tensor_scalar(
                    R("la")[:, :], R("ga")[:, :], R("cvec")[:, h: h + 1], None, ALU.mult))
                p.wait("vector", cV, t)
                t = p.op("vector", cV, lambda e, h=h: e.tensor_tensor_scan(
                    R("li")[:, :], R("onesf")[:, :], R("la")[:, :],
                    R("lc")[:, h: h + 1], ALU.mult, ALU.add))
                p.wait("vector", cV, t)
                lvt = p.op("vector", cV, lambda e, h=h: [
                    e.tensor_copy(R("lc")[:, h: h + 1], R("li")[:, 511:512]),
                    e.tensor_sub(R("lx")[:, :], R("li")[:, :], R("la")[:, :]),
                    e.tensor_scalar(R("lx")[:, :], R("lx")[:, :], LN7, None, ALU.max)][-1])
                p.wait("scalar", cV, lvt)
                et = p.op("scalar", cS, lambda e: [
                    e.activation(R("pp")[:, :], R("lx")[:, :], AF.Exp),
                    e.activation(R("ppi")[:, :], R("lx")[:, :], AF.Exp, scale=-1.0),
                    e.activation(R("a2")[:, :], R("la")[:, :], AF.Exp, scale=2.0)][-1])
                p.wait("vector", cS, et)
                t = p.op("vector", cV, lambda e: [
                    e.tensor_scalar(R("ml")[:, :], R("a2")[:, :], -1.0, 1.0, ALU.mult, ALU.add),
                    e.tensor_scalar(R("ml")[:, :], R("ml")[:, :], 1e-7, None, ALU.max)][-1])
                p.wait("scalar", cV, t)
                sqt = p.op("scalar", cS, lambda e: e.activation(R("ml")[:, :], R("ml")[:, :], AF.Sqrt))
                p.wait("vector", cS, sqt)
                if h == 0:
                    p.wait("vector", vX, xt_tok)
                t = p.op("vector", cV, lambda e, h=h: [
                    e.tensor_mul(R("nx")[:, :], R("xt")[:, h * 512:(h + 1) * 512], R("gx")[:, :]),
                    e.tensor_mul(R("nx")[:, :], R("nx")[:, :], R("ml")[:, :]),
                    e.tensor_mul(R("nx")[:, :], R("nx")[:, :], R("ppi")[:, :])][-1])
                p.wait("vector", cV, t)
                t = p.op("vector", cV, lambda e, h=h: e.tensor_tensor_scan(
                    R("ss")[:, :], R("onesf")[:, :], R("nx")[:, :],
                    R("sc")[:, h: h + 1], ALU.mult, ALU.add))
                p.wait("vector", cV, t)
                y0_toks.append(p.op("vector", cV, lambda e, h=h: [
                    e.tensor_copy(R("sc")[:, h: h + 1], R("ss")[:, 511:512]),
                    e.tensor_mul(R("ss")[:, :], R("pp")[:, :], R("ss")[:, :]),
                    e.tensor_add(R("y0")[:, h * 512:(h + 1) * 512],
                                 R("xt")[:, h * 512:(h + 1) * 512], R("ss")[:, :])][-1]))

            def store_fn(e, s_lv=s_lv, ti=ti):
                return e.dma_start(
                    bass.AP(io.x3, p.off("scalar", 0, [(s_lv, 1024 * 4096), (ti, TT)]),
                            [[4096, 128], [128 * 4096, 8], [1, TT]]),
                    bass.AP(R("xout"), 0, [[4096, 128], [512, 8], [1, TT]]))

            emit_ln_store(p, sems, y0_toks, store_fn, oX)


def emit_l1ffn(p, S):
    gX, gC, sW, sX, oX, vX = S.gX, S.gC, S.sW, S.sX, S.oX, S.vX
    gP1 = S.gP1
    tQ, cS, cV = S.tQ, S.cS, S.cV
    sems = dict(tQ=tQ, cS=cS, cV=cV)
    CSZ = 3072 * 512
    TR_S, TR_TG, TR_E, TR_NC = C_TRIPS

    R = lambda n: p.r[n] if not p.dry else None
    io = p.io

    for c in range(24):
        p.op("gpsimd", gP1, lambda e, c=c: e.dma_start(
            io.wbf1[c * 8192:(c + 1) * 8192, :], io.w1w2_1[c * 8192:(c + 1) * 8192, :]))
    pc1 = gP1.tok()

    p.op("gpsimd", gC, lambda e: e.iota(R("xt")[:, 0:128], [[1, 128]], channel_multiplier=0,
                                        allow_small_or_imprecise_dtypes=True))
    it = p.op("gpsimd", gC, lambda e: e.iota(R("y0")[:, 0:128], [[0, 128]], channel_multiplier=1,
                                             allow_small_or_imprecise_dtypes=True))
    p.wait("vector", gC, it)
    ones_tok = p.op("vector", cV, lambda e: [
        e.tensor_tensor(R("identf")[:, :], R("xt")[:, 0:128], R("y0")[:, 0:128], ALU.is_equal),
        e.memset(R("ones128")[:, :], 1.0),
        e.memset(R("onesbf")[:, :], 1.0)][-1])
    p.op("gpsimd", gX, lambda e: e.dma_start(R("wrb")[:, :], io.wr1[:, :]))
    p.op("gpsimd", gX, lambda e: e.dma_start(
        R("rbb")[:, :], bass.AP(io.rb, 0, [[0, 128], [1, 8]])))

    with p.loop(TR_S) as s_lv:
        with p.loop(TR_TG) as tg_lv:
            p.barrier()
            p.wait("vector", oX, oX.tok())
            p.wait("scalar", oX, oX.tok())
            p.wait("sync", gP1, pc1)
            p.wait("sync", sW, sW.tok())
            for par in range(2):
                for ci, wdst in ((0, "wa_"), (1, "wg_"), (2, "wb2_")):
                    pat = [[4096, 128], [512, 8], [1, 512]] if ci < 2 else [[4096, 128], [1024, 4], [1, 1024]]
                    spat = [[512, 128], [128 * 512, 8], [1, 512]] if ci < 2 else [[1024, 128], [128 * 1024, 4], [1, 1024]]
                    p.op("sync", sW, lambda e, par=par, ci=ci, wdst=wdst, pat=pat, spat=spat: e.dma_start(
                        bass.AP(R(wdst + str(par)), 0, pat),
                        bass.AP(io.wbf1, par * CSZ + ci * 1024 * 512, spat)))
            p.op("gpsimd", gX, lambda e: e.dma_start(
                bass.AP(R("xbf"), 0, [[8192, 128], [1024, 8], [1, 1024]]),
                bass.AP(io.x3, p.off("gpsimd", 0, [(s_lv, 1024 * 4096), (tg_lv, 1024)]),
                        [[4096, 128], [128 * 4096, 8], [1, 1024]])))
            xbf_tok = gX.tok()
            p.op("vector", cV, lambda e: e.memset(R("outacc")[:, :], 0.0))

            p.wait("tensor", gX, xbf_tok)
            p.wait("tensor", cV, ones_tok)
            lg_toks = []
            for t2 in range(2):
                plg = ("pA", "pB")[t2]
                mt = p.op("tensor", tQ, lambda e, t2=t2, plg=plg: [
                    e.matmul(R(plg)[0:8, :], R("wrb")[:, kc * 8:(kc + 1) * 8],
                             R("xbf")[:, kc * 1024 + t2 * 512: kc * 1024 + t2 * 512 + 512],
                             start=(kc == 0), stop=(kc == 7)) for kc in range(8)][-1])
                p.wait("scalar", tQ, mt)
                lg_toks.append(p.op("scalar", cS, lambda e, t2=t2, plg=plg: e.copy(
                    R("lg")[0:8, t2 * 512:(t2 + 1) * 512], R(plg)[0:8, :])))
            c_toks = []
            rtm_toks = []
            for sub in range(8):
                p.wait("tensor", cS, lg_toks[1])
                if sub >= 1:
                    p.wait("tensor", cS, rtm_toks[sub - 1])
                tt = p.op("tensor", tQ, lambda e, sub=sub: e.transpose(
                    R("pA")[0:128, 0:8], R("lg")[0:8, sub * 128:(sub + 1) * 128],
                    R("identf")[0:8, 0:8]))
                p.wait("scalar", tQ, tt)
                if sub >= 1:
                    p.wait("scalar", cV, c_toks[sub - 1])
                rt = p.op("scalar", cS, lambda e: e.copy(R("rtm")[:, :], R("pA")[0:128, 0:8]))
                rtm_toks.append(rt)
                p.wait("vector", cS, rt)
                t = p.vchain("vector", cV, [
                    lambda e: e.tensor_add(R("rtm")[:, :], R("rtm")[:, :], R("rbb")[:, :]),
                    lambda e: e.reduce_max(R("srow")[:, 0:1], R("rtm")[:, :], mybir.AxisListType.X),
                    lambda e, sub=sub: e.tensor_scalar(R("ctm")[:, sub * 8:(sub + 1) * 8],
                                                       R("rtm")[:, :], R("srow")[:, 0:1],
                                                       None, ALU.subtract),
                ])
                p.wait("scalar", cV, t)
                et = p.op("scalar", cS, lambda e, sub=sub: e.activation(
                    R("ctm")[:, sub * 8:(sub + 1) * 8], R("ctm")[:, sub * 8:(sub + 1) * 8], AF.Exp))
                p.wait("vector", cS, et)
                c_toks.append(p.vchain("vector", cV, [
                    lambda e, sub=sub: e.reduce_sum(R("srow")[:, 1:2],
                                                    R("ctm")[:, sub * 8:(sub + 1) * 8],
                                                    mybir.AxisListType.X),
                    lambda e: e.reciprocal(R("srow")[:, 2:3], R("srow")[:, 1:2]),
                    lambda e, sub=sub: e.reduce_max(R("srow")[:, 3:4],
                                                    R("ctm")[:, sub * 8:(sub + 1) * 8],
                                                    mybir.AxisListType.X),
                    lambda e: e.tensor_mul(R("srow")[:, 4:5], R("srow")[:, 3:4], R("srow")[:, 2:3]),
                    lambda e, sub=sub: e.tensor_scalar(R("rtm")[:, :],
                                                       R("ctm")[:, sub * 8:(sub + 1) * 8],
                                                       R("srow")[:, 3:4], None, ALU.is_ge),
                    lambda e, sub=sub: e.tensor_scalar(R("ctm")[:, sub * 8:(sub + 1) * 8],
                                                       R("rtm")[:, :],
                                                       R("srow")[:, 4:5], None, ALU.mult),
                ]))
            cf_toks = []
            for sub in range(8):
                p.wait("tensor", cV, c_toks[sub])
                if sub >= 1:
                    p.wait("tensor", cS, cf_toks[sub - 1])
                tt = p.op("tensor", tQ, lambda e, sub=sub: e.transpose(
                    R("pB")[0:8, 0:128], R("ctm")[:, sub * 8:(sub + 1) * 8], R("identf")[:, :]))
                p.wait("scalar", tQ, tt)
                cf_toks.append(p.op("scalar", cS, lambda e, sub=sub: e.copy(
                    R("cfm")[0:8, sub * 128:(sub + 1) * 128], R("pB")[0:8, 0:128])))
            p.wait("sync", cS, cf_toks[-1])
            cdr_tok = p.op("sync", oX, lambda e: e.dma_start(io.cdr1[:, :], R("cfm")[:, :]))

            with p.loop(TR_E) as e_lv:
                cv_start = cV.tok()
                cs_start = cS.tok()
                p.wait("gpsimd", oX, cdr_tok)
                p.wait("gpsimd", cV, cv_start)
                ceb_tok = p.op("gpsimd", gX, lambda e: e.dma_start(
                    R("ceb")[:, :],
                    bass.AP(io.cdr1, p.off("gpsimd", 0, [(e_lv, 1024)]), [[0, 128], [1, 1024]])))
                with p.loop(TR_NC) as nc_lv:
                    sw_it = sW.tok()
                    cs_it = cS.tok()
                    cv_it = cV.tok()
                    for par in range(2):
                        wa, wg, wb2 = f"wa_{par}", f"wg_{par}", f"wb2_{par}"
                        p.wait("tensor", sW, sw_it)
                        p.wait("tensor", gX, ceb_tok) if False else None
                        add_toks = []
                        mm_last = None
                        for t2 in range(2):
                            p.wait("tensor", cS, cs_it if t2 == 0 else sil_tok)
                            if t2 == 0:
                                p.wait("tensor", cV, cv_it)
                            else:
                                p.wait("tensor", cV, hid_tok)
                            mm_a = p.op("tensor", tQ, lambda e, wa=wa, t2=t2: [
                                e.matmul(R(("pC", "pD", "pE", "pF")[oc])[:, :],
                                         R(wa)[:, kc * 512 + oc * 128: kc * 512 + (oc + 1) * 128],
                                         R("xbf")[:, kc * 1024 + t2 * 512: kc * 1024 + t2 * 512 + 512],
                                         start=(kc == 0), stop=(kc == 7))
                                for oc in range(4) for kc in range(8)][-1])
                            p.wait("scalar", tQ, mm_a)
                            sil_tok = p.op("scalar", cS, lambda e: [
                                e.activation(R("sa")[:, oc * 512:(oc + 1) * 512],
                                             R(("pC", "pD", "pE", "pF")[oc])[:, :], AF.Silu)
                                for oc in range(4)][-1])
                            p.wait("tensor", cS, sil_tok)
                            mm_g = p.op("tensor", tQ, lambda e, wg=wg, t2=t2: [
                                e.matmul(R(("pC", "pD", "pE", "pF")[oc])[:, :],
                                         R(wg)[:, kc * 512 + oc * 128: kc * 512 + (oc + 1) * 128],
                                         R("xbf")[:, kc * 1024 + t2 * 512: kc * 1024 + t2 * 512 + 512],
                                         start=(kc == 0), stop=(kc == 7))
                                for oc in range(4) for kc in range(8)][-1])
                            p.wait("vector", tQ, mm_g)
                            p.wait("vector", gX, ceb_tok)
                            gs_tok = p.op("vector", cV, lambda e, t2=t2: [
                                e.tensor_scalar(R("gs")[:, oc * 512:(oc + 1) * 512],
                                                R(("pC", "pD", "pE", "pF")[oc])[:, :],
                                                1.0, None, ALU.mult)
                                for oc in range(4)][-1])
                            p.wait("vector", cS, sil_tok)
                            hid_tok = p.op("vector", cV, lambda e, t2=t2: [
                                [e.tensor_mul(R("gs")[:, oc * 512:(oc + 1) * 512],
                                              R("gs")[:, oc * 512:(oc + 1) * 512],
                                              R("ceb")[:, t2 * 512:(t2 + 1) * 512]),
                                 e.tensor_mul(R("hid")[:, oc * 512:(oc + 1) * 512],
                                              R("gs")[:, oc * 512:(oc + 1) * 512],
                                              R("sa")[:, oc * 512:(oc + 1) * 512])][-1]
                                for oc in range(4)][-1])
                            p.wait("tensor", cV, hid_tok)
                            for jc in range(8):
                                pw = ("pG", "pH")[jc % 2]
                                a_idx = t2 * 8 + jc
                                if a_idx >= 2:
                                    p.wait("tensor", cV, add_toks[a_idx - 2])
                                else:
                                    p.wait("tensor", cV, cv_it)
                                mm_last = p.op("tensor", tQ, lambda e, wb2=wb2, jc=jc, pw=pw: [
                                    e.matmul(R(pw)[:, :],
                                             R(wb2)[:, kc2 * 1024 + jc * 128: kc2 * 1024 + (jc + 1) * 128],
                                             R("hid")[:, kc2 * 512:(kc2 + 1) * 512],
                                             start=(kc2 == 0), stop=(kc2 == 3)) for kc2 in range(4)][-1])
                                p.wait("vector", tQ, mm_last)
                                add_toks.append(p.op("vector", cV, lambda e, jc=jc, t2=t2, pw=pw: e.tensor_add(
                                    R("outacc")[:, jc * 1024 + t2 * 512: jc * 1024 + t2 * 512 + 512],
                                    R("outacc")[:, jc * 1024 + t2 * 512: jc * 1024 + t2 * 512 + 512],
                                    R(pw)[:, :])))
                        p.wait("sync", tQ, mm_last)
                        for ci, wdst in ((0, "wa_"), (1, "wg_"), (2, "wb2_")):
                            pat = [[4096, 128], [512, 8], [1, 512]] if ci < 2 else [[4096, 128], [1024, 4], [1, 1024]]
                            spat = [[512, 128], [128 * 512, 8], [1, 512]] if ci < 2 else [[1024, 128], [128 * 1024, 4], [1, 1024]]
                            p.op("sync", sW, lambda e, par=par, ci=ci, wdst=wdst, pat=pat, spat=spat: e.dma_start(
                                bass.AP(R(wdst + str(par)), 0, pat),
                                bass.AP(io.wbf1, p.off("sync", (par + 2) * CSZ + ci * 1024 * 512,
                                                      [(e_lv, 8 * CSZ), (nc_lv, 2 * CSZ)]),
                                        spat)))

            p.wait("vector", sW, sW.tok())
            for t2 in range(2):
                p.wait("vector", cV, cV.tok())
                xt_tok = p.op("scalar", vX, lambda e, t2=t2: e.dma_start(
                    bass.AP(R("xt"), 0, [[4096, 128], [512, 8], [1, 512]]),
                    bass.AP(io.x3, p.off("scalar", t2 * 512, [(s_lv, 1024 * 4096), (tg_lv, 1024)]),
                            [[4096, 128], [128 * 4096, 8], [1, 512]])))
                p.wait("vector", vX, xt_tok)
                y0_toks = []
                for jc in range(8):
                    y0_toks.append(p.op("vector", cV, lambda e, jc=jc, t2=t2: e.tensor_add(
                        R("y0")[:, jc * 512:(jc + 1) * 512],
                        R("outacc")[:, jc * 1024 + t2 * 512: jc * 1024 + t2 * 512 + 512],
                        R("xt")[:, jc * 512:(jc + 1) * 512])))

                def store_fn(e, t2=t2):
                    return e.dma_start(
                        bass.AP(io.x4, p.off("scalar", t2 * 512, [(s_lv, 1024 * 4096), (tg_lv, 1024)]),
                                [[4096, 128], [128 * 4096, 8], [1, 512]]),
                        bass.AP(R("xout"), 0, [[4096, 128], [512, 8], [1, 512]]))

                emit_ln_store(p, sems, y0_toks, store_fn, oX,
                              yb=("wa_0", 0), y2b=("wg_0", 0))


S1DEFS = [
    ("wqkvb", [128, 8 * 3072], BF16), ("woutb", [128, 8 * 1024], BF16),
    ("bqkv", [128, 24], F32), ("bout", [128, 8], F32), ("omega", [128, 256], BF16),
    ("ones128", [1, 128], F32), ("onesbf", [128, 1], BF16), ("vb", [128, 1024], F32),
    ("bqr", [1, 1024], F32), ("xbt", [128, 4096], BF16), ("xt", [128, 4096], F32),
    ("stk", [128, 16 * 512], BF16), ("vsb", [128, 4 * 16 * 65], BF16),
    ("kvacc", [128, 16 * 130], F32), ("kvbf", [128, 16 * 130], BF16),
    ("kpsb", [128, 2 * 512], BF16), ("rbsb", [64, 512], F32), ("hosb", [128, 8 * 512], BF16),
    ("wo0", [128, 512], F32), ("wo1", [128, 512], F32), ("y0", [128, 4096], F32),
    ("xout", [128, 4096], F32), ("lnrow", [1, 6 * 512], F32), ("mbr", [128, 512], F32),
    ("rbr", [128, 512], F32),
]
S2DEFS = [
    ("identf", [128, 128], F32), ("it1", [128, 128], F32), ("it2", [128, 128], F32),
    ("ones128", [1, 128], F32), ("onesbf", [128, 1], BF16), ("wrb", [128, 64], BF16),
    ("xbf", [128, 8 * 1024], BF16), ("outacc", [128, 8 * 1024], F32),
    ("xs", [128, 8 * 1024], BF16),
    ("wb1_0", [128, 4096], BF16), ("wb1_1", [128, 4096], BF16),
    ("wb2_0", [128, 4096], BF16), ("wb2_1", [128, 4096], BF16),
    ("hid", [128, 4 * 512], BF16), ("ceb", [128, 1024], BF16), ("lg", [8, 1024], F32),
    ("rtm", [128, 8], F32), ("srow", [128, 8], F32), ("ctm", [128, 64], F32),
    ("cfm", [8, 1024], F32), ("xt", [128, 4096], F32), ("y0", [128, 4096], F32),
    ("xout", [128, 4096], F32), ("stk", [128, 8192], BF16), ("lnrow", [1, 6 * 512], F32),
    ("mbr", [128, 512], F32), ("rbr", [128, 512], F32),
]
S3DEFS = [
    ("igwb", [128, 8 * 128], BF16), ("agwb", [128, 8 * 128], BF16),
    ("igb", [128, 8], F32), ("agb", [128, 8], F32), ("apv", [128, 8], F32),
    ("cvec", [128, 8], F32), ("sp8", [128, 4 * 8], F32),
    ("ones128", [1, 128], F32), ("onesbf", [128, 1], BF16), ("onesf", [128, 512], F32),
    ("xt", [128, 4096], F32), ("xbt", [128, 4096], BF16),
    ("gx", [128, 512], F32), ("ga", [128, 512], F32), ("la", [128, 512], F32),
    ("li", [128, 512], F32), ("lx", [128, 512], F32), ("pp", [128, 512], F32),
    ("ppi", [128, 512], F32), ("a2", [128, 512], F32), ("ml", [128, 512], F32),
    ("nx", [128, 512], F32), ("ss", [128, 512], F32),
    ("lc", [128, 8], F32), ("sc", [128, 8], F32), ("y0", [128, 4096], F32),
    ("xout", [128, 4096], F32), ("stk", [128, 8192], BF16), ("lnrow", [1, 6 * 512], F32),
    ("mbr", [128, 512], F32), ("rbr", [128, 512], F32),
]
S4DEFS = [
    ("identf", [128, 128], F32), ("ones128", [1, 128], F32), ("onesbf", [128, 1], BF16),
    ("wrb", [128, 64], BF16), ("rbb", [128, 8], F32), ("xbf", [128, 8 * 1024], BF16),
    ("outacc", [128, 8 * 1024], F32),
    ("wa_0", [128, 4096], BF16), ("wa_1", [128, 4096], BF16),
    ("wg_0", [128, 4096], BF16), ("wg_1", [128, 4096], BF16),
    ("wb2_0", [128, 4096], BF16), ("wb2_1", [128, 4096], BF16),
    ("sa", [128, 2048], F32), ("gs", [128, 2048], F32), ("hid", [128, 4 * 512], BF16),
    ("ceb", [128, 1024], F32), ("lg", [8, 1024], F32), ("rtm", [128, 8], F32),
    ("srow", [128, 8], F32), ("ctm", [128, 64], F32), ("cfm", [8, 1024], F32),
    ("xt", [128, 4096], F32), ("y0", [128, 4096], F32), ("xout", [128, 4096], F32),
    ("lnrow", [1, 6 * 512], F32), ("mbr", [128, 512], F32), ("rbr", [128, 512], F32),
]


def prog_full(p):
    S = NS()
    for owner, name, step in (("gpsimd", "gP0", 16), ("gpsimd", "gP1", 16),
                              ("gpsimd", "gX", 16), ("gpsimd", "gC", 1),
                              ("sync", "sW", 16), ("sync", "sX", 16),
                              ("scalar", "vX", 16),
                              ("sync", "oX", 16), ("tensor", "tQ", 1),
                              ("tensor", "tV", 1), ("scalar", "cS", 1),
                              ("vector", "cV", 1), ("sync", "bS", 1),
                              ("gpsimd", "bG", 1)):
        S[name] = p.sem(owner, name, step)

    if not p.dry:
        nc, C = p.nc, p.ctx
        io = p.io
        io.x = nc.dram_tensor("x", [SPC * 1024, 4096], F32, kind="ExternalInput")
        io.wqkv = nc.dram_tensor("wqkv", [1024, 3072], F32, kind="ExternalInput")
        io.bqkv = nc.dram_tensor("bqkv_t", [128, 24], F32, kind="ExternalInput")
        io.bqv = nc.dram_tensor("bqv", [1, 1024], F32, kind="ExternalInput")
        io.wout = nc.dram_tensor("wout", [1024, 1024], F32, kind="ExternalInput")
        io.bout = nc.dram_tensor("bout_t", [128, 8], F32, kind="ExternalInput")
        io.omega = nc.dram_tensor("omega", [64, 256], F32, kind="ExternalInput")
        io.wr0 = nc.dram_tensor("wr0", [128, 64], F32, kind="ExternalInput")
        io.w1w2_0 = nc.dram_tensor("w1w2_0", [64 * 2048, 512], F32, kind="ExternalInput")
        io.ap = nc.dram_tensor("ap_t", [128, 8], F32, kind="ExternalInput")
        io.igw = nc.dram_tensor("igw", [1024, 128], F32, kind="ExternalInput")
        io.igb = nc.dram_tensor("igb_t", [128, 8], F32, kind="ExternalInput")
        io.agw = nc.dram_tensor("agw", [1024, 128], F32, kind="ExternalInput")
        io.agb = nc.dram_tensor("agb_t", [128, 8], F32, kind="ExternalInput")
        io.wr1 = nc.dram_tensor("wr1", [128, 64], F32, kind="ExternalInput")
        io.rb = nc.dram_tensor("rb", [1, 8], F32, kind="ExternalInput")
        io.w1w2_1 = nc.dram_tensor("w1w2_1", [64 * 3072, 512], F32, kind="ExternalInput")
        io.x4 = nc.dram_tensor("x4", [SPC * 1024, 4096], F32, kind="ExternalOutput")
        io.x1 = nc.dram_tensor("x1", [SPC * 1024, 4096], F32)
        io.x2 = nc.dram_tensor("x2", [SPC * 1024, 4096], F32)
        io.x3 = nc.dram_tensor("x3", [SPC * 1024, 4096], F32)
        io.wbf0 = nc.dram_tensor("wbf0", [66 * 2048, 512], BF16)
        io.wbf1 = nc.dram_tensor("wbf1", [66 * 3072, 512], BF16)
        io.cdr0 = nc.dram_tensor("cdr0", [8, 1024], F32)
        io.cdr1 = nc.dram_tensor("cdr1", [8, 1024], F32)
        io.bsc = nc.dram_tensor("bsc", [4, 4], F32)
        A = Arena(nc, C.st)
        p.sr = []
        for pref, defs in (("s1_", S1DEFS), ("s2_", S2DEFS),
                           ("s3_", S3DEFS), ("s4_", S4DEFS)):
            A.reset()
            d = NS()
            for nm, shape, dt in defs:
                d[nm] = A.sbuf(pref + nm, shape, dt)
            p.sr.append(d)
        A.allocate()
        for nm in ("pA", "pB", "pC", "pD", "pE", "pF", "pG", "pH"):
            h = C.psum(nm, [128, 512])
            for d in p.sr:
                d[nm] = h

    set_stage(p, 0)
    emit_l0attn(p, S)
    stage_barrier(p, S)
    set_stage(p, 1)
    emit_l0ffn(p, S)
    stage_barrier(p, S)
    set_stage(p, 2)
    emit_rglru(p, S)
    stage_barrier(p, S)
    set_stage(p, 3)
    emit_l1ffn(p, S)
    p.finish()
    p.wait("gpsimd", S.oX, S.oX.tok())


def host_full_inputs(inputs):
    d = {
        "wqkv": np.ascontiguousarray(inputs["L0_attn_Wqkv"]),
        "bqkv_t": np.ascontiguousarray(inputs["L0_attn_bqkv"].reshape(24, 128).T),
        "bqv": np.ascontiguousarray(inputs["L0_attn_bqkv"][2048:].reshape(1, 1024)),
        "wout": np.ascontiguousarray(inputs["L0_attn_Wout"]),
        "bout_t": np.ascontiguousarray(inputs["L0_attn_bout"].reshape(8, 128).T),
        "omega": np.ascontiguousarray(inputs["L0_attn_omega"]),
        "wr0": np.ascontiguousarray(
            inputs["L0_ffn_Wr"].reshape(8, 128, 8).transpose(1, 0, 2).reshape(128, 64)),
        "ap_t": np.ascontiguousarray(inputs["L1_attn_a_param"].reshape(8, 128).T),
        "igw": np.ascontiguousarray(inputs["L1_attn_ig_w"].reshape(1024, 128)),
        "igb_t": np.ascontiguousarray(inputs["L1_attn_ig_b"].reshape(8, 128).T),
        "agw": np.ascontiguousarray(inputs["L1_attn_ag_w"].reshape(1024, 128)),
        "agb_t": np.ascontiguousarray(inputs["L1_attn_ag_b"].reshape(8, 128).T),
        "wr1": np.ascontiguousarray(
            inputs["L1_ffn_router_w"].reshape(8, 128, 8).transpose(1, 0, 2).reshape(128, 64)),
        "rb": np.ascontiguousarray(inputs["L1_ffn_router_b"].reshape(1, 8)),
    }
    W1, W2 = inputs["L0_ffn_W1"], inputs["L0_ffn_W2"]
    chunks = []
    for e in range(8):
        for ncc in range(8):
            chunks.append(np.ascontiguousarray(W1[e][:, ncc * 512:(ncc + 1) * 512]))
            chunks.append(W2[e][ncc * 512:(ncc + 1) * 512, :].reshape(1024, 512))
    d["w1w2_0"] = np.ascontiguousarray(np.concatenate(chunks, axis=0))
    w1, w2 = inputs["L1_ffn_w1"], inputs["L1_ffn_w2"]
    chunks = []
    for e in range(8):
        for ncc in range(8):
            chunks.append(np.ascontiguousarray(w1[e][:, ncc * 512:(ncc + 1) * 512]))
            chunks.append(np.ascontiguousarray(w1[e][:, 4096 + ncc * 512: 4096 + (ncc + 1) * 512]))
            chunks.append(w2[e][ncc * 512:(ncc + 1) * 512, :].reshape(1024, 512))
    d["w1w2_1"] = np.ascontiguousarray(np.concatenate(chunks, axis=0))
    return d


_PROGS = {}


def _get_prog(name, fn):
    if name not in _PROGS:
        _PROGS[name] = build(fn)
    return _PROGS[name]


def kernel(**inputs):
    p = _get_prog("full", prog_full)
    cores = list(range(NCORE))
    shared = host_full_inputs(inputs)
    ims = []
    for c in cores:
        im = dict(shared)
        im["x"] = np.ascontiguousarray(
            inputs["L0_attn_x"][2 * c: 2 * c + 2].transpose(0, 2, 1)).reshape(SPC * 1024, 4096)
        ims.append(im)
    res = run_bass_kernel_spmd(p.nc, ims, core_ids=cores)

    out = np.empty((B, N, D), np.float32)
    for c in cores:
        x4 = np.asarray(res.results[c]["x4"]).reshape(SPC, 1024, 4096)
        out[2 * c: 2 * c + 2] = x4.transpose(0, 2, 1)
    return out


# version 04
# speedup vs baseline: 1.0226x; 1.0226x over previous
import contextlib
import numpy as np
import bass_rust
import concourse.bass as bass
import concourse.mybir as mybir
from concourse.bass_utils import run_bass_kernel_spmd

F32 = mybir.dt.float32
BF16 = mybir.dt.bfloat16
AF = mybir.ActivationFunctionType
ALU = mybir.AluOpType

B, N, D = 16, 4096, 1024
NCORE = 8
SPC = 2
TT = 512
NTILE = N // TT
NH, MF, DH = 16, 256, 64
LN_EPS = 1e-5
LN7 = -16.11809565095832
SEM_SHIFT = 0
C_TRIPS = (SPC, 4, 8, 4)
C_STAGE = 3
ENGS = ("sync", "tensor", "vector", "scalar", "gpsimd")


class LoopVar:
    __slots__ = ("lvl", "var")

    def __init__(self, lvl, var):
        self.lvl = lvl
        self.var = var


class Tok:
    __slots__ = ("c", "terms")

    def __init__(self, c, terms=()):
        self.c = c
        self.terms = tuple(terms)

    def __add__(self, k):
        return Tok(self.c + k, self.terms)


class Sem:
    def __init__(self, p, name, owner, step=1):
        self.p, self.name, self.owner, self.step = p, name, owner, step
        self.c = SEM_SHIFT
        self.stack = []
        if not p.dry:
            self.h = p.ctx.sem(name)
            if SEM_SHIFT:
                getattr(p.nc, owner).sem_inc(self.h, SEM_SHIFT)

    def tok(self, delta=0):
        return Tok(self.c + delta, [(lvl, pi) for (lvl, _s, pi) in self.stack if pi])


class P:
    MAXDEPTH = 4

    def __init__(self, dry, totals=None):
        self.dry = dry
        self.totals = totals if totals is not None else {}
        self.loop_seq = 0
        self.depth = 0
        self.sems = []
        self.r = NS()
        self.io = NS()
        self.regs = {}
        if not dry:
            self.ctx = Ctx()
            self.nc = self.ctx.nc
            for eng in ENGS:
                E = getattr(self.nc, eng)
                self.regs[eng] = {
                    "ri": [E.alloc_register(f"{eng}_ri{l}") for l in range(self.MAXDEPTH)],
                    "t0": E.alloc_register(f"{eng}_t0"),
                    "t1": E.alloc_register(f"{eng}_t1"),
                }

    def sem(self, owner, name, step=1):
        s = Sem(self, name, owner, step)
        self.sems.append(s)
        return s

    @contextlib.contextmanager
    def loop(self, n):
        idx = self.loop_seq
        self.loop_seq += 1
        lvl = self.depth
        self.depth += 1

        if self.dry:
            snap = [(s, s.c) for s in self.sems]
            for s in self.sems:
                s.stack.append([lvl, s.c, 0])
            yield LoopVar(lvl, 0)
            pis = {}
            for s, c0 in snap:
                pi = s.c - c0
                pis[s.name] = pi
                s.stack.pop()
                s.c = c0 + n * pi
            self.totals[idx] = pis
        else:
            pis = self.totals[idx]
            nc = self.nc
            loop_start = f"ploop{idx}_loop"
            loop_end = f"ploop{idx}_end"
            ris = bass_rust.RegisterHandles(
                tuple(self.regs[eng]["ri"][lvl] for eng in ENGS))
            for eng in ENGS:
                E = getattr(nc, eng)
                E.reg_mov(self.regs[eng]["ri"][lvl], -1)
            nc.br(loop_start)
            with nc.body(loop_start):
                for eng in ENGS:
                    E = getattr(nc, eng)
                    E.reg_add(self.regs[eng]["ri"][lvl],
                              self.regs[eng]["ri"][lvl], 1)
                snap = [(s, s.c) for s in self.sems]
                for s in self.sems:
                    s.stack.append([lvl, s.c, pis.get(s.name, 0)])
                yield LoopVar(lvl, None)
                for s, c0 in snap:
                    got = s.c - c0
                    assert got == pis.get(s.name, 0), \
                        f"loop {idx} sem {s.name}: emitted {got} != dry {pis.get(s.name, 0)}"
                    s.stack.pop()
                    s.c = c0 + n * got
                nc.br_lt(ris, n - 1, on_true=loop_start, on_false=loop_end)
            nc.switch_bb(loop_end)
        self.depth -= 1

    def _regval(self, eng, const, terms):
        E = getattr(self.nc, eng)
        rg = self.regs[eng]
        first = True
        for lvl, coef in terms:
            if coef == 0:
                continue
            dst = rg["t0"] if first else rg["t1"]
            E.reg_mul(dst, rg["ri"][lvl], coef)
            if not first:
                E.reg_add(rg["t0"], rg["t0"], rg["t1"])
            first = False
        assert not first
        E.reg_add(rg["t0"], rg["t0"], const)
        return rg["t0"]

    def wait(self, eng, sem, tok):
        if self.dry:
            return
        terms = [(l, c) for (l, c) in tok.terms if c]
        if not terms:
            assert tok.c >= 0
            getattr(self.nc, eng).wait_ge(sem.h, tok.c)
        else:
            reg = self._regval(eng, tok.c, terms)
            getattr(self.nc, eng).wait_ge(sem.h, reg)

    def off(self, eng, const, terms):
        E = getattr(self.nc, eng)
        rg = self.regs[eng]
        terms = [(lv, c) for (lv, c) in terms if c]
        if not terms:
            return const
        o = E.alloc_register()
        first = True
        for lv, coef in terms:
            if first:
                E.reg_mul(o, rg["ri"][lv.lvl], coef)
            else:
                E.reg_mul(rg["t1"], rg["ri"][lv.lvl], coef)
                E.reg_add(o, o, rg["t1"])
            first = False
        E.reg_add(o, o, const)
        return E.snap(o, donate=True)

    def op(self, eng, sem, fn, k=1):
        if not self.dry:
            ins = fn(getattr(self.nc, eng))
            ins.then_inc(sem.h, sem.step * k)
        sem.c += sem.step * k
        return sem.tok()

    def raw(self, eng, fn):
        if not self.dry:
            fn(getattr(self.nc, eng))

    def vchain(self, eng, sem, fns):
        t = None
        for fn in fns:
            t = self.op(eng, sem, fn)
            self.wait(eng, sem, t)
        return t

    def finish(self):
        for s in self.sems:
            if s.step == 16:
                self.wait(s.owner, s, s.tok())

    def barrier(self, sems=None):
        sems = sems if sems is not None else self.sems
        toks = [(s, s.tok()) for s in sems if s.step == 1]
        for eng in ENGS:
            for s, t in toks:
                if s.owner != eng:
                    self.wait(eng, s, t)
        return {s.name: t for s, t in toks}


class Ctx:
    def __init__(self):
        self.nc = bass.Bass("TRN2", target_bir_lowering=False, detect_race_conditions=False)
        self.st = contextlib.ExitStack()

    def sem(self, name):
        return self.st.enter_context(self.nc.semaphore(name))

    def sbuf(self, name, shape, dtype):
        return self.st.enter_context(self.nc.sbuf_tensor(name, shape, dtype))

    def psum(self, name, shape, dtype=F32):
        return self.st.enter_context(self.nc.psum_tensor(name, shape, dtype))


def build(program_fn, *args):
    p1 = P(dry=True)
    program_fn(p1, *args)
    p2 = P(dry=False, totals=p1.totals)
    program_fn(p2, *args)
    return p2


class NS(dict):
    __getattr__ = dict.__getitem__
    __setattr__ = dict.__setitem__


class Arena:

    def __init__(self, nc, st):
        self.nc, self.st = nc, st
        self.addr = 0
        self.size = 0
        self.tensors = []

    def sbuf(self, name, shape, dtype):
        per = 1
        for s in shape[1:]:
            per *= s
        per *= mybir.dt.size(dtype)
        a = self.addr
        self.addr = (a + per + 31) & ~31
        self.size = max(self.size, self.addr)
        h = self.nc.virtual_sbuf_tensor(name, shape, dtype)
        self.tensors.append((h, a))
        return h

    def reset(self):
        self.addr = 0

    def allocate(self):
        handle = self.st.enter_context(
            self.nc.sbuf_tensor("arena", [128, self.size], mybir.dt.uint8))
        off = self.nc.lookup_mloc(handle).addr
        for h, a in self.tensors:
            self.nc._tensor(h.name, h.shape, h.dtype, type="SB")
            m = self.nc.lookup_mloc(h)
            m.allocated = True
            m.addr = off + a


def set_stage(p, k):
    if not p.dry:
        p.r = p.sr[k]


def stage_barrier(p, S):
    for s in p.sems:
        if s.step == 16 and s.name not in ("gP0", "gP1"):
            p.wait(s.owner, s, s.tok())
    R = lambda n: p.r[n] if not p.dry else None
    io = p.io
    incs = []
    t = p.op("tensor", S.tQ, lambda e: e.matmul(
        R("pA")[0:1, 0:1], R("onesbf")[:, 0:1], R("onesbf")[:, 0:1],
        start=True, stop=True))
    incs.append(("tensor", S.tQ, t))
    t = p.op("vector", S.cV, lambda e: e.memset(R("lnrow")[0:1, 0:1], 0.0))
    incs.append(("vector", S.cV, t))
    t = p.op("scalar", S.cS, lambda e: e.copy(R("lnrow")[0:1, 2:3], R("lnrow")[0:1, 3:4]))
    incs.append(("scalar", S.cS, t))
    t = p.op("sync", S.oX, lambda e: e.dma_start(io.bsc[0:1, :], io.bsc[1:2, :]))
    incs.append(("sync", S.oX, t))
    t = p.op("gpsimd", S.gX, lambda e: e.dma_start(io.bsc[2:3, :], io.bsc[3:4, :]))
    incs.append(("gpsimd", S.gX, t))
    for eng in ENGS:
        for o, sem, tk in incs:
            if o != eng:
                p.wait(eng, sem, tk)


def emit_ln_store(p, sems, y0_toks, store_fn, store_sem, yb=("stk", 0), y2b=("stk", 4096)):
    tQ, cS, cV = sems["tQ"], sems["cS"], sems["cV"]
    R = lambda n: p.r[n] if not p.dry else None
    ybn, ybo = yb
    y2n, y2o = y2b
    cast_toks, sq_toks = [], []
    for jc in range(8):
        ct = p.op("vector", cV, lambda e, jc=jc: e.tensor_copy(
            R(ybn)[:, ybo + jc * 512: ybo + (jc + 1) * 512], R("y0")[:, jc * 512:(jc + 1) * 512]))
        p.wait("scalar", cV, y0_toks[jc])
        st = p.op("scalar", cS, lambda e, jc=jc: e.activation(
            R(y2n)[:, y2o + jc * 512: y2o + (jc + 1) * 512],
            R("y0")[:, jc * 512:(jc + 1) * 512], AF.Square))
        cast_toks.append(ct)
        sq_toks.append(st)
    p.wait("tensor", cV, cast_toks[-1])
    p.wait("tensor", cS, sq_toks[-1])
    p.op("tensor", tQ, lambda e: [
        e.matmul(R("pC")[0:1, :], R("onesbf")[:, :],
                 R(ybn)[:, ybo + kc * 512: ybo + (kc + 1) * 512],
                 start=(kc == 0), stop=(kc == 7)) for kc in range(8)][-1])
    t2 = p.op("tensor", tQ, lambda e: [
        e.matmul(R("pD")[0:1, :], R("onesbf")[:, :],
                 R(y2n)[:, y2o + kc * 512: y2o + (kc + 1) * 512],
                 start=(kc == 0), stop=(kc == 7)) for kc in range(8)][-1])
    p.wait("scalar", tQ, t2)
    p.op("scalar", cS, lambda e: e.mul(R("lnrow")[0:1, 0:512], R("pC")[0:1, :], 1.0 / 1024))
    p.op("scalar", cS, lambda e: e.mul(R("lnrow")[0:1, 512:1024], R("pD")[0:1, :], 1.0 / 1024))
    mt = p.op("scalar", cS, lambda e: e.activation(
        R("lnrow")[0:1, 1024:1536], R("lnrow")[0:1, 0:512], AF.Square))
    p.wait("vector", cS, mt)
    vt = p.op("vector", cV, lambda e: [
        e.tensor_sub(R("lnrow")[0:1, 1536:2048], R("lnrow")[0:1, 512:1024],
                     R("lnrow")[0:1, 1024:1536]),
        e.tensor_scalar(R("lnrow")[0:1, 1536:2048], R("lnrow")[0:1, 1536:2048],
                        LN_EPS, None, ALU.add)][-1])
    p.wait("scalar", cV, vt)
    sdt = p.op("scalar", cS, lambda e: e.activation(
        R("lnrow")[0:1, 2048:2560], R("lnrow")[0:1, 1536:2048], AF.Sqrt))
    p.wait("vector", cS, sdt)
    rst = p.op("vector", cV, lambda e: e.reciprocal(
        R("lnrow")[0:1, 2560:3072], R("lnrow")[0:1, 2048:2560]))
    p.wait("tensor", cV, rst)
    bt = p.op("tensor", tQ, lambda e: [
        e.matmul(R("pE")[:, :], R("ones128")[0:1, :], R("lnrow")[0:1, 0:512], start=True, stop=True),
        e.matmul(R("pF")[:, :], R("ones128")[0:1, :], R("lnrow")[0:1, 2560:3072], start=True, stop=True)][-1])
    p.wait("scalar", tQ, bt)
    cbt = p.op("scalar", cS, lambda e: [
        e.copy(R("mbr")[:, :], R("pE")[:, :]),
        e.copy(R("rbr")[:, :], R("pF")[:, :])][-1])
    p.wait("vector", cS, cbt)
    xt_ = None
    for jc in range(8):
        xt_ = p.op("vector", cV, lambda e, jc=jc: [
            e.tensor_sub(R("xout")[:, jc * 512:(jc + 1) * 512],
                         R("y0")[:, jc * 512:(jc + 1) * 512], R("mbr")[:, :]),
            e.tensor_mul(R("xout")[:, jc * 512:(jc + 1) * 512],
                         R("xout")[:, jc * 512:(jc + 1) * 512], R("rbr")[:, :])][-1])
    p.wait("scalar", cV, xt_)
    p.op("scalar", store_sem, lambda e: store_fn(e))


def emit_l0attn(p, S):
    gX, sX, oX, vX = S.gX, S.sX, S.oX, S.vX
    tQ, tV, cS, cV = S.tQ, S.tV, S.cS, S.cV
    sems = dict(tQ=tQ, cS=cS, cV=cV)

    R = lambda n: p.r[n] if not p.dry else None
    io = p.io

    for kc in range(8):
        p.op("gpsimd", gX, lambda e, kc=kc: e.dma_start(
            R("wqkvb")[:, kc * 3072:(kc + 1) * 3072], io.wqkv[kc * 128:(kc + 1) * 128, :]))
        p.op("gpsimd", gX, lambda e, kc=kc: e.dma_start(
            R("woutb")[:, kc * 1024:(kc + 1) * 1024], io.wout[kc * 128:(kc + 1) * 128, :]))
    p.op("sync", sX, lambda e: e.dma_start(R("bqkv")[:, :], io.bqkv[:, :]))
    p.op("sync", sX, lambda e: e.dma_start(R("bout")[:, :], io.bout[:, :]))
    p.op("sync", sX, lambda e: e.dma_start(R("bqr")[0:1, :], io.bqv[0:1, :]))
    p.raw("gpsimd", lambda e: e.memset(R("omega")[:, :], -0.5))
    p.op("gpsimd", gX, lambda e: e.dma_start(R("omega")[0:64, :], io.omega[:, :]))
    p.raw("gpsimd", lambda e: e.memset(R("vsb")[:, :], 1.0))
    ones_tok = p.op("vector", cV, lambda e: [
        e.memset(R("ones128")[:, :], 1.0),
        e.memset(R("onesbf")[:, :], 1.0)][-1])
    p.wait("tensor", gX, gX.tok())
    p.wait("tensor", sX, sX.tok())
    p.wait("tensor", cV, ones_tok)
    bt = p.op("tensor", tQ, lambda e: [
        e.matmul(R("pA")[:, :], R("ones128")[0:1, :], R("bqr")[0:1, 0:512], start=True, stop=True),
        e.matmul(R("pB")[:, :], R("ones128")[0:1, :], R("bqr")[0:1, 512:1024], start=True, stop=True)][-1])
    p.wait("scalar", tQ, bt)
    p.op("scalar", cS, lambda e: [
        e.copy(R("vb")[:, 0:512], R("pA")[:, :]),
        e.copy(R("vb")[:, 512:1024], R("pB")[:, :])][-1])

    with p.loop(SPC) as s_lv:
        p.barrier()
        p.op("vector", cV, lambda e: e.memset(R("kvacc")[:, :], 0.0))

        with p.loop(NTILE) as ti:
            p.barrier()
            p.op("gpsimd", gX, lambda e: e.dma_start(
                bass.AP(R("xbt"), 0, [[4096, 128], [512, 8], [1, TT]]),
                bass.AP(io.x, p.off("gpsimd", 0, [(s_lv, 1024 * 4096), (ti, TT)]),
                        [[4096, 128], [128 * 4096, 8], [1, TT]])))
            p.op("gpsimd", S.gP0, lambda e: e.dma_start(
                bass.AP(io.wbf0, p.off("gpsimd", 0,
                                       [(s_lv, 8 * 8192 * 512), (ti, 8192 * 512)]),
                        [[512, 8192], [1, 512]]),
                bass.AP(io.w1w2_0, p.off("gpsimd", 0,
                                         [(s_lv, 8 * 8192 * 512), (ti, 8192 * 512)]),
                        [[512, 8192], [1, 512]])))
            p.wait("tensor", gX, gX.tok())

            ev = []
            for j in range(8):
                ps = "pA" if j % 2 == 0 else "pB"
                if j >= 2:
                    p.wait("tensor", cS, ev[j - 2])
                mt = p.op("tensor", tQ, lambda e, j=j, ps=ps: [
                    e.matmul(R(ps)[:, :],
                             R("wqkvb")[:, kc * 3072 + 1024 + j * 128: kc * 3072 + 1024 + (j + 1) * 128],
                             R("xbt")[:, kc * 512:(kc + 1) * 512],
                             start=(kc == 0), stop=(kc == 7)) for kc in range(8)][-1])
                p.wait("scalar", tQ, mt)
                h0, h1 = 2 * j, 2 * j + 1
                et = p.op("scalar", cS, lambda e, j=j, ps=ps, h0=h0, h1=h1: [
                    e.activation(R("stk")[0:64, h0 * 512:(h0 + 1) * 512], R(ps)[0:64, :],
                                 AF.Identity, bias=R("bqkv")[0:64, 8 + j: 9 + j]),
                    e.activation(R("stk")[64:128, h0 * 512:(h0 + 1) * 512], R(ps)[0:64, :],
                                 AF.Square, bias=R("bqkv")[0:64, 8 + j: 9 + j]),
                    e.activation(R("stk")[0:64, h1 * 512:(h1 + 1) * 512], R(ps)[64:128, :],
                                 AF.Identity, bias=R("bqkv")[64:128, 8 + j: 9 + j]),
                    e.activation(R("stk")[64:128, h1 * 512:(h1 + 1) * 512], R(ps)[64:128, :],
                                 AF.Square, bias=R("bqkv")[64:128, 8 + j: 9 + j])][-1])
                ev.append(et)

            vev = []
            for idx in range(8):
                sc, grp = idx // 2, idx % 2
                ps = "pC" if idx % 2 == 0 else "pD"
                if idx >= 2:
                    p.wait("tensor", cV, vev[idx - 2])
                mt = p.op("tensor", tQ, lambda e, sc=sc, grp=grp, ps=ps: [
                    e.matmul(R(ps)[:, :],
                             R("xbt")[:, kc * 512 + sc * 128: kc * 512 + (sc + 1) * 128],
                             R("wqkvb")[:, kc * 3072 + 2048 + grp * 512: kc * 3072 + 2048 + (grp + 1) * 512],
                             start=(kc == 0), stop=(kc == 7)) for kc in range(8)][-1])
                p.wait("vector", tQ, mt)

                def vevac(e, sc=sc, grp=grp, ps=ps):
                    dst = bass.AP(R("vsb"), (sc * 16 + grp * 8) * 65,
                                  [[4 * 16 * 65, 128], [65, 8], [1, 64]])
                    src = bass.AP(R(ps), 0, [[512, 128], [64, 8], [1, 64]])
                    bia = bass.AP(R("vb"), grp * 512, [[1024, 128], [64, 8], [1, 64]])
                    return e.tensor_add(dst, src, bia)
                vev.append(p.op("vector", cV, vevac))
            p.wait("tensor", cV, vev[-1])

            kvadd = []
            exp_toks = {}
            for h in range(16):
                if h >= 1:
                    p.wait("tensor", cV, kvadd[h - 1])
                p.wait("tensor", cS, ev[h // 2])
                kt = None
                for sc in range(4):
                    pf = "pE" if sc % 2 == 0 else "pF"
                    if sc >= 2:
                        p.wait("tensor", cS, exp_toks[(h, sc - 2)])
                    elif h >= 1:
                        p.wait("tensor", cS, exp_toks[(h - 1, sc + 2)])
                    ft = p.op("tensor", tQ, lambda e, h=h, sc=sc, pf=pf: e.matmul(
                        R(pf)[:, 0:256], R("stk")[:, h * 512 + sc * 128: h * 512 + (sc + 1) * 128],
                        R("omega")[:, :], start=True, stop=True))
                    p.wait("scalar", tQ, ft)
                    p.wait("scalar", tV, tV.tok())
                    et = p.op("scalar", cS, lambda e, sc=sc, pf=pf: e.activation(
                        R("kpsb")[:, (sc % 2) * 512: (sc % 2) * 512 + 256], R(pf)[:, 0:256], AF.Exp))
                    exp_toks[(h, sc)] = et
                    p.wait("tensor", cS, et)
                    kt = p.op("tensor", tV, lambda e, h=h, sc=sc: [
                        e.matmul(R(("pG", "pH")[mc])[:, 0:65],
                                 R("kpsb")[:, (sc % 2) * 512 + mc * 128: (sc % 2) * 512 + mc * 128 + 128],
                                 R("vsb")[:, (sc * 16 + h) * 65: (sc * 16 + h + 1) * 65],
                                 start=(sc == 0), stop=(sc == 3)) for mc in range(2)][-1])
                p.wait("vector", tV, kt)
                kvadd.append(p.op("vector", cV, lambda e, h=h: [
                    e.tensor_add(R("kvacc")[:, h * 130: h * 130 + 65],
                                 R("kvacc")[:, h * 130: h * 130 + 65], R("pG")[:, 0:65]),
                    e.tensor_add(R("kvacc")[:, h * 130 + 65: h * 130 + 130],
                                 R("kvacc")[:, h * 130 + 65: h * 130 + 130], R("pH")[:, 0:65])][-1]))

        kvcast = p.op("vector", cV, lambda e: e.tensor_copy(R("kvbf")[:, :], R("kvacc")[:, :]))

        with p.loop(NTILE) as ti:
            p.barrier()
            p.wait("vector", oX, oX.tok())
            p.op("gpsimd", gX, lambda e: e.dma_start(
                bass.AP(R("xbt"), 0, [[4096, 128], [512, 8], [1, TT]]),
                bass.AP(io.x, p.off("gpsimd", 0, [(s_lv, 1024 * 4096), (ti, TT)]),
                        [[4096, 128], [128 * 4096, 8], [1, TT]])))
            p.op("gpsimd", S.gP1, lambda e: e.dma_start(
                bass.AP(io.wbf1, p.off("gpsimd", 0,
                                       [(s_lv, 8 * 8192 * 512), (ti, 8192 * 512)]),
                        [[512, 8192], [1, 512]]),
                bass.AP(io.w1w2_1, p.off("gpsimd", 0,
                                         [(s_lv, 8 * 8192 * 512), (ti, 8192 * 512)]),
                        [[512, 8192], [1, 512]])))
            p.op("scalar", vX, lambda e: e.dma_start(
                bass.AP(R("xt"), 0, [[4096, 128], [512, 8], [1, TT]]),
                bass.AP(io.x, p.off("scalar", 0, [(s_lv, 1024 * 4096), (ti, TT)]),
                        [[4096, 128], [128 * 4096, 8], [1, TT]])))
            xload32 = vX.tok()
            p.wait("tensor", gX, gX.tok())

            ev = []
            for j in range(8):
                ps = "pA" if j % 2 == 0 else "pB"
                if j >= 2:
                    p.wait("tensor", cS, ev[j - 2])
                mt = p.op("tensor", tQ, lambda e, j=j, ps=ps: [
                    e.matmul(R(ps)[:, :],
                             R("wqkvb")[:, kc * 3072 + j * 128: kc * 3072 + (j + 1) * 128],
                             R("xbt")[:, kc * 512:(kc + 1) * 512],
                             start=(kc == 0), stop=(kc == 7)) for kc in range(8)][-1])
                p.wait("scalar", tQ, mt)
                h0, h1 = 2 * j, 2 * j + 1
                et = p.op("scalar", cS, lambda e, j=j, ps=ps, h0=h0, h1=h1: [
                    e.activation(R("stk")[0:64, h0 * 512:(h0 + 1) * 512], R(ps)[0:64, :],
                                 AF.Identity, bias=R("bqkv")[0:64, j: j + 1]),
                    e.activation(R("stk")[64:128, h0 * 512:(h0 + 1) * 512], R(ps)[0:64, :],
                                 AF.Square, bias=R("bqkv")[0:64, j: j + 1]),
                    e.activation(R("stk")[0:64, h1 * 512:(h1 + 1) * 512], R(ps)[64:128, :],
                                 AF.Identity, bias=R("bqkv")[64:128, j: j + 1]),
                    e.activation(R("stk")[64:128, h1 * 512:(h1 + 1) * 512], R(ps)[64:128, :],
                                 AF.Square, bias=R("bqkv")[64:128, j: j + 1])][-1])
                ev.append(et)

            ho_toks, nd_toks, rt_toks = [], [], []
            expb = {}
            for h in range(16):
                if h >= 1:
                    p.wait("tensor", cS, expb[h - 1])
                p.wait("tensor", cS, ev[h // 2])
                qt = p.op("tensor", tQ, lambda e, h=h: [
                    e.matmul(R(("pC", "pD")[mc])[:, :], R("omega")[:, mc * 128:(mc + 1) * 128],
                             R("stk")[:, h * 512:(h + 1) * 512], start=True, stop=True)
                    for mc in range(2)][-1])
                p.wait("scalar", tQ, qt)
                if h >= 1:
                    p.wait("scalar", tV, nd_toks[h - 1])
                et = p.op("scalar", cS, lambda e: [
                    e.activation(R("kpsb")[:, mc * 512:(mc + 1) * 512], R(("pC", "pD")[mc])[:, :],
                                 AF.Exp) for mc in range(2)][-1])
                expb[h] = et
                p.wait("tensor", cS, et)
                if h == 0:
                    p.wait("tensor", cV, kvcast)
                if h >= 1:
                    p.wait("tensor", cV, ho_toks[h - 1])
                nt = p.op("tensor", tV, lambda e, h=h: [
                    e.matmul(R("pE")[0:65, :], R("kvbf")[:, h * 130 + mc * 65: h * 130 + (mc + 1) * 65],
                             R("kpsb")[:, mc * 512:(mc + 1) * 512],
                             start=(mc == 0), stop=(mc == 1)) for mc in range(2)][-1])
                nd_toks.append(nt)
                p.wait("vector", tV, nt)
                dt = p.op("vector", cV, lambda e: [
                    e.tensor_scalar(R("lnrow")[0:1, 0:512], R("pE")[64:65, :], 1e-6, None, ALU.add),
                    e.reciprocal(R("lnrow")[0:1, 512:1024], R("lnrow")[0:1, 0:512])][-1])
                p.wait("tensor", cV, dt)
                if h >= 1:
                    p.wait("tensor", cS, rt_toks[h - 1])
                bt2 = p.op("tensor", tQ, lambda e: e.matmul(
                    R("pF")[0:64, :], R("ones128")[0:1, 0:64], R("lnrow")[0:1, 512:1024],
                    start=True, stop=True))
                p.wait("scalar", tQ, bt2)
                if h >= 1:
                    p.wait("scalar", cV, ho_toks[h - 1])
                rt = p.op("scalar", cS, lambda e: e.copy(R("rbsb")[:, :], R("pF")[0:64, :]))
                rt_toks.append(rt)
                p.wait("vector", cS, rt)
                ho_toks.append(p.op("vector", cV, lambda e, h=h: e.tensor_mul(
                    R("hosb")[(h % 2) * 64:(h % 2) * 64 + 64, (h // 2) * 512:(h // 2 + 1) * 512],
                    R("pE")[0:64, :], R("rbsb")[:, :])))

            p.wait("tensor", cV, ho_toks[-1])
            y0_toks, wtoks = [], []
            for j in range(8):
                ps = "pG" if j % 2 == 0 else "pH"
                wo = "wo0" if j % 2 == 0 else "wo1"
                if j >= 2:
                    p.wait("tensor", cS, wtoks[j - 2])
                mt = p.op("tensor", tQ, lambda e, j=j, ps=ps: [
                    e.matmul(R(ps)[:, :],
                             R("woutb")[:, kc * 1024 + j * 128: kc * 1024 + (j + 1) * 128],
                             R("hosb")[:, kc * 512:(kc + 1) * 512],
                             start=(kc == 0), stop=(kc == 7)) for kc in range(8)][-1])
                p.wait("scalar", tQ, mt)
                wt = p.op("scalar", cS, lambda e, j=j, ps=ps, wo=wo: e.activation(
                    R(wo)[:, :], R(ps)[:, :], AF.Identity, bias=R("bout")[:, j: j + 1]))
                wtoks.append(wt)
                p.wait("vector", cS, wt)
                if j == 0:
                    p.wait("vector", vX, xload32)
                y0_toks.append(p.op("vector", cV, lambda e, j=j, wo=wo: e.tensor_add(
                    R("y0")[:, j * 512:(j + 1) * 512], R(wo)[:, :], R("xt")[:, j * 512:(j + 1) * 512])))

            def store_fn(e, s_lv=s_lv, ti=ti):
                return e.dma_start(
                    bass.AP(io.x1, p.off("scalar", 0, [(s_lv, 1024 * 4096), (ti, TT)]),
                            [[4096, 128], [128 * 4096, 8], [1, TT]]),
                    bass.AP(R("xout"), 0, [[4096, 128], [512, 8], [1, TT]]))

            emit_ln_store(p, sems, y0_toks, store_fn, oX)


def emit_l0ffn(p, S):
    gX, gC, sW, sX, oX, vX = S.gX, S.gC, S.sW, S.sX, S.oX, S.vX
    gP0 = S.gP0
    tQ, cS, cV = S.tQ, S.cS, S.cV
    sems = dict(tQ=tQ, cS=cS, cV=cV)
    CSZ = 2048 * 512
    TR_S, TR_TG, TR_E, TR_NC = C_TRIPS

    R = lambda n: p.r[n] if not p.dry else None
    io = p.io

    pc0 = gP0.tok()

    p.op("gpsimd", gC, lambda e: e.iota(R("it1")[:, :], [[1, 128]], channel_multiplier=0,
                                        allow_small_or_imprecise_dtypes=True))
    it = p.op("gpsimd", gC, lambda e: e.iota(R("it2")[:, :], [[0, 128]], channel_multiplier=1,
                                             allow_small_or_imprecise_dtypes=True))
    p.wait("vector", gC, it)
    ones_tok = p.op("vector", cV, lambda e: [
        e.tensor_tensor(R("identf")[:, :], R("it1")[:, :], R("it2")[:, :], ALU.is_equal),
        e.memset(R("ones128")[:, :], 1.0),
        e.memset(R("onesbf")[:, :], 1.0)][-1])
    p.op("gpsimd", gX, lambda e: e.dma_start(R("wrb")[:, :], io.wr0[:, :]))

    with p.loop(TR_S) as s_lv:
        with p.loop(TR_TG) as tg_lv:
            p.barrier()
            p.wait("vector", oX, oX.tok())
            p.wait("scalar", oX, oX.tok())
            p.wait("sync", gP0, pc0)
            p.wait("sync", sW, sW.tok())
            for par in range(2):
                p.op("sync", sW, lambda e, par=par: e.dma_start(
                    bass.AP(R("wb1_" + str(par)), 0, [[4096, 128], [512, 8], [1, 512]]),
                    bass.AP(io.wbf0, par * CSZ, [[512, 128], [128 * 512, 8], [1, 512]])))
                p.op("sync", sW, lambda e, par=par: e.dma_start(
                    bass.AP(R("wb2_" + str(par)), 0, [[4096, 128], [1024, 4], [1, 1024]]),
                    bass.AP(io.wbf0, par * CSZ + 1024 * 512, [[1024, 128], [128 * 1024, 4], [1, 1024]])))
            p.op("gpsimd", gX, lambda e: e.dma_start(
                bass.AP(R("xbf"), 0, [[8192, 128], [1024, 8], [1, 1024]]),
                bass.AP(io.x1, p.off("gpsimd", 0, [(s_lv, 1024 * 4096), (tg_lv, 1024)]),
                        [[4096, 128], [128 * 4096, 8], [1, 1024]])))
            xbf_tok = gX.tok()
            p.op("vector", cV, lambda e: e.memset(R("outacc")[:, :], 0.0))

            ROUTER = C_STAGE >= 1
            EXPERTS = C_STAGE >= 2
            p.wait("tensor", gX, xbf_tok)
            p.wait("tensor", cV, ones_tok)
            lg_toks = []
            for t2 in range(2 if ROUTER else 0):
                plg = ("pA", "pB")[t2]
                mt = p.op("tensor", tQ, lambda e, t2=t2, plg=plg: [
                    e.matmul(R(plg)[0:8, :], R("wrb")[:, kc * 8:(kc + 1) * 8],
                             R("xbf")[:, kc * 1024 + t2 * 512: kc * 1024 + t2 * 512 + 512],
                             start=(kc == 0), stop=(kc == 7)) for kc in range(8)][-1])
                p.wait("scalar", tQ, mt)
                lg_toks.append(p.op("scalar", cS, lambda e, t2=t2, plg=plg: e.copy(
                    R("lg")[0:8, t2 * 512:(t2 + 1) * 512], R(plg)[0:8, :])))
            c_toks = []
            rtm_toks = []
            for sub in range(8 if ROUTER else 0):
                p.wait("tensor", cS, lg_toks[1])
                if sub >= 1:
                    p.wait("tensor", cS, rtm_toks[sub - 1])
                tt = p.op("tensor", tQ, lambda e, sub=sub: e.transpose(
                    R("pA")[0:128, 0:8], R("lg")[0:8, sub * 128:(sub + 1) * 128],
                    R("identf")[0:8, 0:8]))
                p.wait("scalar", tQ, tt)
                if sub >= 1:
                    p.wait("scalar", cV, c_toks[sub - 1])
                rt = p.op("scalar", cS, lambda e: e.copy(R("rtm")[:, :], R("pA")[0:128, 0:8]))
                rtm_toks.append(rt)
                p.wait("vector", cS, rt)
                p.wait("scalar" if False else "vector", cS, rt)
                t = p.op("vector", cV, lambda e: e.reduce_max(
                    R("srow")[:, 0:1], R("rtm")[:, :], mybir.AxisListType.X))
                p.wait("vector", cV, t)
                ct = p.op("vector", cV, lambda e, sub=sub: e.tensor_scalar(
                    R("ctm")[:, sub * 8:(sub + 1) * 8], R("rtm")[:, :],
                    R("srow")[:, 0:1], None, ALU.subtract))
                p.wait("scalar", cV, ct)
                et = p.op("scalar", cS, lambda e, sub=sub: e.activation(
                    R("ctm")[:, sub * 8:(sub + 1) * 8], R("ctm")[:, sub * 8:(sub + 1) * 8], AF.Exp))
                p.wait("vector", cS, et)
                c_toks.append(p.vchain("vector", cV, [
                    lambda e, sub=sub: e.reduce_max(R("srow")[:, 3:4],
                                                    R("ctm")[:, sub * 8:(sub + 1) * 8],
                                                    mybir.AxisListType.X),
                    lambda e, sub=sub: e.tensor_scalar(R("rtm")[:, :],
                                                       R("ctm")[:, sub * 8:(sub + 1) * 8],
                                                       R("srow")[:, 3:4], None, ALU.is_lt),
                    lambda e, sub=sub: e.tensor_mul(R("rtm")[:, :], R("rtm")[:, :],
                                                    R("ctm")[:, sub * 8:(sub + 1) * 8]),
                    lambda e: e.reduce_max(R("srow")[:, 4:5], R("rtm")[:, :],
                                           mybir.AxisListType.X),
                    lambda e: e.tensor_add(R("srow")[:, 5:6], R("srow")[:, 3:4],
                                           R("srow")[:, 4:5]),
                    lambda e: e.reciprocal(R("srow")[:, 6:7], R("srow")[:, 5:6]),
                    lambda e, sub=sub: e.tensor_scalar(R("rtm")[:, :],
                                                       R("ctm")[:, sub * 8:(sub + 1) * 8],
                                                       R("srow")[:, 4:5], None, ALU.is_ge),
                    lambda e, sub=sub: e.tensor_mul(R("rtm")[:, :], R("rtm")[:, :],
                                                    R("ctm")[:, sub * 8:(sub + 1) * 8]),
                    lambda e, sub=sub: e.tensor_scalar(R("ctm")[:, sub * 8:(sub + 1) * 8],
                                                       R("rtm")[:, :],
                                                       R("srow")[:, 6:7], None, ALU.mult),
                ]))
            cf_toks = []
            for sub in range(8 if ROUTER else 0):
                p.wait("tensor", cV, c_toks[sub])
                if sub >= 1:
                    p.wait("tensor", cS, cf_toks[sub - 1])
                tt = p.op("tensor", tQ, lambda e, sub=sub: e.transpose(
                    R("pB")[0:8, 0:128], R("ctm")[:, sub * 8:(sub + 1) * 8], R("identf")[:, :]))
                p.wait("scalar", tQ, tt)
                cf_toks.append(p.op("scalar", cS, lambda e, sub=sub: e.copy(
                    R("cfm")[0:8, sub * 128:(sub + 1) * 128], R("pB")[0:8, 0:128])))
            if ROUTER:
                p.wait("sync", cS, cf_toks[-1])
                cdr_tok = p.op("sync", oX, lambda e: e.dma_start(io.cdr0[:, :], R("cfm")[:, :]))
            else:
                cdr_tok = oX.tok()

            if EXPERTS:
                with p.loop(TR_E) as e_lv:
                    cv_start = cV.tok()
                    cs_start = cS.tok()
                    sw_start = sW.tok()
                    p.wait("gpsimd", oX, cdr_tok)
                    p.wait("gpsimd", cV, cv_start)
                    ceb_tok = p.op("gpsimd", gX, lambda e: e.dma_start(
                        R("ceb")[:, :],
                        bass.AP(io.cdr0, p.off("gpsimd", 0, [(e_lv, 1024)]), [[0, 128], [1, 1024]])))
                    p.wait("vector", gX, ceb_tok)
                    xs_toks = []
                    for kc in range(8):
                        for t2 in range(2):
                            xs_toks.append(p.op("vector", cV, lambda e, kc=kc, t2=t2: e.tensor_mul(
                                R("xs")[:, kc * 1024 + t2 * 512: kc * 1024 + t2 * 512 + 512],
                                R("xbf")[:, kc * 1024 + t2 * 512: kc * 1024 + t2 * 512 + 512],
                                R("ceb")[:, t2 * 512:(t2 + 1) * 512])))
                    if C_STAGE >= 3:
                        with p.loop(TR_NC) as nc_lv:
                            sw_it = sW.tok()
                            cs_it = cS.tok()
                            cv_it = cV.tok()
                            for par in range(2):
                                wb1 = "wb1_" + str(par)
                                wb2 = "wb2_" + str(par)
                                p.wait("tensor", sW, sw_it)
                                p.wait("tensor", cV, xs_toks[-1])
                                relu_toks, add_toks = [], []
                                mm1_tok = mm2_tok = None
                                for t2 in range(2):
                                    p.wait("tensor", cS, cs_it if t2 == 0 else relu_toks[-1])
                                    mm1_tok = p.op("tensor", tQ, lambda e, wb1=wb1, t2=t2: [
                                        e.matmul(R(("pC", "pD", "pE", "pF")[oc])[:, :],
                                                 R(wb1)[:, kc * 512 + oc * 128: kc * 512 + (oc + 1) * 128],
                                                 R("xs")[:, kc * 1024 + t2 * 512: kc * 1024 + t2 * 512 + 512],
                                                 start=(kc == 0), stop=(kc == 7))
                                        for oc in range(4) for kc in range(8)][-1])
                                    p.wait("scalar", tQ, mm1_tok)
                                    rt = p.op("scalar", cS, lambda e: [
                                        e.activation(R("hid")[:, oc * 512:(oc + 1) * 512],
                                                     R(("pC", "pD", "pE", "pF")[oc])[:, :], AF.Relu)
                                        for oc in range(4)][-1])
                                    relu_toks.append(rt)
                                    p.wait("tensor", cS, rt)
                                    for jc in range(8):
                                        pw = ("pG", "pH")[jc % 2]
                                        a_idx = t2 * 8 + jc
                                        if a_idx >= 2:
                                            p.wait("tensor", cV, add_toks[a_idx - 2])
                                        else:
                                            p.wait("tensor", cV, cv_it)
                                        mm2_tok = p.op("tensor", tQ, lambda e, wb2=wb2, jc=jc, pw=pw: [
                                            e.matmul(R(pw)[:, :],
                                                     R(wb2)[:, kc2 * 1024 + jc * 128: kc2 * 1024 + (jc + 1) * 128],
                                                     R("hid")[:, kc2 * 512:(kc2 + 1) * 512],
                                                     start=(kc2 == 0), stop=(kc2 == 3)) for kc2 in range(4)][-1])
                                        p.wait("vector", tQ, mm2_tok)
                                        at = p.op("vector", cV, lambda e, jc=jc, t2=t2, pw=pw: e.tensor_add(
                                            R("outacc")[:, jc * 1024 + t2 * 512: jc * 1024 + t2 * 512 + 512],
                                            R("outacc")[:, jc * 1024 + t2 * 512: jc * 1024 + t2 * 512 + 512],
                                            R(pw)[:, :]))
                                        add_toks = add_toks + [at] if False else add_toks
                                        add_toks.append(at)
                                p.wait("sync", tQ, mm1_tok)
                                p.wait("sync", tQ, mm2_tok)
                                p.wait("sync", gP0, pc0)
                                p.op("sync", sW, lambda e, par=par, wb1=wb1: e.dma_start(
                                    bass.AP(R(wb1), 0, [[4096, 128], [512, 8], [1, 512]]),
                                    bass.AP(io.wbf0, p.off("sync", (par + 2) * CSZ,
                                                           [(e_lv, 8 * CSZ), (nc_lv, 2 * CSZ)]),
                                            [[512, 128], [128 * 512, 8], [1, 512]])))
                                p.op("sync", sW, lambda e, par=par, wb2=wb2: e.dma_start(
                                    bass.AP(R(wb2), 0, [[4096, 128], [1024, 4], [1, 1024]]),
                                    bass.AP(io.wbf0, p.off("sync", (par + 2) * CSZ + 1024 * 512,
                                                           [(e_lv, 8 * CSZ), (nc_lv, 2 * CSZ)]),
                                            [[1024, 128], [128 * 1024, 4], [1, 1024]])))

            for t2 in range(2):
                p.wait("vector", cV, cV.tok())
                xt_tok = p.op("scalar", vX, lambda e, t2=t2: e.dma_start(
                    bass.AP(R("xt"), 0, [[4096, 128], [512, 8], [1, 512]]),
                    bass.AP(io.x1, p.off("scalar", t2 * 512, [(s_lv, 1024 * 4096), (tg_lv, 1024)]),
                            [[4096, 128], [128 * 4096, 8], [1, 512]])))
                p.wait("vector", vX, xt_tok)
                y0_toks = []
                for jc in range(8):
                    y0_toks.append(p.op("vector", cV, lambda e, jc=jc, t2=t2: e.tensor_add(
                        R("y0")[:, jc * 512:(jc + 1) * 512],
                        R("outacc")[:, jc * 1024 + t2 * 512: jc * 1024 + t2 * 512 + 512],
                        R("xt")[:, jc * 512:(jc + 1) * 512])))

                def store_fn(e, t2=t2):
                    return e.dma_start(
                        bass.AP(io.x2, p.off("scalar", t2 * 512, [(s_lv, 1024 * 4096), (tg_lv, 1024)]),
                                [[4096, 128], [128 * 4096, 8], [1, 512]]),
                        bass.AP(R("xout"), 0, [[4096, 128], [512, 8], [1, 512]]))

                emit_ln_store(p, sems, y0_toks, store_fn, oX)


def emit_rglru(p, S):
    gX, sX, oX, vX = S.gX, S.sX, S.oX, S.vX
    tQ, cS, cV = S.tQ, S.cS, S.cV
    sems = dict(tQ=tQ, cS=cS, cV=cV)

    R = lambda n: p.r[n] if not p.dry else None
    io = p.io

    p.op("gpsimd", gX, lambda e: e.dma_start(
        bass.AP(R("igwb"), 0, [[1024, 128], [128, 8], [1, 128]]),
        bass.AP(io.igw, 0, [[128, 128], [128 * 128, 8], [1, 128]])))
    p.op("gpsimd", gX, lambda e: e.dma_start(
        bass.AP(R("agwb"), 0, [[1024, 128], [128, 8], [1, 128]]),
        bass.AP(io.agw, 0, [[128, 128], [128 * 128, 8], [1, 128]])))
    p.op("sync", sX, lambda e: e.dma_start(R("igb")[:, :], io.igb[:, :]))
    p.op("sync", sX, lambda e: e.dma_start(R("agb")[:, :], io.agb[:, :]))
    apt = p.op("sync", sX, lambda e: e.dma_start(R("apv")[:, :], io.ap[:, :]))
    ones_tok = p.op("vector", cV, lambda e: [
        e.memset(R("ones128")[:, :], 1.0),
        e.memset(R("onesbf")[:, :], 1.0),
        e.memset(R("onesf")[:, :], 1.0)][-1])
    p.wait("scalar", sX, apt)
    t = p.op("scalar", cS, lambda e: e.activation(R("sp8")[:, 0:8], R("apv")[:, :], AF.Abs))
    p.wait("scalar", cS, t)
    t = p.op("scalar", cS, lambda e: e.activation(R("sp8")[:, 8:16], R("sp8")[:, 0:8], AF.Exp, scale=-1.0))
    p.wait("vector", cS, t)
    t = p.vchain("vector", cV, [
        lambda e: e.tensor_scalar(R("sp8")[:, 16:24], R("sp8")[:, 8:16], 1.0, None, ALU.add),
    ])
    p.wait("scalar", cV, t)
    t = p.op("scalar", cS, lambda e: e.activation(R("sp8")[:, 24:32], R("sp8")[:, 16:24], AF.Ln))
    p.wait("vector", cS, t)
    p.vchain("vector", cV, [
        lambda e: e.tensor_scalar(R("cvec")[:, :], R("apv")[:, :], 0.0, None, ALU.max),
        lambda e: e.tensor_add(R("cvec")[:, :], R("cvec")[:, :], R("sp8")[:, 24:32]),
        lambda e: e.tensor_scalar(R("cvec")[:, :], R("cvec")[:, :], -8.0, None, ALU.mult),
    ])

    with p.loop(SPC) as s_lv:
        p.barrier()
        p.op("vector", cV, lambda e: [
            e.memset(R("lc")[:, :], 0.0),
            e.memset(R("sc")[:, :], 0.0)][-1])
        with p.loop(NTILE) as ti:
            p.barrier()
            p.wait("vector", oX, oX.tok())
            p.op("gpsimd", gX, lambda e: e.dma_start(
                bass.AP(R("xbt"), 0, [[4096, 128], [512, 8], [1, TT]]),
                bass.AP(io.x2, p.off("gpsimd", 0, [(s_lv, 1024 * 4096), (ti, TT)]),
                        [[4096, 128], [128 * 4096, 8], [1, TT]])))
            xbt_tok = gX.tok()
            p.op("gpsimd", S.gP1, lambda e: e.dma_start(
                bass.AP(io.wbf1, p.off("gpsimd", 131072 * 512,
                                       [(s_lv, 8 * 4096 * 512), (ti, 4096 * 512)]),
                        [[512, 4096], [1, 512]]),
                bass.AP(io.w1w2_1, p.off("gpsimd", 131072 * 512,
                                         [(s_lv, 8 * 4096 * 512), (ti, 4096 * 512)]),
                        [[512, 4096], [1, 512]])))
            p.op("scalar", vX, lambda e: e.dma_start(
                bass.AP(R("xt"), 0, [[4096, 128], [512, 8], [1, TT]]),
                bass.AP(io.x2, p.off("scalar", 0, [(s_lv, 1024 * 4096), (ti, TT)]),
                        [[4096, 128], [128 * 4096, 8], [1, TT]])))
            xt_tok = vX.tok()
            p.wait("tensor", gX, xbt_tok)
            p.wait("tensor", cV, ones_tok)

            y0_toks = []
            sig_toks = []
            for h in range(8):
                pig = ("pA", "pB")[h % 2]
                pag = ("pC", "pD")[h % 2]
                if h >= 2:
                    p.wait("tensor", cS, sig_toks[h - 2])
                mt = p.op("tensor", tQ, lambda e, h=h, pig=pig, pag=pag: [
                    e.matmul(R(pig)[:, :], R("igwb")[:, h * 128:(h + 1) * 128],
                             R("xbt")[:, h * 512:(h + 1) * 512], start=True, stop=True),
                    e.matmul(R(pag)[:, :], R("agwb")[:, h * 128:(h + 1) * 128],
                             R("xbt")[:, h * 512:(h + 1) * 512], start=True, stop=True)][-1])
                p.wait("scalar", tQ, mt)
                if h >= 1:
                    p.wait("scalar", cV, y0_toks[h - 1])
                st = p.op("scalar", cS, lambda e, h=h, pig=pig, pag=pag: [
                    e.activation(R("gx")[:, :], R(pig)[:, :], AF.Sigmoid,
                                 bias=R("igb")[:, h: h + 1]),
                    e.activation(R("ga")[:, :], R(pag)[:, :], AF.Sigmoid,
                                 bias=R("agb")[:, h: h + 1])][-1])
                sig_toks.append(st)
                p.wait("vector", cS, st)
                t = p.op("vector", cV, lambda e, h=h: e.tensor_scalar(
                    R("la")[:, :], R("ga")[:, :], R("cvec")[:, h: h + 1], None, ALU.mult))
                p.wait("vector", cV, t)
                t = p.op("vector", cV, lambda e, h=h: e.tensor_tensor_scan(
                    R("li")[:, :], R("onesf")[:, :], R("la")[:, :],
                    R("lc")[:, h: h + 1], ALU.mult, ALU.add))
                p.wait("vector", cV, t)
                lvt = p.op("vector", cV, lambda e, h=h: [
                    e.tensor_copy(R("lc")[:, h: h + 1], R("li")[:, 511:512]),
                    e.tensor_sub(R("lx")[:, :], R("li")[:, :], R("la")[:, :]),
                    e.tensor_scalar(R("lx")[:, :], R("lx")[:, :], LN7, None, ALU.max)][-1])
                p.wait("scalar", cV, lvt)
                et = p.op("scalar", cS, lambda e: [
                    e.activation(R("pp")[:, :], R("lx")[:, :], AF.Exp),
                    e.activation(R("ppi")[:, :], R("lx")[:, :], AF.Exp, scale=-1.0),
                    e.activation(R("a2")[:, :], R("la")[:, :], AF.Exp, scale=2.0)][-1])
                p.wait("vector", cS, et)
                t = p.op("vector", cV, lambda e: [
                    e.tensor_scalar(R("ml")[:, :], R("a2")[:, :], -1.0, 1.0, ALU.mult, ALU.add),
                    e.tensor_scalar(R("ml")[:, :], R("ml")[:, :], 1e-7, None, ALU.max)][-1])
                p.wait("scalar", cV, t)
                sqt = p.op("scalar", cS, lambda e: e.activation(R("ml")[:, :], R("ml")[:, :], AF.Sqrt))
                p.wait("vector", cS, sqt)
                if h == 0:
                    p.wait("vector", vX, xt_tok)
                t = p.op("vector", cV, lambda e, h=h: [
                    e.tensor_mul(R("nx")[:, :], R("xt")[:, h * 512:(h + 1) * 512], R("gx")[:, :]),
                    e.tensor_mul(R("nx")[:, :], R("nx")[:, :], R("ml")[:, :]),
                    e.tensor_mul(R("nx")[:, :], R("nx")[:, :], R("ppi")[:, :])][-1])
                p.wait("vector", cV, t)
                t = p.op("vector", cV, lambda e, h=h: e.tensor_tensor_scan(
                    R("ss")[:, :], R("onesf")[:, :], R("nx")[:, :],
                    R("sc")[:, h: h + 1], ALU.mult, ALU.add))
                p.wait("vector", cV, t)
                y0_toks.append(p.op("vector", cV, lambda e, h=h: [
                    e.tensor_copy(R("sc")[:, h: h + 1], R("ss")[:, 511:512]),
                    e.tensor_mul(R("ss")[:, :], R("pp")[:, :], R("ss")[:, :]),
                    e.tensor_add(R("y0")[:, h * 512:(h + 1) * 512],
                                 R("xt")[:, h * 512:(h + 1) * 512], R("ss")[:, :])][-1]))

            def store_fn(e, s_lv=s_lv, ti=ti):
                return e.dma_start(
                    bass.AP(io.x3, p.off("scalar", 0, [(s_lv, 1024 * 4096), (ti, TT)]),
                            [[4096, 128], [128 * 4096, 8], [1, TT]]),
                    bass.AP(R("xout"), 0, [[4096, 128], [512, 8], [1, TT]]))

            emit_ln_store(p, sems, y0_toks, store_fn, oX)


def emit_l1ffn(p, S):
    gX, gC, sW, sX, oX, vX = S.gX, S.gC, S.sW, S.sX, S.oX, S.vX
    gP1 = S.gP1
    tQ, cS, cV = S.tQ, S.cS, S.cV
    sems = dict(tQ=tQ, cS=cS, cV=cV)
    CSZ = 3072 * 512
    TR_S, TR_TG, TR_E, TR_NC = C_TRIPS

    R = lambda n: p.r[n] if not p.dry else None
    io = p.io

    pc1 = gP1.tok()

    p.op("gpsimd", gC, lambda e: e.iota(R("xt")[:, 0:128], [[1, 128]], channel_multiplier=0,
                                        allow_small_or_imprecise_dtypes=True))
    it = p.op("gpsimd", gC, lambda e: e.iota(R("y0")[:, 0:128], [[0, 128]], channel_multiplier=1,
                                             allow_small_or_imprecise_dtypes=True))
    p.wait("vector", gC, it)
    ones_tok = p.op("vector", cV, lambda e: [
        e.tensor_tensor(R("identf")[:, :], R("xt")[:, 0:128], R("y0")[:, 0:128], ALU.is_equal),
        e.memset(R("ones128")[:, :], 1.0),
        e.memset(R("onesbf")[:, :], 1.0)][-1])
    p.op("gpsimd", gX, lambda e: e.dma_start(R("wrb")[:, :], io.wr1[:, :]))
    p.op("gpsimd", gX, lambda e: e.dma_start(
        R("rbb")[:, :], bass.AP(io.rb, 0, [[0, 128], [1, 8]])))

    with p.loop(TR_S) as s_lv:
        with p.loop(TR_TG) as tg_lv:
            p.barrier()
            p.wait("vector", oX, oX.tok())
            p.wait("scalar", oX, oX.tok())
            p.wait("sync", gP1, pc1)
            p.wait("sync", sW, sW.tok())
            for par in range(2):
                for ci, wdst in ((0, "wa_"), (1, "wg_"), (2, "wb2_")):
                    pat = [[4096, 128], [512, 8], [1, 512]] if ci < 2 else [[4096, 128], [1024, 4], [1, 1024]]
                    spat = [[512, 128], [128 * 512, 8], [1, 512]] if ci < 2 else [[1024, 128], [128 * 1024, 4], [1, 1024]]
                    p.op("sync", sW, lambda e, par=par, ci=ci, wdst=wdst, pat=pat, spat=spat: e.dma_start(
                        bass.AP(R(wdst + str(par)), 0, pat),
                        bass.AP(io.wbf1, par * CSZ + ci * 1024 * 512, spat)))
            p.op("gpsimd", gX, lambda e: e.dma_start(
                bass.AP(R("xbf"), 0, [[8192, 128], [1024, 8], [1, 1024]]),
                bass.AP(io.x3, p.off("gpsimd", 0, [(s_lv, 1024 * 4096), (tg_lv, 1024)]),
                        [[4096, 128], [128 * 4096, 8], [1, 1024]])))
            xbf_tok = gX.tok()
            p.op("vector", cV, lambda e: e.memset(R("outacc")[:, :], 0.0))

            p.wait("tensor", gX, xbf_tok)
            p.wait("tensor", cV, ones_tok)
            lg_toks = []
            for t2 in range(2):
                plg = ("pA", "pB")[t2]
                mt = p.op("tensor", tQ, lambda e, t2=t2, plg=plg: [
                    e.matmul(R(plg)[0:8, :], R("wrb")[:, kc * 8:(kc + 1) * 8],
                             R("xbf")[:, kc * 1024 + t2 * 512: kc * 1024 + t2 * 512 + 512],
                             start=(kc == 0), stop=(kc == 7)) for kc in range(8)][-1])
                p.wait("scalar", tQ, mt)
                lg_toks.append(p.op("scalar", cS, lambda e, t2=t2, plg=plg: e.copy(
                    R("lg")[0:8, t2 * 512:(t2 + 1) * 512], R(plg)[0:8, :])))
            c_toks = []
            rtm_toks = []
            for sub in range(8):
                p.wait("tensor", cS, lg_toks[1])
                if sub >= 1:
                    p.wait("tensor", cS, rtm_toks[sub - 1])
                tt = p.op("tensor", tQ, lambda e, sub=sub: e.transpose(
                    R("pA")[0:128, 0:8], R("lg")[0:8, sub * 128:(sub + 1) * 128],
                    R("identf")[0:8, 0:8]))
                p.wait("scalar", tQ, tt)
                if sub >= 1:
                    p.wait("scalar", cV, c_toks[sub - 1])
                rt = p.op("scalar", cS, lambda e: e.copy(R("rtm")[:, :], R("pA")[0:128, 0:8]))
                rtm_toks.append(rt)
                p.wait("vector", cS, rt)
                t = p.vchain("vector", cV, [
                    lambda e: e.tensor_add(R("rtm")[:, :], R("rtm")[:, :], R("rbb")[:, :]),
                    lambda e: e.reduce_max(R("srow")[:, 0:1], R("rtm")[:, :], mybir.AxisListType.X),
                    lambda e, sub=sub: e.tensor_scalar(R("ctm")[:, sub * 8:(sub + 1) * 8],
                                                       R("rtm")[:, :], R("srow")[:, 0:1],
                                                       None, ALU.subtract),
                ])
                p.wait("scalar", cV, t)
                et = p.op("scalar", cS, lambda e, sub=sub: e.activation(
                    R("ctm")[:, sub * 8:(sub + 1) * 8], R("ctm")[:, sub * 8:(sub + 1) * 8], AF.Exp))
                p.wait("vector", cS, et)
                c_toks.append(p.vchain("vector", cV, [
                    lambda e, sub=sub: e.reduce_sum(R("srow")[:, 1:2],
                                                    R("ctm")[:, sub * 8:(sub + 1) * 8],
                                                    mybir.AxisListType.X),
                    lambda e: e.reciprocal(R("srow")[:, 2:3], R("srow")[:, 1:2]),
                    lambda e, sub=sub: e.reduce_max(R("srow")[:, 3:4],
                                                    R("ctm")[:, sub * 8:(sub + 1) * 8],
                                                    mybir.AxisListType.X),
                    lambda e: e.tensor_mul(R("srow")[:, 4:5], R("srow")[:, 3:4], R("srow")[:, 2:3]),
                    lambda e, sub=sub: e.tensor_scalar(R("rtm")[:, :],
                                                       R("ctm")[:, sub * 8:(sub + 1) * 8],
                                                       R("srow")[:, 3:4], None, ALU.is_ge),
                    lambda e, sub=sub: e.tensor_scalar(R("ctm")[:, sub * 8:(sub + 1) * 8],
                                                       R("rtm")[:, :],
                                                       R("srow")[:, 4:5], None, ALU.mult),
                ]))
            cf_toks = []
            for sub in range(8):
                p.wait("tensor", cV, c_toks[sub])
                if sub >= 1:
                    p.wait("tensor", cS, cf_toks[sub - 1])
                tt = p.op("tensor", tQ, lambda e, sub=sub: e.transpose(
                    R("pB")[0:8, 0:128], R("ctm")[:, sub * 8:(sub + 1) * 8], R("identf")[:, :]))
                p.wait("scalar", tQ, tt)
                cf_toks.append(p.op("scalar", cS, lambda e, sub=sub: e.copy(
                    R("cfm")[0:8, sub * 128:(sub + 1) * 128], R("pB")[0:8, 0:128])))
            p.wait("sync", cS, cf_toks[-1])
            cdr_tok = p.op("sync", oX, lambda e: e.dma_start(io.cdr1[:, :], R("cfm")[:, :]))

            with p.loop(TR_E) as e_lv:
                cv_start = cV.tok()
                cs_start = cS.tok()
                p.wait("gpsimd", oX, cdr_tok)
                p.wait("gpsimd", cV, cv_start)
                ceb_tok = p.op("gpsimd", gX, lambda e: e.dma_start(
                    R("ceb")[:, :],
                    bass.AP(io.cdr1, p.off("gpsimd", 0, [(e_lv, 1024)]), [[0, 128], [1, 1024]])))
                with p.loop(TR_NC) as nc_lv:
                    sw_it = sW.tok()
                    cs_it = cS.tok()
                    cv_it = cV.tok()
                    for par in range(2):
                        wa, wg, wb2 = f"wa_{par}", f"wg_{par}", f"wb2_{par}"
                        p.wait("tensor", sW, sw_it)
                        p.wait("tensor", gX, ceb_tok) if False else None
                        add_toks = []
                        mm_last = None
                        for t2 in range(2):
                            p.wait("tensor", cS, cs_it if t2 == 0 else sil_tok)
                            if t2 == 0:
                                p.wait("tensor", cV, cv_it)
                            else:
                                p.wait("tensor", cV, hid_tok)
                            mm_a = p.op("tensor", tQ, lambda e, wa=wa, t2=t2: [
                                e.matmul(R(("pC", "pD", "pE", "pF")[oc])[:, :],
                                         R(wa)[:, kc * 512 + oc * 128: kc * 512 + (oc + 1) * 128],
                                         R("xbf")[:, kc * 1024 + t2 * 512: kc * 1024 + t2 * 512 + 512],
                                         start=(kc == 0), stop=(kc == 7))
                                for oc in range(4) for kc in range(8)][-1])
                            p.wait("scalar", tQ, mm_a)
                            sil_tok = p.op("scalar", cS, lambda e: [
                                e.activation(R("sa")[:, oc * 512:(oc + 1) * 512],
                                             R(("pC", "pD", "pE", "pF")[oc])[:, :], AF.Silu)
                                for oc in range(4)][-1])
                            p.wait("tensor", cS, sil_tok)
                            mm_g = p.op("tensor", tQ, lambda e, wg=wg, t2=t2: [
                                e.matmul(R(("pC", "pD", "pE", "pF")[oc])[:, :],
                                         R(wg)[:, kc * 512 + oc * 128: kc * 512 + (oc + 1) * 128],
                                         R("xbf")[:, kc * 1024 + t2 * 512: kc * 1024 + t2 * 512 + 512],
                                         start=(kc == 0), stop=(kc == 7))
                                for oc in range(4) for kc in range(8)][-1])
                            p.wait("vector", tQ, mm_g)
                            p.wait("vector", gX, ceb_tok)
                            gs_tok = p.op("vector", cV, lambda e, t2=t2: [
                                e.tensor_scalar(R("gs")[:, oc * 512:(oc + 1) * 512],
                                                R(("pC", "pD", "pE", "pF")[oc])[:, :],
                                                1.0, None, ALU.mult)
                                for oc in range(4)][-1])
                            p.wait("vector", cS, sil_tok)
                            hid_tok = p.op("vector", cV, lambda e, t2=t2: [
                                [e.tensor_mul(R("gs")[:, oc * 512:(oc + 1) * 512],
                                              R("gs")[:, oc * 512:(oc + 1) * 512],
                                              R("ceb")[:, t2 * 512:(t2 + 1) * 512]),
                                 e.tensor_mul(R("hid")[:, oc * 512:(oc + 1) * 512],
                                              R("gs")[:, oc * 512:(oc + 1) * 512],
                                              R("sa")[:, oc * 512:(oc + 1) * 512])][-1]
                                for oc in range(4)][-1])
                            p.wait("tensor", cV, hid_tok)
                            for jc in range(8):
                                pw = ("pG", "pH")[jc % 2]
                                a_idx = t2 * 8 + jc
                                if a_idx >= 2:
                                    p.wait("tensor", cV, add_toks[a_idx - 2])
                                else:
                                    p.wait("tensor", cV, cv_it)
                                mm_last = p.op("tensor", tQ, lambda e, wb2=wb2, jc=jc, pw=pw: [
                                    e.matmul(R(pw)[:, :],
                                             R(wb2)[:, kc2 * 1024 + jc * 128: kc2 * 1024 + (jc + 1) * 128],
                                             R("hid")[:, kc2 * 512:(kc2 + 1) * 512],
                                             start=(kc2 == 0), stop=(kc2 == 3)) for kc2 in range(4)][-1])
                                p.wait("vector", tQ, mm_last)
                                add_toks.append(p.op("vector", cV, lambda e, jc=jc, t2=t2, pw=pw: e.tensor_add(
                                    R("outacc")[:, jc * 1024 + t2 * 512: jc * 1024 + t2 * 512 + 512],
                                    R("outacc")[:, jc * 1024 + t2 * 512: jc * 1024 + t2 * 512 + 512],
                                    R(pw)[:, :])))
                        p.wait("sync", tQ, mm_last)
                        for ci, wdst in ((0, "wa_"), (1, "wg_"), (2, "wb2_")):
                            pat = [[4096, 128], [512, 8], [1, 512]] if ci < 2 else [[4096, 128], [1024, 4], [1, 1024]]
                            spat = [[512, 128], [128 * 512, 8], [1, 512]] if ci < 2 else [[1024, 128], [128 * 1024, 4], [1, 1024]]
                            p.op("sync", sW, lambda e, par=par, ci=ci, wdst=wdst, pat=pat, spat=spat: e.dma_start(
                                bass.AP(R(wdst + str(par)), 0, pat),
                                bass.AP(io.wbf1, p.off("sync", (par + 2) * CSZ + ci * 1024 * 512,
                                                      [(e_lv, 8 * CSZ), (nc_lv, 2 * CSZ)]),
                                        spat)))

            p.wait("vector", sW, sW.tok())
            for t2 in range(2):
                p.wait("vector", cV, cV.tok())
                xt_tok = p.op("scalar", vX, lambda e, t2=t2: e.dma_start(
                    bass.AP(R("xt"), 0, [[4096, 128], [512, 8], [1, 512]]),
                    bass.AP(io.x3, p.off("scalar", t2 * 512, [(s_lv, 1024 * 4096), (tg_lv, 1024)]),
                            [[4096, 128], [128 * 4096, 8], [1, 512]])))
                p.wait("vector", vX, xt_tok)
                y0_toks = []
                for jc in range(8):
                    y0_toks.append(p.op("vector", cV, lambda e, jc=jc, t2=t2: e.tensor_add(
                        R("y0")[:, jc * 512:(jc + 1) * 512],
                        R("outacc")[:, jc * 1024 + t2 * 512: jc * 1024 + t2 * 512 + 512],
                        R("xt")[:, jc * 512:(jc + 1) * 512])))

                def store_fn(e, t2=t2):
                    return e.dma_start(
                        bass.AP(io.x4, p.off("scalar", t2 * 512, [(s_lv, 1024 * 4096), (tg_lv, 1024)]),
                                [[4096, 128], [128 * 4096, 8], [1, 512]]),
                        bass.AP(R("xout"), 0, [[4096, 128], [512, 8], [1, 512]]))

                emit_ln_store(p, sems, y0_toks, store_fn, oX,
                              yb=("wa_0", 0), y2b=("wg_0", 0))


S1DEFS = [
    ("wqkvb", [128, 8 * 3072], BF16), ("woutb", [128, 8 * 1024], BF16),
    ("bqkv", [128, 24], F32), ("bout", [128, 8], F32), ("omega", [128, 256], BF16),
    ("ones128", [1, 128], F32), ("onesbf", [128, 1], BF16), ("vb", [128, 1024], F32),
    ("bqr", [1, 1024], F32), ("xbt", [128, 4096], BF16), ("xt", [128, 4096], F32),
    ("stk", [128, 16 * 512], BF16), ("vsb", [128, 4 * 16 * 65], BF16),
    ("kvacc", [128, 16 * 130], F32), ("kvbf", [128, 16 * 130], BF16),
    ("kpsb", [128, 2 * 512], BF16), ("rbsb", [64, 512], F32), ("hosb", [128, 8 * 512], BF16),
    ("wo0", [128, 512], F32), ("wo1", [128, 512], F32), ("y0", [128, 4096], F32),
    ("xout", [128, 4096], F32), ("lnrow", [1, 6 * 512], F32), ("mbr", [128, 512], F32),
    ("rbr", [128, 512], F32),
]
S2DEFS = [
    ("identf", [128, 128], F32), ("it1", [128, 128], F32), ("it2", [128, 128], F32),
    ("ones128", [1, 128], F32), ("onesbf", [128, 1], BF16), ("wrb", [128, 64], BF16),
    ("xbf", [128, 8 * 1024], BF16), ("outacc", [128, 8 * 1024], F32),
    ("xs", [128, 8 * 1024], BF16),
    ("wb1_0", [128, 4096], BF16), ("wb1_1", [128, 4096], BF16),
    ("wb2_0", [128, 4096], BF16), ("wb2_1", [128, 4096], BF16),
    ("hid", [128, 4 * 512], BF16), ("ceb", [128, 1024], BF16), ("lg", [8, 1024], F32),
    ("rtm", [128, 8], F32), ("srow", [128, 8], F32), ("ctm", [128, 64], F32),
    ("cfm", [8, 1024], F32), ("xt", [128, 4096], F32), ("y0", [128, 4096], F32),
    ("xout", [128, 4096], F32), ("stk", [128, 8192], BF16), ("lnrow", [1, 6 * 512], F32),
    ("mbr", [128, 512], F32), ("rbr", [128, 512], F32),
]
S3DEFS = [
    ("igwb", [128, 8 * 128], BF16), ("agwb", [128, 8 * 128], BF16),
    ("igb", [128, 8], F32), ("agb", [128, 8], F32), ("apv", [128, 8], F32),
    ("cvec", [128, 8], F32), ("sp8", [128, 4 * 8], F32),
    ("ones128", [1, 128], F32), ("onesbf", [128, 1], BF16), ("onesf", [128, 512], F32),
    ("xt", [128, 4096], F32), ("xbt", [128, 4096], BF16),
    ("gx", [128, 512], F32), ("ga", [128, 512], F32), ("la", [128, 512], F32),
    ("li", [128, 512], F32), ("lx", [128, 512], F32), ("pp", [128, 512], F32),
    ("ppi", [128, 512], F32), ("a2", [128, 512], F32), ("ml", [128, 512], F32),
    ("nx", [128, 512], F32), ("ss", [128, 512], F32),
    ("lc", [128, 8], F32), ("sc", [128, 8], F32), ("y0", [128, 4096], F32),
    ("xout", [128, 4096], F32), ("stk", [128, 8192], BF16), ("lnrow", [1, 6 * 512], F32),
    ("mbr", [128, 512], F32), ("rbr", [128, 512], F32),
]
S4DEFS = [
    ("identf", [128, 128], F32), ("ones128", [1, 128], F32), ("onesbf", [128, 1], BF16),
    ("wrb", [128, 64], BF16), ("rbb", [128, 8], F32), ("xbf", [128, 8 * 1024], BF16),
    ("outacc", [128, 8 * 1024], F32),
    ("wa_0", [128, 4096], BF16), ("wa_1", [128, 4096], BF16),
    ("wg_0", [128, 4096], BF16), ("wg_1", [128, 4096], BF16),
    ("wb2_0", [128, 4096], BF16), ("wb2_1", [128, 4096], BF16),
    ("sa", [128, 2048], F32), ("gs", [128, 2048], F32), ("hid", [128, 4 * 512], BF16),
    ("ceb", [128, 1024], F32), ("lg", [8, 1024], F32), ("rtm", [128, 8], F32),
    ("srow", [128, 8], F32), ("ctm", [128, 64], F32), ("cfm", [8, 1024], F32),
    ("xt", [128, 4096], F32), ("y0", [128, 4096], F32), ("xout", [128, 4096], F32),
    ("lnrow", [1, 6 * 512], F32), ("mbr", [128, 512], F32), ("rbr", [128, 512], F32),
]


def prog_full(p):
    S = NS()
    for owner, name, step in (("gpsimd", "gP0", 16), ("gpsimd", "gP1", 16),
                              ("gpsimd", "gX", 16), ("gpsimd", "gC", 1),
                              ("sync", "sW", 16), ("sync", "sX", 16),
                              ("scalar", "vX", 16),
                              ("sync", "oX", 16), ("tensor", "tQ", 1),
                              ("tensor", "tV", 1), ("scalar", "cS", 1),
                              ("vector", "cV", 1), ("sync", "bS", 1),
                              ("gpsimd", "bG", 1)):
        S[name] = p.sem(owner, name, step)

    if not p.dry:
        nc, C = p.nc, p.ctx
        io = p.io
        io.x = nc.dram_tensor("x", [SPC * 1024, 4096], F32, kind="ExternalInput")
        io.wqkv = nc.dram_tensor("wqkv", [1024, 3072], F32, kind="ExternalInput")
        io.bqkv = nc.dram_tensor("bqkv_t", [128, 24], F32, kind="ExternalInput")
        io.bqv = nc.dram_tensor("bqv", [1, 1024], F32, kind="ExternalInput")
        io.wout = nc.dram_tensor("wout", [1024, 1024], F32, kind="ExternalInput")
        io.bout = nc.dram_tensor("bout_t", [128, 8], F32, kind="ExternalInput")
        io.omega = nc.dram_tensor("omega", [64, 256], F32, kind="ExternalInput")
        io.wr0 = nc.dram_tensor("wr0", [128, 64], F32, kind="ExternalInput")
        io.w1w2_0 = nc.dram_tensor("w1w2_0", [64 * 2048, 512], F32, kind="ExternalInput")
        io.ap = nc.dram_tensor("ap_t", [128, 8], F32, kind="ExternalInput")
        io.igw = nc.dram_tensor("igw", [1024, 128], F32, kind="ExternalInput")
        io.igb = nc.dram_tensor("igb_t", [128, 8], F32, kind="ExternalInput")
        io.agw = nc.dram_tensor("agw", [1024, 128], F32, kind="ExternalInput")
        io.agb = nc.dram_tensor("agb_t", [128, 8], F32, kind="ExternalInput")
        io.wr1 = nc.dram_tensor("wr1", [128, 64], F32, kind="ExternalInput")
        io.rb = nc.dram_tensor("rb", [1, 8], F32, kind="ExternalInput")
        io.w1w2_1 = nc.dram_tensor("w1w2_1", [64 * 3072, 512], F32, kind="ExternalInput")
        io.x4 = nc.dram_tensor("x4", [SPC * 1024, 4096], F32, kind="ExternalOutput")
        io.x1 = nc.dram_tensor("x1", [SPC * 1024, 4096], F32)
        io.x2 = nc.dram_tensor("x2", [SPC * 1024, 4096], F32)
        io.x3 = nc.dram_tensor("x3", [SPC * 1024, 4096], F32)
        io.wbf0 = nc.dram_tensor("wbf0", [66 * 2048, 512], BF16)
        io.wbf1 = nc.dram_tensor("wbf1", [66 * 3072, 512], BF16)
        io.cdr0 = nc.dram_tensor("cdr0", [8, 1024], F32)
        io.cdr1 = nc.dram_tensor("cdr1", [8, 1024], F32)
        io.bsc = nc.dram_tensor("bsc", [4, 4], F32)
        A = Arena(nc, C.st)
        p.sr = []
        for pref, defs in (("s1_", S1DEFS), ("s2_", S2DEFS),
                           ("s3_", S3DEFS), ("s4_", S4DEFS)):
            A.reset()
            d = NS()
            for nm, shape, dt in defs:
                d[nm] = A.sbuf(pref + nm, shape, dt)
            p.sr.append(d)
        A.allocate()
        for nm in ("pA", "pB", "pC", "pD", "pE", "pF", "pG", "pH"):
            h = C.psum(nm, [128, 512])
            for d in p.sr:
                d[nm] = h

    set_stage(p, 0)
    emit_l0attn(p, S)
    stage_barrier(p, S)
    set_stage(p, 1)
    emit_l0ffn(p, S)
    stage_barrier(p, S)
    set_stage(p, 2)
    emit_rglru(p, S)
    stage_barrier(p, S)
    set_stage(p, 3)
    emit_l1ffn(p, S)
    p.finish()
    p.wait("gpsimd", S.oX, S.oX.tok())


def host_full_inputs(inputs):
    d = {
        "wqkv": np.ascontiguousarray(inputs["L0_attn_Wqkv"]),
        "bqkv_t": np.ascontiguousarray(inputs["L0_attn_bqkv"].reshape(24, 128).T),
        "bqv": np.ascontiguousarray(inputs["L0_attn_bqkv"][2048:].reshape(1, 1024)),
        "wout": np.ascontiguousarray(inputs["L0_attn_Wout"]),
        "bout_t": np.ascontiguousarray(inputs["L0_attn_bout"].reshape(8, 128).T),
        "omega": np.ascontiguousarray(inputs["L0_attn_omega"]),
        "wr0": np.ascontiguousarray(
            inputs["L0_ffn_Wr"].reshape(8, 128, 8).transpose(1, 0, 2).reshape(128, 64)),
        "ap_t": np.ascontiguousarray(inputs["L1_attn_a_param"].reshape(8, 128).T),
        "igw": np.ascontiguousarray(inputs["L1_attn_ig_w"].reshape(1024, 128)),
        "igb_t": np.ascontiguousarray(inputs["L1_attn_ig_b"].reshape(8, 128).T),
        "agw": np.ascontiguousarray(inputs["L1_attn_ag_w"].reshape(1024, 128)),
        "agb_t": np.ascontiguousarray(inputs["L1_attn_ag_b"].reshape(8, 128).T),
        "wr1": np.ascontiguousarray(
            inputs["L1_ffn_router_w"].reshape(8, 128, 8).transpose(1, 0, 2).reshape(128, 64)),
        "rb": np.ascontiguousarray(inputs["L1_ffn_router_b"].reshape(1, 8)),
    }
    W1, W2 = inputs["L0_ffn_W1"], inputs["L0_ffn_W2"]
    chunks = []
    for e in range(8):
        for ncc in range(8):
            chunks.append(np.ascontiguousarray(W1[e][:, ncc * 512:(ncc + 1) * 512]))
            chunks.append(W2[e][ncc * 512:(ncc + 1) * 512, :].reshape(1024, 512))
    d["w1w2_0"] = np.ascontiguousarray(np.concatenate(chunks, axis=0))
    w1, w2 = inputs["L1_ffn_w1"], inputs["L1_ffn_w2"]
    chunks = []
    for e in range(8):
        for ncc in range(8):
            chunks.append(np.ascontiguousarray(w1[e][:, ncc * 512:(ncc + 1) * 512]))
            chunks.append(np.ascontiguousarray(w1[e][:, 4096 + ncc * 512: 4096 + (ncc + 1) * 512]))
            chunks.append(w2[e][ncc * 512:(ncc + 1) * 512, :].reshape(1024, 512))
    d["w1w2_1"] = np.ascontiguousarray(np.concatenate(chunks, axis=0))
    return d


_PROGS = {}


def _get_prog(name, fn):
    if name not in _PROGS:
        _PROGS[name] = build(fn)
    return _PROGS[name]


def kernel(**inputs):
    p = _get_prog("full", prog_full)
    cores = list(range(NCORE))
    shared = host_full_inputs(inputs)
    ims = []
    for c in cores:
        im = dict(shared)
        im["x"] = np.ascontiguousarray(
            inputs["L0_attn_x"][2 * c: 2 * c + 2].transpose(0, 2, 1)).reshape(SPC * 1024, 4096)
        ims.append(im)
    res = run_bass_kernel_spmd(p.nc, ims, core_ids=cores)

    out = np.empty((B, N, D), np.float32)
    for c in cores:
        x4 = np.asarray(res.results[c]["x4"]).reshape(SPC, 1024, 4096)
        out[2 * c: 2 * c + 2] = x4.transpose(0, 2, 1)
    return out


# version 05
# speedup vs baseline: 1.1041x; 1.0797x over previous
import contextlib
import numpy as np
import bass_rust
import concourse.bass as bass
import concourse.mybir as mybir
from concourse.bass_utils import run_bass_kernel_spmd

F32 = mybir.dt.float32
BF16 = mybir.dt.bfloat16
AF = mybir.ActivationFunctionType
ALU = mybir.AluOpType

B, N, D = 16, 4096, 1024
NCORE = 8
SPC = 2
TT = 512
NTILE = N // TT
NH, MF, DH = 16, 256, 64
LN_EPS = 1e-5
LN7 = -16.11809565095832
SEM_SHIFT = 0
C_TRIPS = (SPC, 4, 8, 4)
C_STAGE = 3
ENGS = ("sync", "tensor", "vector", "scalar", "gpsimd")


class LoopVar:
    __slots__ = ("lvl", "var")

    def __init__(self, lvl, var):
        self.lvl = lvl
        self.var = var


class Tok:
    __slots__ = ("c", "terms")

    def __init__(self, c, terms=()):
        self.c = c
        self.terms = tuple(terms)

    def __add__(self, k):
        return Tok(self.c + k, self.terms)


class Sem:
    def __init__(self, p, name, owner, step=1):
        self.p, self.name, self.owner, self.step = p, name, owner, step
        self.c = SEM_SHIFT
        self.stack = []
        if not p.dry:
            self.h = p.ctx.sem(name)
            if SEM_SHIFT:
                getattr(p.nc, owner).sem_inc(self.h, SEM_SHIFT)

    def tok(self, delta=0):
        return Tok(self.c + delta, [(lvl, pi) for (lvl, _s, pi) in self.stack if pi])


class P:
    MAXDEPTH = 4

    def __init__(self, dry, totals=None):
        self.dry = dry
        self.totals = totals if totals is not None else {}
        self.loop_seq = 0
        self.depth = 0
        self.sems = []
        self.r = NS()
        self.io = NS()
        self.regs = {}
        if not dry:
            self.ctx = Ctx()
            self.nc = self.ctx.nc
            for eng in ENGS:
                E = getattr(self.nc, eng)
                self.regs[eng] = {
                    "ri": [E.alloc_register(f"{eng}_ri{l}") for l in range(self.MAXDEPTH)],
                    "t0": E.alloc_register(f"{eng}_t0"),
                    "t1": E.alloc_register(f"{eng}_t1"),
                }

    def sem(self, owner, name, step=1):
        s = Sem(self, name, owner, step)
        self.sems.append(s)
        return s

    @contextlib.contextmanager
    def loop(self, n):
        idx = self.loop_seq
        self.loop_seq += 1
        lvl = self.depth
        self.depth += 1

        if self.dry:
            snap = [(s, s.c) for s in self.sems]
            for s in self.sems:
                s.stack.append([lvl, s.c, 0])
            yield LoopVar(lvl, 0)
            pis = {}
            for s, c0 in snap:
                pi = s.c - c0
                pis[s.name] = pi
                s.stack.pop()
                s.c = c0 + n * pi
            self.totals[idx] = pis
        else:
            pis = self.totals[idx]
            nc = self.nc
            loop_start = f"ploop{idx}_loop"
            loop_end = f"ploop{idx}_end"
            ris = bass_rust.RegisterHandles(
                tuple(self.regs[eng]["ri"][lvl] for eng in ENGS))
            for eng in ENGS:
                E = getattr(nc, eng)
                E.reg_mov(self.regs[eng]["ri"][lvl], -1)
            nc.br(loop_start)
            with nc.body(loop_start):
                for eng in ENGS:
                    E = getattr(nc, eng)
                    E.reg_add(self.regs[eng]["ri"][lvl],
                              self.regs[eng]["ri"][lvl], 1)
                snap = [(s, s.c) for s in self.sems]
                for s in self.sems:
                    s.stack.append([lvl, s.c, pis.get(s.name, 0)])
                yield LoopVar(lvl, None)
                for s, c0 in snap:
                    got = s.c - c0
                    assert got == pis.get(s.name, 0), \
                        f"loop {idx} sem {s.name}: emitted {got} != dry {pis.get(s.name, 0)}"
                    s.stack.pop()
                    s.c = c0 + n * got
                nc.br_lt(ris, n - 1, on_true=loop_start, on_false=loop_end)
            nc.switch_bb(loop_end)
        self.depth -= 1

    def _regval(self, eng, const, terms):
        E = getattr(self.nc, eng)
        rg = self.regs[eng]
        first = True
        for lvl, coef in terms:
            if coef == 0:
                continue
            dst = rg["t0"] if first else rg["t1"]
            E.reg_mul(dst, rg["ri"][lvl], coef)
            if not first:
                E.reg_add(rg["t0"], rg["t0"], rg["t1"])
            first = False
        assert not first
        E.reg_add(rg["t0"], rg["t0"], const)
        return rg["t0"]

    def wait(self, eng, sem, tok):
        if self.dry:
            return
        terms = [(l, c) for (l, c) in tok.terms if c]
        if not terms:
            assert tok.c >= 0
            getattr(self.nc, eng).wait_ge(sem.h, tok.c)
        else:
            reg = self._regval(eng, tok.c, terms)
            getattr(self.nc, eng).wait_ge(sem.h, reg)

    def off(self, eng, const, terms):
        E = getattr(self.nc, eng)
        rg = self.regs[eng]
        terms = [(lv, c) for (lv, c) in terms if c]
        if not terms:
            return const
        o = E.alloc_register()
        first = True
        for lv, coef in terms:
            if first:
                E.reg_mul(o, rg["ri"][lv.lvl], coef)
            else:
                E.reg_mul(rg["t1"], rg["ri"][lv.lvl], coef)
                E.reg_add(o, o, rg["t1"])
            first = False
        E.reg_add(o, o, const)
        return E.snap(o, donate=True)

    def op(self, eng, sem, fn, k=1):
        if not self.dry:
            ins = fn(getattr(self.nc, eng))
            ins.then_inc(sem.h, sem.step * k)
        sem.c += sem.step * k
        return sem.tok()

    def raw(self, eng, fn):
        if not self.dry:
            fn(getattr(self.nc, eng))

    def vchain(self, eng, sem, fns):
        t = None
        for fn in fns:
            t = self.op(eng, sem, fn)
            self.wait(eng, sem, t)
        return t

    def finish(self):
        for s in self.sems:
            if s.step == 16:
                self.wait(s.owner, s, s.tok())

    def barrier(self, sems=None):
        sems = sems if sems is not None else self.sems
        toks = [(s, s.tok()) for s in sems if s.step == 1]
        for eng in ENGS:
            for s, t in toks:
                if s.owner != eng:
                    self.wait(eng, s, t)
        return {s.name: t for s, t in toks}


class Ctx:
    def __init__(self):
        self.nc = bass.Bass("TRN2", target_bir_lowering=False, detect_race_conditions=False)
        self.st = contextlib.ExitStack()

    def sem(self, name):
        return self.st.enter_context(self.nc.semaphore(name))

    def sbuf(self, name, shape, dtype):
        return self.st.enter_context(self.nc.sbuf_tensor(name, shape, dtype))

    def psum(self, name, shape, dtype=F32):
        return self.st.enter_context(self.nc.psum_tensor(name, shape, dtype))


def build(program_fn, *args):
    p1 = P(dry=True)
    program_fn(p1, *args)
    p2 = P(dry=False, totals=p1.totals)
    program_fn(p2, *args)
    return p2


class NS(dict):
    __getattr__ = dict.__getitem__
    __setattr__ = dict.__setitem__


class Arena:

    def __init__(self, nc, st):
        self.nc, self.st = nc, st
        self.addr = 0
        self.size = 0
        self.tensors = []

    def sbuf(self, name, shape, dtype):
        per = 1
        for s in shape[1:]:
            per *= s
        per *= mybir.dt.size(dtype)
        a = self.addr
        self.addr = (a + per + 31) & ~31
        self.size = max(self.size, self.addr)
        h = self.nc.virtual_sbuf_tensor(name, shape, dtype)
        self.tensors.append((h, a))
        return h

    def reset(self):
        self.addr = 0

    def allocate(self):
        handle = self.st.enter_context(
            self.nc.sbuf_tensor("arena", [128, self.size], mybir.dt.uint8))
        off = self.nc.lookup_mloc(handle).addr
        for h, a in self.tensors:
            self.nc._tensor(h.name, h.shape, h.dtype, type="SB")
            m = self.nc.lookup_mloc(h)
            m.allocated = True
            m.addr = off + a


def set_stage(p, k):
    if not p.dry:
        p.r = p.sr[k]


def stage_barrier(p, S):
    for s in p.sems:
        if s.step == 16 and s.name not in ("gP0", "gP1"):
            p.wait(s.owner, s, s.tok())
    R = lambda n: p.r[n] if not p.dry else None
    io = p.io
    incs = []
    t = p.op("tensor", S.tQ, lambda e: e.matmul(
        R("pA")[0:1, 0:1], R("onesbf")[:, 0:1], R("onesbf")[:, 0:1],
        start=True, stop=True))
    incs.append(("tensor", S.tQ, t))
    t = p.op("vector", S.cV, lambda e: e.memset(R("lnrow")[0:1, 0:1], 0.0))
    incs.append(("vector", S.cV, t))
    t = p.op("scalar", S.cS, lambda e: e.copy(R("lnrow")[0:1, 2:3], R("lnrow")[0:1, 3:4]))
    incs.append(("scalar", S.cS, t))
    t = p.op("sync", S.oX, lambda e: e.dma_start(io.bsc[0:1, :], io.bsc[1:2, :]))
    incs.append(("sync", S.oX, t))
    t = p.op("gpsimd", S.gX, lambda e: e.dma_start(io.bsc[2:3, :], io.bsc[3:4, :]))
    incs.append(("gpsimd", S.gX, t))
    for eng in ENGS:
        for o, sem, tk in incs:
            if o != eng:
                p.wait(eng, sem, tk)


def emit_ln_store(p, sems, y0_toks, store_fn, store_sem, yb=("stk", 0), y2b=("stk", 4096)):
    tQ, cS, cV = sems["tQ"], sems["cS"], sems["cV"]
    R = lambda n: p.r[n] if not p.dry else None
    ybn, ybo = yb
    y2n, y2o = y2b
    cast_toks, sq_toks = [], []
    for jc in range(8):
        ct = p.op("vector", cV, lambda e, jc=jc: e.tensor_copy(
            R(ybn)[:, ybo + jc * 512: ybo + (jc + 1) * 512], R("y0")[:, jc * 512:(jc + 1) * 512]))
        p.wait("scalar", cV, y0_toks[jc])
        st = p.op("scalar", cS, lambda e, jc=jc: e.activation(
            R(y2n)[:, y2o + jc * 512: y2o + (jc + 1) * 512],
            R("y0")[:, jc * 512:(jc + 1) * 512], AF.Square))
        cast_toks.append(ct)
        sq_toks.append(st)
    p.wait("tensor", cV, cast_toks[-1])
    p.wait("tensor", cS, sq_toks[-1])
    p.op("tensor", tQ, lambda e: [
        e.matmul(R("pC")[0:1, :], R("onesbf")[:, :],
                 R(ybn)[:, ybo + kc * 512: ybo + (kc + 1) * 512],
                 start=(kc == 0), stop=(kc == 7)) for kc in range(8)][-1])
    t2 = p.op("tensor", tQ, lambda e: [
        e.matmul(R("pD")[0:1, :], R("onesbf")[:, :],
                 R(y2n)[:, y2o + kc * 512: y2o + (kc + 1) * 512],
                 start=(kc == 0), stop=(kc == 7)) for kc in range(8)][-1])
    p.wait("scalar", tQ, t2)
    p.op("scalar", cS, lambda e: e.mul(R("lnrow")[0:1, 0:512], R("pC")[0:1, :], 1.0 / 1024))
    p.op("scalar", cS, lambda e: e.mul(R("lnrow")[0:1, 512:1024], R("pD")[0:1, :], 1.0 / 1024))
    mt = p.op("scalar", cS, lambda e: e.activation(
        R("lnrow")[0:1, 1024:1536], R("lnrow")[0:1, 0:512], AF.Square))
    p.wait("vector", cS, mt)
    vt = p.op("vector", cV, lambda e: [
        e.tensor_sub(R("lnrow")[0:1, 1536:2048], R("lnrow")[0:1, 512:1024],
                     R("lnrow")[0:1, 1024:1536]),
        e.tensor_scalar(R("lnrow")[0:1, 1536:2048], R("lnrow")[0:1, 1536:2048],
                        LN_EPS, None, ALU.add)][-1])
    p.wait("scalar", cV, vt)
    sdt = p.op("scalar", cS, lambda e: e.activation(
        R("lnrow")[0:1, 2048:2560], R("lnrow")[0:1, 1536:2048], AF.Sqrt))
    p.wait("vector", cS, sdt)
    rst = p.op("vector", cV, lambda e: e.reciprocal(
        R("lnrow")[0:1, 2560:3072], R("lnrow")[0:1, 2048:2560]))
    p.wait("tensor", cV, rst)
    bt = p.op("tensor", tQ, lambda e: [
        e.matmul(R("pE")[:, :], R("ones128")[0:1, :], R("lnrow")[0:1, 0:512], start=True, stop=True),
        e.matmul(R("pF")[:, :], R("ones128")[0:1, :], R("lnrow")[0:1, 2560:3072], start=True, stop=True)][-1])
    p.wait("scalar", tQ, bt)
    cbt = p.op("scalar", cS, lambda e: [
        e.copy(R("mbr")[:, :], R("pE")[:, :]),
        e.copy(R("rbr")[:, :], R("pF")[:, :])][-1])
    p.wait("vector", cS, cbt)
    xt_ = None
    for jc in range(8):
        xt_ = p.op("vector", cV, lambda e, jc=jc: [
            e.tensor_sub(R("xout")[:, jc * 512:(jc + 1) * 512],
                         R("y0")[:, jc * 512:(jc + 1) * 512], R("mbr")[:, :]),
            e.tensor_mul(R("xout")[:, jc * 512:(jc + 1) * 512],
                         R("xout")[:, jc * 512:(jc + 1) * 512], R("rbr")[:, :])][-1])
    p.wait("scalar", cV, xt_)
    p.op("scalar", store_sem, lambda e: store_fn(e))


def emit_l0attn(p, S):
    gX, sX, oX, vX = S.gX, S.sX, S.oX, S.vX
    tQ, tV, cS, cV = S.tQ, S.tV, S.cS, S.cV
    sems = dict(tQ=tQ, cS=cS, cV=cV)

    R = lambda n: p.r[n] if not p.dry else None
    io = p.io

    for kc in range(8):
        p.op("gpsimd", gX, lambda e, kc=kc: e.dma_start(
            R("wqkvb")[:, kc * 3072:(kc + 1) * 3072], io.wqkv[kc * 128:(kc + 1) * 128, :]))
        p.op("gpsimd", gX, lambda e, kc=kc: e.dma_start(
            R("woutb")[:, kc * 1024:(kc + 1) * 1024], io.wout[kc * 128:(kc + 1) * 128, :]))
    p.op("sync", sX, lambda e: e.dma_start(R("bqkv")[:, :], io.bqkv[:, :]))
    p.op("sync", sX, lambda e: e.dma_start(R("bout")[:, :], io.bout[:, :]))
    p.op("sync", sX, lambda e: e.dma_start(R("bqr")[0:1, :], io.bqv[0:1, :]))
    p.raw("gpsimd", lambda e: e.memset(R("omega")[:, :], -0.5))
    p.op("gpsimd", gX, lambda e: e.dma_start(R("omega")[0:64, :], io.omega[:, :]))
    p.raw("gpsimd", lambda e: e.memset(R("vsb")[:, :], 1.0))
    ones_tok = p.op("vector", cV, lambda e: [
        e.memset(R("ones128")[:, :], 1.0),
        e.memset(R("onesbf")[:, :], 1.0)][-1])
    p.wait("tensor", gX, gX.tok())
    p.wait("tensor", sX, sX.tok())
    p.wait("tensor", cV, ones_tok)
    bt = p.op("tensor", tQ, lambda e: [
        e.matmul(R("pA")[:, :], R("ones128")[0:1, :], R("bqr")[0:1, 0:512], start=True, stop=True),
        e.matmul(R("pB")[:, :], R("ones128")[0:1, :], R("bqr")[0:1, 512:1024], start=True, stop=True)][-1])
    p.wait("scalar", tQ, bt)
    p.op("scalar", cS, lambda e: [
        e.copy(R("vb")[:, 0:512], R("pA")[:, :]),
        e.copy(R("vb")[:, 512:1024], R("pB")[:, :])][-1])

    with p.loop(SPC) as s_lv:
        p.barrier()
        p.op("vector", cV, lambda e: e.memset(R("kvacc")[:, :], 0.0))

        with p.loop(NTILE) as ti:
            p.barrier()
            p.op("gpsimd", gX, lambda e: e.dma_start(
                bass.AP(R("xbt"), 0, [[4096, 128], [512, 8], [1, TT]]),
                bass.AP(io.x, p.off("gpsimd", 0, [(s_lv, 1024 * 4096), (ti, TT)]),
                        [[4096, 128], [128 * 4096, 8], [1, TT]])))
            p.op("gpsimd", S.gP0, lambda e: e.dma_start(
                bass.AP(io.wbf0, p.off("gpsimd", 0,
                                       [(s_lv, 8 * 8192 * 512), (ti, 8192 * 512)]),
                        [[512, 8192], [1, 512]]),
                bass.AP(io.w1w2_0, p.off("gpsimd", 0,
                                         [(s_lv, 8 * 8192 * 512), (ti, 8192 * 512)]),
                        [[512, 8192], [1, 512]])))
            p.wait("tensor", gX, gX.tok())

            ev = []
            for j in range(8):
                ps = "pA" if j % 2 == 0 else "pB"
                if j >= 2:
                    p.wait("tensor", cS, ev[j - 2])
                mt = p.op("tensor", tQ, lambda e, j=j, ps=ps: [
                    e.matmul(R(ps)[:, :],
                             R("wqkvb")[:, kc * 3072 + 1024 + j * 128: kc * 3072 + 1024 + (j + 1) * 128],
                             R("xbt")[:, kc * 512:(kc + 1) * 512],
                             start=(kc == 0), stop=(kc == 7)) for kc in range(8)][-1])
                p.wait("scalar", tQ, mt)
                h0, h1 = 2 * j, 2 * j + 1
                et = p.op("scalar", cS, lambda e, j=j, ps=ps, h0=h0, h1=h1: [
                    e.activation(R("stk")[0:64, h0 * 512:(h0 + 1) * 512], R(ps)[0:64, :],
                                 AF.Identity, bias=R("bqkv")[0:64, 8 + j: 9 + j]),
                    e.activation(R("stk")[64:128, h0 * 512:(h0 + 1) * 512], R(ps)[0:64, :],
                                 AF.Square, bias=R("bqkv")[0:64, 8 + j: 9 + j]),
                    e.activation(R("stk")[0:64, h1 * 512:(h1 + 1) * 512], R(ps)[64:128, :],
                                 AF.Identity, bias=R("bqkv")[64:128, 8 + j: 9 + j]),
                    e.activation(R("stk")[64:128, h1 * 512:(h1 + 1) * 512], R(ps)[64:128, :],
                                 AF.Square, bias=R("bqkv")[64:128, 8 + j: 9 + j])][-1])
                ev.append(et)

            vev = []
            for idx in range(8):
                sc, grp = idx // 2, idx % 2
                ps = "pC" if idx % 2 == 0 else "pD"
                if idx >= 2:
                    p.wait("tensor", cV, vev[idx - 2])
                mt = p.op("tensor", tQ, lambda e, sc=sc, grp=grp, ps=ps: [
                    e.matmul(R(ps)[:, :],
                             R("xbt")[:, kc * 512 + sc * 128: kc * 512 + (sc + 1) * 128],
                             R("wqkvb")[:, kc * 3072 + 2048 + grp * 512: kc * 3072 + 2048 + (grp + 1) * 512],
                             start=(kc == 0), stop=(kc == 7)) for kc in range(8)][-1])
                p.wait("vector", tQ, mt)

                def vevac(e, sc=sc, grp=grp, ps=ps):
                    dst = bass.AP(R("vsb"), (sc * 16 + grp * 8) * 65,
                                  [[4 * 16 * 65, 128], [65, 8], [1, 64]])
                    src = bass.AP(R(ps), 0, [[512, 128], [64, 8], [1, 64]])
                    bia = bass.AP(R("vb"), grp * 512, [[1024, 128], [64, 8], [1, 64]])
                    return e.tensor_add(dst, src, bia)
                vev.append(p.op("vector", cV, vevac))
            p.wait("tensor", cV, vev[-1])

            kvadd = []
            exp_toks = {}
            for h in range(16):
                if h >= 1:
                    p.wait("tensor", cV, kvadd[h - 1])
                p.wait("tensor", cS, ev[h // 2])
                kt = None
                for sc in range(4):
                    pf = "pE" if sc % 2 == 0 else "pF"
                    if sc >= 2:
                        p.wait("tensor", cS, exp_toks[(h, sc - 2)])
                    elif h >= 1:
                        p.wait("tensor", cS, exp_toks[(h - 1, sc + 2)])
                    ft = p.op("tensor", tQ, lambda e, h=h, sc=sc, pf=pf: e.matmul(
                        R(pf)[:, 0:256], R("stk")[:, h * 512 + sc * 128: h * 512 + (sc + 1) * 128],
                        R("omega")[:, :], start=True, stop=True))
                    p.wait("scalar", tQ, ft)
                    p.wait("scalar", tV, tV.tok())
                    et = p.op("scalar", cS, lambda e, sc=sc, pf=pf: e.activation(
                        R("kpsb")[:, (sc % 2) * 512: (sc % 2) * 512 + 256], R(pf)[:, 0:256], AF.Exp))
                    exp_toks[(h, sc)] = et
                    p.wait("tensor", cS, et)
                    kt = p.op("tensor", tV, lambda e, h=h, sc=sc: [
                        e.matmul(R(("pG", "pH")[mc])[:, 0:65],
                                 R("kpsb")[:, (sc % 2) * 512 + mc * 128: (sc % 2) * 512 + mc * 128 + 128],
                                 R("vsb")[:, (sc * 16 + h) * 65: (sc * 16 + h + 1) * 65],
                                 start=(sc == 0), stop=(sc == 3)) for mc in range(2)][-1])
                p.wait("vector", tV, kt)
                kvadd.append(p.op("vector", cV, lambda e, h=h: [
                    e.tensor_add(R("kvacc")[:, h * 130: h * 130 + 65],
                                 R("kvacc")[:, h * 130: h * 130 + 65], R("pG")[:, 0:65]),
                    e.tensor_add(R("kvacc")[:, h * 130 + 65: h * 130 + 130],
                                 R("kvacc")[:, h * 130 + 65: h * 130 + 130], R("pH")[:, 0:65])][-1]))

        kvcast = p.op("vector", cV, lambda e: e.tensor_copy(R("kvbf")[:, :], R("kvacc")[:, :]))

        with p.loop(NTILE) as ti:
            p.barrier()
            p.wait("vector", oX, oX.tok())
            p.op("gpsimd", gX, lambda e: e.dma_start(
                bass.AP(R("xbt"), 0, [[4096, 128], [512, 8], [1, TT]]),
                bass.AP(io.x, p.off("gpsimd", 0, [(s_lv, 1024 * 4096), (ti, TT)]),
                        [[4096, 128], [128 * 4096, 8], [1, TT]])))
            p.op("gpsimd", S.gP1, lambda e: e.dma_start(
                bass.AP(io.wbf1, p.off("gpsimd", 0,
                                       [(s_lv, 8 * 8192 * 512), (ti, 8192 * 512)]),
                        [[512, 8192], [1, 512]]),
                bass.AP(io.w1w2_1, p.off("gpsimd", 0,
                                         [(s_lv, 8 * 8192 * 512), (ti, 8192 * 512)]),
                        [[512, 8192], [1, 512]])))
            p.op("scalar", vX, lambda e: e.dma_start(
                bass.AP(R("xt"), 0, [[4096, 128], [512, 8], [1, TT]]),
                bass.AP(io.x, p.off("scalar", 0, [(s_lv, 1024 * 4096), (ti, TT)]),
                        [[4096, 128], [128 * 4096, 8], [1, TT]])))
            xload32 = vX.tok()
            p.wait("tensor", gX, gX.tok())

            ev = []
            for j in range(8):
                ps = "pA" if j % 2 == 0 else "pB"
                if j >= 2:
                    p.wait("tensor", cS, ev[j - 2])
                mt = p.op("tensor", tQ, lambda e, j=j, ps=ps: [
                    e.matmul(R(ps)[:, :],
                             R("wqkvb")[:, kc * 3072 + j * 128: kc * 3072 + (j + 1) * 128],
                             R("xbt")[:, kc * 512:(kc + 1) * 512],
                             start=(kc == 0), stop=(kc == 7)) for kc in range(8)][-1])
                p.wait("scalar", tQ, mt)
                h0, h1 = 2 * j, 2 * j + 1
                et = p.op("scalar", cS, lambda e, j=j, ps=ps, h0=h0, h1=h1: [
                    e.activation(R("stk")[0:64, h0 * 512:(h0 + 1) * 512], R(ps)[0:64, :],
                                 AF.Identity, bias=R("bqkv")[0:64, j: j + 1]),
                    e.activation(R("stk")[64:128, h0 * 512:(h0 + 1) * 512], R(ps)[0:64, :],
                                 AF.Square, bias=R("bqkv")[0:64, j: j + 1]),
                    e.activation(R("stk")[0:64, h1 * 512:(h1 + 1) * 512], R(ps)[64:128, :],
                                 AF.Identity, bias=R("bqkv")[64:128, j: j + 1]),
                    e.activation(R("stk")[64:128, h1 * 512:(h1 + 1) * 512], R(ps)[64:128, :],
                                 AF.Square, bias=R("bqkv")[64:128, j: j + 1])][-1])
                ev.append(et)

            ho_toks, nd_toks, rt_toks = [], [], []
            expb = {}
            for h in range(16):
                if h >= 1:
                    p.wait("tensor", cS, expb[h - 1])
                p.wait("tensor", cS, ev[h // 2])
                qt = p.op("tensor", tQ, lambda e, h=h: [
                    e.matmul(R(("pC", "pD")[mc])[:, :], R("omega")[:, mc * 128:(mc + 1) * 128],
                             R("stk")[:, h * 512:(h + 1) * 512], start=True, stop=True)
                    for mc in range(2)][-1])
                p.wait("scalar", tQ, qt)
                if h >= 1:
                    p.wait("scalar", tV, nd_toks[h - 1])
                et = p.op("scalar", cS, lambda e: [
                    e.activation(R("kpsb")[:, mc * 512:(mc + 1) * 512], R(("pC", "pD")[mc])[:, :],
                                 AF.Exp) for mc in range(2)][-1])
                expb[h] = et
                p.wait("tensor", cS, et)
                if h == 0:
                    p.wait("tensor", cV, kvcast)
                if h >= 1:
                    p.wait("tensor", cV, ho_toks[h - 1])
                nt = p.op("tensor", tV, lambda e, h=h: [
                    e.matmul(R("pE")[0:65, :], R("kvbf")[:, h * 130 + mc * 65: h * 130 + (mc + 1) * 65],
                             R("kpsb")[:, mc * 512:(mc + 1) * 512],
                             start=(mc == 0), stop=(mc == 1)) for mc in range(2)][-1])
                nd_toks.append(nt)
                p.wait("vector", tV, nt)
                dt = p.op("vector", cV, lambda e: [
                    e.tensor_scalar(R("lnrow")[0:1, 0:512], R("pE")[64:65, :], 1e-6, None, ALU.add),
                    e.reciprocal(R("lnrow")[0:1, 512:1024], R("lnrow")[0:1, 0:512])][-1])
                p.wait("tensor", cV, dt)
                if h >= 1:
                    p.wait("tensor", cS, rt_toks[h - 1])
                bt2 = p.op("tensor", tQ, lambda e: e.matmul(
                    R("pF")[0:64, :], R("ones128")[0:1, 0:64], R("lnrow")[0:1, 512:1024],
                    start=True, stop=True))
                p.wait("scalar", tQ, bt2)
                if h >= 1:
                    p.wait("scalar", cV, ho_toks[h - 1])
                rt = p.op("scalar", cS, lambda e: e.copy(R("rbsb")[:, :], R("pF")[0:64, :]))
                rt_toks.append(rt)
                p.wait("vector", cS, rt)
                ho_toks.append(p.op("vector", cV, lambda e, h=h: e.tensor_mul(
                    R("hosb")[(h % 2) * 64:(h % 2) * 64 + 64, (h // 2) * 512:(h // 2 + 1) * 512],
                    R("pE")[0:64, :], R("rbsb")[:, :])))

            p.wait("tensor", cV, ho_toks[-1])
            y0_toks, wtoks = [], []
            for j in range(8):
                ps = "pG" if j % 2 == 0 else "pH"
                wo = "wo0" if j % 2 == 0 else "wo1"
                if j >= 2:
                    p.wait("tensor", cS, wtoks[j - 2])
                mt = p.op("tensor", tQ, lambda e, j=j, ps=ps: [
                    e.matmul(R(ps)[:, :],
                             R("woutb")[:, kc * 1024 + j * 128: kc * 1024 + (j + 1) * 128],
                             R("hosb")[:, kc * 512:(kc + 1) * 512],
                             start=(kc == 0), stop=(kc == 7)) for kc in range(8)][-1])
                p.wait("scalar", tQ, mt)
                wt = p.op("scalar", cS, lambda e, j=j, ps=ps, wo=wo: e.activation(
                    R(wo)[:, :], R(ps)[:, :], AF.Identity, bias=R("bout")[:, j: j + 1]))
                wtoks.append(wt)
                p.wait("vector", cS, wt)
                if j == 0:
                    p.wait("vector", vX, xload32)
                y0_toks.append(p.op("vector", cV, lambda e, j=j, wo=wo: e.tensor_add(
                    R("y0")[:, j * 512:(j + 1) * 512], R(wo)[:, :], R("xt")[:, j * 512:(j + 1) * 512])))

            def store_fn(e, s_lv=s_lv, ti=ti):
                return e.dma_start(
                    bass.AP(io.x1, p.off("scalar", 0, [(s_lv, 1024 * 4096), (ti, TT)]),
                            [[4096, 128], [128 * 4096, 8], [1, TT]]),
                    bass.AP(R("xout"), 0, [[4096, 128], [512, 8], [1, TT]]))

            emit_ln_store(p, sems, y0_toks, store_fn, oX)


def emit_l0ffn(p, S):
    gX, gC, sW, sX, oX, vX = S.gX, S.gC, S.sW, S.sX, S.oX, S.vX
    gP0 = S.gP0
    tQ, cS, cV = S.tQ, S.cS, S.cV
    sems = dict(tQ=tQ, cS=cS, cV=cV)
    CSZ = 2048 * 512
    TR_S, TR_TG, TR_E, TR_NC = C_TRIPS

    R = lambda n: p.r[n] if not p.dry else None
    io = p.io

    pc0 = gP0.tok()

    p.op("gpsimd", gC, lambda e: e.iota(R("it1")[:, :], [[1, 128]], channel_multiplier=0,
                                        allow_small_or_imprecise_dtypes=True))
    it = p.op("gpsimd", gC, lambda e: e.iota(R("it2")[:, :], [[0, 128]], channel_multiplier=1,
                                             allow_small_or_imprecise_dtypes=True))
    p.wait("vector", gC, it)
    ones_tok = p.op("vector", cV, lambda e: [
        e.tensor_tensor(R("identf")[:, :], R("it1")[:, :], R("it2")[:, :], ALU.is_equal),
        e.memset(R("ones128")[:, :], 1.0),
        e.memset(R("onesbf")[:, :], 1.0)][-1])
    p.op("gpsimd", gX, lambda e: e.dma_start(R("wrb")[:, :], io.wr0[:, :]))

    with p.loop(TR_S) as s_lv:
        with p.loop(TR_TG) as tg_lv:
            p.barrier()
            p.wait("vector", oX, oX.tok())
            p.wait("scalar", oX, oX.tok())
            p.wait("sync", gP0, pc0)
            p.wait("sync", sW, sW.tok())
            for par in range(2):
                p.op("sync", sW, lambda e, par=par: e.dma_start(
                    bass.AP(R("wb1_" + str(par)), 0, [[4096, 128], [512, 8], [1, 512]]),
                    bass.AP(io.wbf0, par * CSZ, [[512, 128], [128 * 512, 8], [1, 512]])))
                p.op("sync", sW, lambda e, par=par: e.dma_start(
                    bass.AP(R("wb2_" + str(par)), 0, [[4096, 128], [1024, 4], [1, 1024]]),
                    bass.AP(io.wbf0, par * CSZ + 1024 * 512, [[1024, 128], [128 * 1024, 4], [1, 1024]])))
            p.op("gpsimd", gX, lambda e: e.dma_start(
                bass.AP(R("xbf"), 0, [[8192, 128], [1024, 8], [1, 1024]]),
                bass.AP(io.x1, p.off("gpsimd", 0, [(s_lv, 1024 * 4096), (tg_lv, 1024)]),
                        [[4096, 128], [128 * 4096, 8], [1, 1024]])))
            xbf_tok = gX.tok()
            p.op("vector", cV, lambda e: e.memset(R("outacc")[:, :], 0.0))

            ROUTER = C_STAGE >= 1
            EXPERTS = C_STAGE >= 2
            p.wait("tensor", gX, xbf_tok)
            p.wait("tensor", cV, ones_tok)
            lg_toks = []
            for t2 in range(2 if ROUTER else 0):
                plg = ("pA", "pB")[t2]
                mt = p.op("tensor", tQ, lambda e, t2=t2, plg=plg: [
                    e.matmul(R(plg)[0:8, :], R("wrb")[:, kc * 8:(kc + 1) * 8],
                             R("xbf")[:, kc * 1024 + t2 * 512: kc * 1024 + t2 * 512 + 512],
                             start=(kc == 0), stop=(kc == 7)) for kc in range(8)][-1])
                p.wait("scalar", tQ, mt)
                lg_toks.append(p.op("scalar", cS, lambda e, t2=t2, plg=plg: e.copy(
                    R("lg")[0:8, t2 * 512:(t2 + 1) * 512], R(plg)[0:8, :])))
            c_toks = []
            rtm_toks = []
            for sub in range(8 if ROUTER else 0):
                p.wait("tensor", cS, lg_toks[1])
                if sub >= 1:
                    p.wait("tensor", cS, rtm_toks[sub - 1])
                tt = p.op("tensor", tQ, lambda e, sub=sub: e.transpose(
                    R("pA")[0:128, 0:8], R("lg")[0:8, sub * 128:(sub + 1) * 128],
                    R("identf")[0:8, 0:8]))
                p.wait("scalar", tQ, tt)
                if sub >= 1:
                    p.wait("scalar", cV, c_toks[sub - 1])
                rt = p.op("scalar", cS, lambda e: e.copy(R("rtm")[:, :], R("pA")[0:128, 0:8]))
                rtm_toks.append(rt)
                p.wait("vector", cS, rt)
                p.wait("scalar" if False else "vector", cS, rt)
                t = p.op("vector", cV, lambda e: e.reduce_max(
                    R("srow")[:, 0:1], R("rtm")[:, :], mybir.AxisListType.X))
                p.wait("vector", cV, t)
                ct = p.op("vector", cV, lambda e, sub=sub: e.tensor_scalar(
                    R("ctm")[:, sub * 8:(sub + 1) * 8], R("rtm")[:, :],
                    R("srow")[:, 0:1], None, ALU.subtract))
                p.wait("scalar", cV, ct)
                et = p.op("scalar", cS, lambda e, sub=sub: e.activation(
                    R("ctm")[:, sub * 8:(sub + 1) * 8], R("ctm")[:, sub * 8:(sub + 1) * 8], AF.Exp))
                p.wait("vector", cS, et)
                c_toks.append(p.vchain("vector", cV, [
                    lambda e, sub=sub: e.reduce_max(R("srow")[:, 3:4],
                                                    R("ctm")[:, sub * 8:(sub + 1) * 8],
                                                    mybir.AxisListType.X),
                    lambda e, sub=sub: e.tensor_scalar(R("rtm")[:, :],
                                                       R("ctm")[:, sub * 8:(sub + 1) * 8],
                                                       R("srow")[:, 3:4], None, ALU.is_lt),
                    lambda e, sub=sub: e.tensor_mul(R("rtm")[:, :], R("rtm")[:, :],
                                                    R("ctm")[:, sub * 8:(sub + 1) * 8]),
                    lambda e: e.reduce_max(R("srow")[:, 4:5], R("rtm")[:, :],
                                           mybir.AxisListType.X),
                    lambda e: e.tensor_add(R("srow")[:, 5:6], R("srow")[:, 3:4],
                                           R("srow")[:, 4:5]),
                    lambda e: e.reciprocal(R("srow")[:, 6:7], R("srow")[:, 5:6]),
                    lambda e, sub=sub: e.tensor_scalar(R("rtm")[:, :],
                                                       R("ctm")[:, sub * 8:(sub + 1) * 8],
                                                       R("srow")[:, 4:5], None, ALU.is_ge),
                    lambda e, sub=sub: e.tensor_mul(R("rtm")[:, :], R("rtm")[:, :],
                                                    R("ctm")[:, sub * 8:(sub + 1) * 8]),
                    lambda e, sub=sub: e.tensor_scalar(R("ctm")[:, sub * 8:(sub + 1) * 8],
                                                       R("rtm")[:, :],
                                                       R("srow")[:, 6:7], None, ALU.mult),
                ]))
            cf_toks = []
            for sub in range(8 if ROUTER else 0):
                p.wait("tensor", cV, c_toks[sub])
                if sub >= 1:
                    p.wait("tensor", cS, cf_toks[sub - 1])
                tt = p.op("tensor", tQ, lambda e, sub=sub: e.transpose(
                    R("pB")[0:8, 0:128], R("ctm")[:, sub * 8:(sub + 1) * 8], R("identf")[:, :]))
                p.wait("scalar", tQ, tt)
                cf_toks.append(p.op("scalar", cS, lambda e, sub=sub: e.copy(
                    R("cfm")[0:8, sub * 128:(sub + 1) * 128], R("pB")[0:8, 0:128])))
            if ROUTER:
                p.wait("sync", cS, cf_toks[-1])
                cdr_tok = p.op("sync", oX, lambda e: e.dma_start(io.cdr0[:, :], R("cfm")[:, :]))
            else:
                cdr_tok = oX.tok()

            if EXPERTS:
                with p.loop(TR_E) as e_lv:
                    cv_start = cV.tok()
                    cs_start = cS.tok()
                    sw_start = sW.tok()
                    p.wait("gpsimd", oX, cdr_tok)
                    p.wait("gpsimd", cV, cv_start)
                    ceb_tok = p.op("gpsimd", gX, lambda e: e.dma_start(
                        R("ceb")[:, :],
                        bass.AP(io.cdr0, p.off("gpsimd", 0, [(e_lv, 1024)]), [[0, 128], [1, 1024]])))
                    p.wait("vector", gX, ceb_tok)
                    xs_toks = []
                    for kc in range(8):
                        for t2 in range(2):
                            xs_toks.append(p.op("vector", cV, lambda e, kc=kc, t2=t2: e.tensor_mul(
                                R("xs")[:, kc * 1024 + t2 * 512: kc * 1024 + t2 * 512 + 512],
                                R("xbf")[:, kc * 1024 + t2 * 512: kc * 1024 + t2 * 512 + 512],
                                R("ceb")[:, t2 * 512:(t2 + 1) * 512])))
                    if C_STAGE >= 3:
                        with p.loop(TR_NC) as nc_lv:
                            sw_it = sW.tok()
                            cs_it = cS.tok()
                            cv_it = cV.tok()
                            for par in range(2):
                                wb1 = "wb1_" + str(par)
                                wb2 = "wb2_" + str(par)
                                p.wait("tensor", sW, sw_it)
                                p.wait("tensor", cV, xs_toks[-1])
                                relu_toks, add_toks = [], []
                                mm1_tok = mm2_tok = None
                                for t2 in range(2):
                                    p.wait("tensor", cS, cs_it if t2 == 0 else relu_toks[-1])
                                    mm1_tok = p.op("tensor", tQ, lambda e, wb1=wb1, t2=t2: [
                                        e.matmul(R(("pC", "pD", "pE", "pF")[oc])[:, :],
                                                 R(wb1)[:, kc * 512 + oc * 128: kc * 512 + (oc + 1) * 128],
                                                 R("xs")[:, kc * 1024 + t2 * 512: kc * 1024 + t2 * 512 + 512],
                                                 start=(kc == 0), stop=(kc == 7))
                                        for oc in range(4) for kc in range(8)][-1])
                                    p.wait("scalar", tQ, mm1_tok)
                                    rt = p.op("scalar", cS, lambda e: [
                                        e.activation(R("hid")[:, oc * 512:(oc + 1) * 512],
                                                     R(("pC", "pD", "pE", "pF")[oc])[:, :], AF.Relu)
                                        for oc in range(4)][-1])
                                    relu_toks.append(rt)
                                    p.wait("tensor", cS, rt)
                                    for jc in range(8):
                                        pw = ("pG", "pH")[jc % 2]
                                        a_idx = t2 * 8 + jc
                                        if a_idx >= 2:
                                            p.wait("tensor", cV, add_toks[a_idx - 2])
                                        else:
                                            p.wait("tensor", cV, cv_it)
                                        mm2_tok = p.op("tensor", tQ, lambda e, wb2=wb2, jc=jc, pw=pw: [
                                            e.matmul(R(pw)[:, :],
                                                     R(wb2)[:, kc2 * 1024 + jc * 128: kc2 * 1024 + (jc + 1) * 128],
                                                     R("hid")[:, kc2 * 512:(kc2 + 1) * 512],
                                                     start=(kc2 == 0), stop=(kc2 == 3)) for kc2 in range(4)][-1])
                                        p.wait("vector", tQ, mm2_tok)
                                        at = p.op("vector", cV, lambda e, jc=jc, t2=t2, pw=pw: e.tensor_add(
                                            R("outacc")[:, jc * 1024 + t2 * 512: jc * 1024 + t2 * 512 + 512],
                                            R("outacc")[:, jc * 1024 + t2 * 512: jc * 1024 + t2 * 512 + 512],
                                            R(pw)[:, :]))
                                        add_toks = add_toks + [at] if False else add_toks
                                        add_toks.append(at)
                                p.wait("sync", tQ, mm1_tok)
                                p.wait("sync", tQ, mm2_tok)
                                p.wait("sync", gP0, pc0)
                                p.op("sync", sW, lambda e, par=par, wb1=wb1: e.dma_start(
                                    bass.AP(R(wb1), 0, [[4096, 128], [512, 8], [1, 512]]),
                                    bass.AP(io.wbf0, p.off("sync", (par + 2) * CSZ,
                                                           [(e_lv, 8 * CSZ), (nc_lv, 2 * CSZ)]),
                                            [[512, 128], [128 * 512, 8], [1, 512]])))
                                p.op("sync", sW, lambda e, par=par, wb2=wb2: e.dma_start(
                                    bass.AP(R(wb2), 0, [[4096, 128], [1024, 4], [1, 1024]]),
                                    bass.AP(io.wbf0, p.off("sync", (par + 2) * CSZ + 1024 * 512,
                                                           [(e_lv, 8 * CSZ), (nc_lv, 2 * CSZ)]),
                                            [[1024, 128], [128 * 1024, 4], [1, 1024]])))

            for t2 in range(2):
                p.wait("vector", cV, cV.tok())
                xt_tok = p.op("scalar", vX, lambda e, t2=t2: e.dma_start(
                    bass.AP(R("xt"), 0, [[4096, 128], [512, 8], [1, 512]]),
                    bass.AP(io.x1, p.off("scalar", t2 * 512, [(s_lv, 1024 * 4096), (tg_lv, 1024)]),
                            [[4096, 128], [128 * 4096, 8], [1, 512]])))
                p.wait("vector", vX, xt_tok)
                y0_toks = []
                for jc in range(8):
                    y0_toks.append(p.op("vector", cV, lambda e, jc=jc, t2=t2: e.tensor_add(
                        R("y0")[:, jc * 512:(jc + 1) * 512],
                        R("outacc")[:, jc * 1024 + t2 * 512: jc * 1024 + t2 * 512 + 512],
                        R("xt")[:, jc * 512:(jc + 1) * 512])))

                def store_fn(e, t2=t2):
                    return e.dma_start(
                        bass.AP(io.x2, p.off("scalar", t2 * 512, [(s_lv, 1024 * 4096), (tg_lv, 1024)]),
                                [[4096, 128], [128 * 4096, 8], [1, 512]]),
                        bass.AP(R("xout"), 0, [[4096, 128], [512, 8], [1, 512]]))

                emit_ln_store(p, sems, y0_toks, store_fn, oX)


def emit_rglru(p, S):
    gX, sX, oX, vX = S.gX, S.sX, S.oX, S.vX
    tQ, cS, cV = S.tQ, S.cS, S.cV
    sems = dict(tQ=tQ, cS=cS, cV=cV)

    R = lambda n: p.r[n] if not p.dry else None
    io = p.io

    p.op("gpsimd", gX, lambda e: e.dma_start(
        bass.AP(R("igwb"), 0, [[1024, 128], [128, 8], [1, 128]]),
        bass.AP(io.igw, 0, [[128, 128], [128 * 128, 8], [1, 128]])))
    p.op("gpsimd", gX, lambda e: e.dma_start(
        bass.AP(R("agwb"), 0, [[1024, 128], [128, 8], [1, 128]]),
        bass.AP(io.agw, 0, [[128, 128], [128 * 128, 8], [1, 128]])))
    p.op("sync", sX, lambda e: e.dma_start(R("igb")[:, :], io.igb[:, :]))
    p.op("sync", sX, lambda e: e.dma_start(R("agb")[:, :], io.agb[:, :]))
    apt = p.op("sync", sX, lambda e: e.dma_start(R("apv")[:, :], io.ap[:, :]))
    ones_tok = p.op("vector", cV, lambda e: [
        e.memset(R("ones128")[:, :], 1.0),
        e.memset(R("onesbf")[:, :], 1.0),
        e.memset(R("onesf")[:, :], 1.0)][-1])
    p.wait("scalar", sX, apt)
    t = p.op("scalar", cS, lambda e: e.activation(R("sp8")[:, 0:8], R("apv")[:, :], AF.Abs))
    p.wait("scalar", cS, t)
    t = p.op("scalar", cS, lambda e: e.activation(R("sp8")[:, 8:16], R("sp8")[:, 0:8], AF.Exp, scale=-1.0))
    p.wait("vector", cS, t)
    t = p.vchain("vector", cV, [
        lambda e: e.tensor_scalar(R("sp8")[:, 16:24], R("sp8")[:, 8:16], 1.0, None, ALU.add),
    ])
    p.wait("scalar", cV, t)
    t = p.op("scalar", cS, lambda e: e.activation(R("sp8")[:, 24:32], R("sp8")[:, 16:24], AF.Ln))
    p.wait("vector", cS, t)
    p.vchain("vector", cV, [
        lambda e: e.tensor_scalar(R("cvec")[:, :], R("apv")[:, :], 0.0, None, ALU.max),
        lambda e: e.tensor_add(R("cvec")[:, :], R("cvec")[:, :], R("sp8")[:, 24:32]),
        lambda e: e.tensor_scalar(R("cvec")[:, :], R("cvec")[:, :], -8.0, None, ALU.mult),
    ])

    with p.loop(SPC) as s_lv:
        p.barrier()
        p.op("vector", cV, lambda e: [
            e.memset(R("lc")[:, :], 0.0),
            e.memset(R("sc")[:, :], 0.0)][-1])
        with p.loop(NTILE) as ti:
            p.barrier()
            p.wait("vector", oX, oX.tok())
            p.op("gpsimd", gX, lambda e: e.dma_start(
                bass.AP(R("xbt"), 0, [[4096, 128], [512, 8], [1, TT]]),
                bass.AP(io.x2, p.off("gpsimd", 0, [(s_lv, 1024 * 4096), (ti, TT)]),
                        [[4096, 128], [128 * 4096, 8], [1, TT]])))
            xbt_tok = gX.tok()
            p.op("gpsimd", S.gP1, lambda e: e.dma_start(
                bass.AP(io.wbf1, p.off("gpsimd", 131072 * 512,
                                       [(s_lv, 8 * 4096 * 512), (ti, 4096 * 512)]),
                        [[512, 4096], [1, 512]]),
                bass.AP(io.w1w2_1, p.off("gpsimd", 131072 * 512,
                                         [(s_lv, 8 * 4096 * 512), (ti, 4096 * 512)]),
                        [[512, 4096], [1, 512]])))
            p.op("scalar", vX, lambda e: e.dma_start(
                bass.AP(R("xt"), 0, [[4096, 128], [512, 8], [1, TT]]),
                bass.AP(io.x2, p.off("scalar", 0, [(s_lv, 1024 * 4096), (ti, TT)]),
                        [[4096, 128], [128 * 4096, 8], [1, TT]])))
            xt_tok = vX.tok()
            p.wait("tensor", gX, xbt_tok)
            p.wait("tensor", cV, ones_tok)

            y0_toks = []
            sig_toks = []
            for h in range(8):
                pig = ("pA", "pB")[h % 2]
                pag = ("pC", "pD")[h % 2]
                if h >= 2:
                    p.wait("tensor", cS, sig_toks[h - 2])
                mt = p.op("tensor", tQ, lambda e, h=h, pig=pig, pag=pag: [
                    e.matmul(R(pig)[:, :], R("igwb")[:, h * 128:(h + 1) * 128],
                             R("xbt")[:, h * 512:(h + 1) * 512], start=True, stop=True),
                    e.matmul(R(pag)[:, :], R("agwb")[:, h * 128:(h + 1) * 128],
                             R("xbt")[:, h * 512:(h + 1) * 512], start=True, stop=True)][-1])
                p.wait("scalar", tQ, mt)
                if h >= 1:
                    p.wait("scalar", cV, y0_toks[h - 1])
                st = p.op("scalar", cS, lambda e, h=h, pig=pig, pag=pag: [
                    e.activation(R("gx")[:, :], R(pig)[:, :], AF.Sigmoid,
                                 bias=R("igb")[:, h: h + 1]),
                    e.activation(R("ga")[:, :], R(pag)[:, :], AF.Sigmoid,
                                 bias=R("agb")[:, h: h + 1])][-1])
                sig_toks.append(st)
                p.wait("vector", cS, st)
                t = p.op("vector", cV, lambda e, h=h: e.tensor_scalar(
                    R("la")[:, :], R("ga")[:, :], R("cvec")[:, h: h + 1], None, ALU.mult))
                p.wait("vector", cV, t)
                t = p.op("vector", cV, lambda e, h=h: e.tensor_tensor_scan(
                    R("li")[:, :], R("onesf")[:, :], R("la")[:, :],
                    R("lc")[:, h: h + 1], ALU.mult, ALU.add))
                p.wait("vector", cV, t)
                lvt = p.op("vector", cV, lambda e, h=h: [
                    e.tensor_copy(R("lc")[:, h: h + 1], R("li")[:, 511:512]),
                    e.tensor_sub(R("lx")[:, :], R("li")[:, :], R("la")[:, :]),
                    e.tensor_scalar(R("lx")[:, :], R("lx")[:, :], LN7, None, ALU.max)][-1])
                p.wait("scalar", cV, lvt)
                et = p.op("scalar", cS, lambda e: [
                    e.activation(R("pp")[:, :], R("lx")[:, :], AF.Exp),
                    e.activation(R("ppi")[:, :], R("lx")[:, :], AF.Exp, scale=-1.0),
                    e.activation(R("a2")[:, :], R("la")[:, :], AF.Exp, scale=2.0)][-1])
                p.wait("vector", cS, et)
                t = p.op("vector", cV, lambda e: [
                    e.tensor_scalar(R("ml")[:, :], R("a2")[:, :], -1.0, 1.0, ALU.mult, ALU.add),
                    e.tensor_scalar(R("ml")[:, :], R("ml")[:, :], 1e-7, None, ALU.max)][-1])
                p.wait("scalar", cV, t)
                sqt = p.op("scalar", cS, lambda e: e.activation(R("ml")[:, :], R("ml")[:, :], AF.Sqrt))
                p.wait("vector", cS, sqt)
                if h == 0:
                    p.wait("vector", vX, xt_tok)
                t = p.op("vector", cV, lambda e, h=h: [
                    e.tensor_mul(R("nx")[:, :], R("xt")[:, h * 512:(h + 1) * 512], R("gx")[:, :]),
                    e.tensor_mul(R("nx")[:, :], R("nx")[:, :], R("ml")[:, :]),
                    e.tensor_mul(R("nx")[:, :], R("nx")[:, :], R("ppi")[:, :])][-1])
                p.wait("vector", cV, t)
                t = p.op("vector", cV, lambda e, h=h: e.tensor_tensor_scan(
                    R("ss")[:, :], R("onesf")[:, :], R("nx")[:, :],
                    R("sc")[:, h: h + 1], ALU.mult, ALU.add))
                p.wait("vector", cV, t)
                y0_toks.append(p.op("vector", cV, lambda e, h=h: [
                    e.tensor_copy(R("sc")[:, h: h + 1], R("ss")[:, 511:512]),
                    e.tensor_mul(R("ss")[:, :], R("pp")[:, :], R("ss")[:, :]),
                    e.tensor_add(R("y0")[:, h * 512:(h + 1) * 512],
                                 R("xt")[:, h * 512:(h + 1) * 512], R("ss")[:, :])][-1]))

            def store_fn(e, s_lv=s_lv, ti=ti):
                return e.dma_start(
                    bass.AP(io.x3, p.off("scalar", 0, [(s_lv, 1024 * 4096), (ti, TT)]),
                            [[4096, 128], [128 * 4096, 8], [1, TT]]),
                    bass.AP(R("xout"), 0, [[4096, 128], [512, 8], [1, TT]]))

            emit_ln_store(p, sems, y0_toks, store_fn, oX)


def emit_l1ffn(p, S):
    gX, gC, sW, sX, oX, vX = S.gX, S.gC, S.sW, S.sX, S.oX, S.vX
    gP1 = S.gP1
    tQ, cS, cV = S.tQ, S.cS, S.cV
    sems = dict(tQ=tQ, cS=cS, cV=cV)
    CSZ = 3072 * 512
    TR_S, TR_TG, TR_E, TR_NC = C_TRIPS

    R = lambda n: p.r[n] if not p.dry else None
    io = p.io

    pc1 = gP1.tok()

    p.op("gpsimd", gC, lambda e: e.iota(R("xt")[:, 0:128], [[1, 128]], channel_multiplier=0,
                                        allow_small_or_imprecise_dtypes=True))
    it = p.op("gpsimd", gC, lambda e: e.iota(R("y0")[:, 0:128], [[0, 128]], channel_multiplier=1,
                                             allow_small_or_imprecise_dtypes=True))
    p.wait("vector", gC, it)
    ones_tok = p.op("vector", cV, lambda e: [
        e.tensor_tensor(R("identf")[:, :], R("xt")[:, 0:128], R("y0")[:, 0:128], ALU.is_equal),
        e.memset(R("ones128")[:, :], 1.0),
        e.memset(R("onesbf")[:, :], 1.0)][-1])
    p.op("gpsimd", gX, lambda e: e.dma_start(R("wrb")[:, :], io.wr1[:, :]))
    p.op("gpsimd", gX, lambda e: e.dma_start(
        R("rbb")[:, :], bass.AP(io.rb, 0, [[0, 128], [1, 8]])))

    with p.loop(TR_S) as s_lv:
        with p.loop(TR_TG) as tg_lv:
            p.barrier()
            p.wait("vector", oX, oX.tok())
            p.wait("scalar", oX, oX.tok())
            p.wait("sync", gP1, pc1)
            p.wait("sync", sW, sW.tok())
            for par in range(2):
                for ci, wdst in ((0, "wa_"), (1, "wg_"), (2, "wb2_")):
                    pat = [[4096, 128], [512, 8], [1, 512]] if ci < 2 else [[4096, 128], [1024, 4], [1, 1024]]
                    spat = [[512, 128], [128 * 512, 8], [1, 512]] if ci < 2 else [[1024, 128], [128 * 1024, 4], [1, 1024]]
                    p.op("sync", sW, lambda e, par=par, ci=ci, wdst=wdst, pat=pat, spat=spat: e.dma_start(
                        bass.AP(R(wdst + str(par)), 0, pat),
                        bass.AP(io.wbf1, par * CSZ + ci * 1024 * 512, spat)))
            p.op("gpsimd", gX, lambda e: e.dma_start(
                bass.AP(R("xbf"), 0, [[8192, 128], [1024, 8], [1, 1024]]),
                bass.AP(io.x3, p.off("gpsimd", 0, [(s_lv, 1024 * 4096), (tg_lv, 1024)]),
                        [[4096, 128], [128 * 4096, 8], [1, 1024]])))
            xbf_tok = gX.tok()
            p.op("vector", cV, lambda e: e.memset(R("outacc")[:, :], 0.0))

            p.wait("tensor", gX, xbf_tok)
            p.wait("tensor", cV, ones_tok)
            lg_toks = []
            for t2 in range(2):
                plg = ("pA", "pB")[t2]
                mt = p.op("tensor", tQ, lambda e, t2=t2, plg=plg: [
                    e.matmul(R(plg)[0:8, :], R("wrb")[:, kc * 8:(kc + 1) * 8],
                             R("xbf")[:, kc * 1024 + t2 * 512: kc * 1024 + t2 * 512 + 512],
                             start=(kc == 0), stop=(kc == 7)) for kc in range(8)][-1])
                p.wait("scalar", tQ, mt)
                lg_toks.append(p.op("scalar", cS, lambda e, t2=t2, plg=plg: e.copy(
                    R("lg")[0:8, t2 * 512:(t2 + 1) * 512], R(plg)[0:8, :])))
            c_toks = []
            rtm_toks = []
            for sub in range(8):
                p.wait("tensor", cS, lg_toks[1])
                if sub >= 1:
                    p.wait("tensor", cS, rtm_toks[sub - 1])
                tt = p.op("tensor", tQ, lambda e, sub=sub: e.transpose(
                    R("pA")[0:128, 0:8], R("lg")[0:8, sub * 128:(sub + 1) * 128],
                    R("identf")[0:8, 0:8]))
                p.wait("scalar", tQ, tt)
                if sub >= 1:
                    p.wait("scalar", cV, c_toks[sub - 1])
                rt = p.op("scalar", cS, lambda e: e.copy(R("rtm")[:, :], R("pA")[0:128, 0:8]))
                rtm_toks.append(rt)
                p.wait("vector", cS, rt)
                t = p.vchain("vector", cV, [
                    lambda e: e.tensor_add(R("rtm")[:, :], R("rtm")[:, :], R("rbb")[:, :]),
                    lambda e: e.reduce_max(R("srow")[:, 0:1], R("rtm")[:, :], mybir.AxisListType.X),
                    lambda e, sub=sub: e.tensor_scalar(R("ctm")[:, sub * 8:(sub + 1) * 8],
                                                       R("rtm")[:, :], R("srow")[:, 0:1],
                                                       None, ALU.subtract),
                ])
                p.wait("scalar", cV, t)
                et = p.op("scalar", cS, lambda e, sub=sub: e.activation(
                    R("ctm")[:, sub * 8:(sub + 1) * 8], R("ctm")[:, sub * 8:(sub + 1) * 8], AF.Exp))
                p.wait("vector", cS, et)
                c_toks.append(p.vchain("vector", cV, [
                    lambda e, sub=sub: e.reduce_sum(R("srow")[:, 1:2],
                                                    R("ctm")[:, sub * 8:(sub + 1) * 8],
                                                    mybir.AxisListType.X),
                    lambda e: e.reciprocal(R("srow")[:, 2:3], R("srow")[:, 1:2]),
                    lambda e, sub=sub: e.reduce_max(R("srow")[:, 3:4],
                                                    R("ctm")[:, sub * 8:(sub + 1) * 8],
                                                    mybir.AxisListType.X),
                    lambda e: e.tensor_mul(R("srow")[:, 4:5], R("srow")[:, 3:4], R("srow")[:, 2:3]),
                    lambda e, sub=sub: e.tensor_scalar(R("rtm")[:, :],
                                                       R("ctm")[:, sub * 8:(sub + 1) * 8],
                                                       R("srow")[:, 3:4], None, ALU.is_ge),
                    lambda e, sub=sub: e.tensor_scalar(R("ctm")[:, sub * 8:(sub + 1) * 8],
                                                       R("rtm")[:, :],
                                                       R("srow")[:, 4:5], None, ALU.mult),
                ]))
            cf_toks = []
            for sub in range(8):
                p.wait("tensor", cV, c_toks[sub])
                if sub >= 1:
                    p.wait("tensor", cS, cf_toks[sub - 1])
                tt = p.op("tensor", tQ, lambda e, sub=sub: e.transpose(
                    R("pB")[0:8, 0:128], R("ctm")[:, sub * 8:(sub + 1) * 8], R("identf")[:, :]))
                p.wait("scalar", tQ, tt)
                cf_toks.append(p.op("scalar", cS, lambda e, sub=sub: e.copy(
                    R("cfm")[0:8, sub * 128:(sub + 1) * 128], R("pB")[0:8, 0:128])))
            p.wait("sync", cS, cf_toks[-1])
            cdr_tok = p.op("sync", oX, lambda e: e.dma_start(io.cdr1[:, :], R("cfm")[:, :]))

            with p.loop(TR_E) as e_lv:
                cv_start = cV.tok()
                cs_start = cS.tok()
                p.wait("gpsimd", oX, cdr_tok)
                p.wait("gpsimd", cV, cv_start)
                ceb_tok = p.op("gpsimd", gX, lambda e: e.dma_start(
                    R("ceb")[:, :],
                    bass.AP(io.cdr1, p.off("gpsimd", 0, [(e_lv, 1024)]), [[0, 128], [1, 1024]])))
                with p.loop(TR_NC) as nc_lv:
                    sw_it = sW.tok()
                    cs_it = cS.tok()
                    cv_it = cV.tok()
                    for par in range(2):
                        wa, wg, wb2 = f"wa_{par}", f"wg_{par}", f"wb2_{par}"
                        p.wait("tensor", sW, sw_it)
                        p.wait("tensor", gX, ceb_tok) if False else None
                        add_toks = []
                        mm_last = None
                        for t2 in range(2):
                            p.wait("tensor", cS, cs_it if t2 == 0 else sil_tok)
                            if t2 == 0:
                                p.wait("tensor", cV, cv_it)
                            else:
                                p.wait("tensor", cV, hid_tok)
                            mm_a = p.op("tensor", tQ, lambda e, wa=wa, t2=t2: [
                                e.matmul(R(("pC", "pD", "pE", "pF")[oc])[:, :],
                                         R(wa)[:, kc * 512 + oc * 128: kc * 512 + (oc + 1) * 128],
                                         R("xbf")[:, kc * 1024 + t2 * 512: kc * 1024 + t2 * 512 + 512],
                                         start=(kc == 0), stop=(kc == 7))
                                for oc in range(4) for kc in range(8)][-1])
                            mm_g01 = p.op("tensor", tQ, lambda e, wg=wg, t2=t2: [
                                e.matmul(R(("pA", "pB")[oc])[:, :],
                                         R(wg)[:, kc * 512 + oc * 128: kc * 512 + (oc + 1) * 128],
                                         R("xbf")[:, kc * 1024 + t2 * 512: kc * 1024 + t2 * 512 + 512],
                                         start=(kc == 0), stop=(kc == 7))
                                for oc in range(2) for kc in range(8)][-1])
                            p.wait("scalar", tQ, mm_a)
                            sil_tok = p.op("scalar", cS, lambda e: [
                                e.activation(R("sa")[:, oc * 512:(oc + 1) * 512],
                                             R(("pC", "pD", "pE", "pF")[oc])[:, :], AF.Silu)
                                for oc in range(4)][-1])
                            p.wait("vector", tQ, mm_g01)
                            p.wait("vector", gX, ceb_tok)
                            gs01 = p.op("vector", cV, lambda e: [
                                e.tensor_scalar(R("gs")[:, oc * 512:(oc + 1) * 512],
                                                R(("pA", "pB")[oc])[:, :],
                                                1.0, None, ALU.mult)
                                for oc in range(2)][-1])
                            p.wait("tensor", cS, sil_tok)
                            mm_g23 = p.op("tensor", tQ, lambda e, wg=wg, t2=t2: [
                                e.matmul(R(("pC", "pD")[oc - 2])[:, :],
                                         R(wg)[:, kc * 512 + oc * 128: kc * 512 + (oc + 1) * 128],
                                         R("xbf")[:, kc * 1024 + t2 * 512: kc * 1024 + t2 * 512 + 512],
                                         start=(kc == 0), stop=(kc == 7))
                                for oc in (2, 3) for kc in range(8)][-1])
                            p.wait("vector", tQ, mm_g23)
                            gs_tok = p.op("vector", cV, lambda e: [
                                e.tensor_scalar(R("gs")[:, oc * 512:(oc + 1) * 512],
                                                R(("pC", "pD")[oc - 2])[:, :],
                                                1.0, None, ALU.mult)
                                for oc in (2, 3)][-1])
                            p.wait("vector", cS, sil_tok)
                            hid_tok = p.op("vector", cV, lambda e, t2=t2: [
                                [e.tensor_mul(R("gs")[:, oc * 512:(oc + 1) * 512],
                                              R("gs")[:, oc * 512:(oc + 1) * 512],
                                              R("ceb")[:, t2 * 512:(t2 + 1) * 512]),
                                 e.tensor_mul(R("hid")[:, oc * 512:(oc + 1) * 512],
                                              R("gs")[:, oc * 512:(oc + 1) * 512],
                                              R("sa")[:, oc * 512:(oc + 1) * 512])][-1]
                                for oc in range(4)][-1])
                            p.wait("tensor", cV, hid_tok)
                            for jc in range(8):
                                pw = ("pG", "pH")[jc % 2]
                                a_idx = t2 * 8 + jc
                                if a_idx >= 2:
                                    p.wait("tensor", cV, add_toks[a_idx - 2])
                                else:
                                    p.wait("tensor", cV, cv_it)
                                mm_last = p.op("tensor", tQ, lambda e, wb2=wb2, jc=jc, pw=pw: [
                                    e.matmul(R(pw)[:, :],
                                             R(wb2)[:, kc2 * 1024 + jc * 128: kc2 * 1024 + (jc + 1) * 128],
                                             R("hid")[:, kc2 * 512:(kc2 + 1) * 512],
                                             start=(kc2 == 0), stop=(kc2 == 3)) for kc2 in range(4)][-1])
                                p.wait("vector", tQ, mm_last)
                                add_toks.append(p.op("vector", cV, lambda e, jc=jc, t2=t2, pw=pw: e.tensor_add(
                                    R("outacc")[:, jc * 1024 + t2 * 512: jc * 1024 + t2 * 512 + 512],
                                    R("outacc")[:, jc * 1024 + t2 * 512: jc * 1024 + t2 * 512 + 512],
                                    R(pw)[:, :])))
                        p.wait("sync", tQ, mm_last)
                        for ci, wdst in ((0, "wa_"), (1, "wg_"), (2, "wb2_")):
                            pat = [[4096, 128], [512, 8], [1, 512]] if ci < 2 else [[4096, 128], [1024, 4], [1, 1024]]
                            spat = [[512, 128], [128 * 512, 8], [1, 512]] if ci < 2 else [[1024, 128], [128 * 1024, 4], [1, 1024]]
                            p.op("sync", sW, lambda e, par=par, ci=ci, wdst=wdst, pat=pat, spat=spat: e.dma_start(
                                bass.AP(R(wdst + str(par)), 0, pat),
                                bass.AP(io.wbf1, p.off("sync", (par + 2) * CSZ + ci * 1024 * 512,
                                                      [(e_lv, 8 * CSZ), (nc_lv, 2 * CSZ)]),
                                        spat)))

            p.wait("vector", sW, sW.tok())
            for t2 in range(2):
                p.wait("vector", cV, cV.tok())
                xt_tok = p.op("scalar", vX, lambda e, t2=t2: e.dma_start(
                    bass.AP(R("xt"), 0, [[4096, 128], [512, 8], [1, 512]]),
                    bass.AP(io.x3, p.off("scalar", t2 * 512, [(s_lv, 1024 * 4096), (tg_lv, 1024)]),
                            [[4096, 128], [128 * 4096, 8], [1, 512]])))
                p.wait("vector", vX, xt_tok)
                y0_toks = []
                for jc in range(8):
                    y0_toks.append(p.op("vector", cV, lambda e, jc=jc, t2=t2: e.tensor_add(
                        R("y0")[:, jc * 512:(jc + 1) * 512],
                        R("outacc")[:, jc * 1024 + t2 * 512: jc * 1024 + t2 * 512 + 512],
                        R("xt")[:, jc * 512:(jc + 1) * 512])))

                def store_fn(e, t2=t2):
                    return e.dma_start(
                        bass.AP(io.x4, p.off("scalar", t2 * 512, [(s_lv, 1024 * 4096), (tg_lv, 1024)]),
                                [[4096, 128], [128 * 4096, 8], [1, 512]]),
                        bass.AP(R("xout"), 0, [[4096, 128], [512, 8], [1, 512]]))

                emit_ln_store(p, sems, y0_toks, store_fn, oX,
                              yb=("wa_0", 0), y2b=("wg_0", 0))


S1DEFS = [
    ("wqkvb", [128, 8 * 3072], BF16), ("woutb", [128, 8 * 1024], BF16),
    ("bqkv", [128, 24], F32), ("bout", [128, 8], F32), ("omega", [128, 256], BF16),
    ("ones128", [1, 128], F32), ("onesbf", [128, 1], BF16), ("vb", [128, 1024], F32),
    ("bqr", [1, 1024], F32), ("xbt", [128, 4096], BF16), ("xt", [128, 4096], F32),
    ("stk", [128, 16 * 512], BF16), ("vsb", [128, 4 * 16 * 65], BF16),
    ("kvacc", [128, 16 * 130], F32), ("kvbf", [128, 16 * 130], BF16),
    ("kpsb", [128, 2 * 512], BF16), ("rbsb", [64, 512], F32), ("hosb", [128, 8 * 512], BF16),
    ("wo0", [128, 512], F32), ("wo1", [128, 512], F32), ("y0", [128, 4096], F32),
    ("xout", [128, 4096], F32), ("lnrow", [1, 6 * 512], F32), ("mbr", [128, 512], F32),
    ("rbr", [128, 512], F32),
]
S2DEFS = [
    ("identf", [128, 128], F32), ("it1", [128, 128], F32), ("it2", [128, 128], F32),
    ("ones128", [1, 128], F32), ("onesbf", [128, 1], BF16), ("wrb", [128, 64], BF16),
    ("xbf", [128, 8 * 1024], BF16), ("outacc", [128, 8 * 1024], F32),
    ("xs", [128, 8 * 1024], BF16),
    ("wb1_0", [128, 4096], BF16), ("wb1_1", [128, 4096], BF16),
    ("wb2_0", [128, 4096], BF16), ("wb2_1", [128, 4096], BF16),
    ("hid", [128, 4 * 512], BF16), ("ceb", [128, 1024], BF16), ("lg", [8, 1024], F32),
    ("rtm", [128, 8], F32), ("srow", [128, 8], F32), ("ctm", [128, 64], F32),
    ("cfm", [8, 1024], F32), ("xt", [128, 4096], F32), ("y0", [128, 4096], F32),
    ("xout", [128, 4096], F32), ("stk", [128, 8192], BF16), ("lnrow", [1, 6 * 512], F32),
    ("mbr", [128, 512], F32), ("rbr", [128, 512], F32),
]
S3DEFS = [
    ("igwb", [128, 8 * 128], BF16), ("agwb", [128, 8 * 128], BF16),
    ("igb", [128, 8], F32), ("agb", [128, 8], F32), ("apv", [128, 8], F32),
    ("cvec", [128, 8], F32), ("sp8", [128, 4 * 8], F32),
    ("ones128", [1, 128], F32), ("onesbf", [128, 1], BF16), ("onesf", [128, 512], F32),
    ("xt", [128, 4096], F32), ("xbt", [128, 4096], BF16),
    ("gx", [128, 512], F32), ("ga", [128, 512], F32), ("la", [128, 512], F32),
    ("li", [128, 512], F32), ("lx", [128, 512], F32), ("pp", [128, 512], F32),
    ("ppi", [128, 512], F32), ("a2", [128, 512], F32), ("ml", [128, 512], F32),
    ("nx", [128, 512], F32), ("ss", [128, 512], F32),
    ("lc", [128, 8], F32), ("sc", [128, 8], F32), ("y0", [128, 4096], F32),
    ("xout", [128, 4096], F32), ("stk", [128, 8192], BF16), ("lnrow", [1, 6 * 512], F32),
    ("mbr", [128, 512], F32), ("rbr", [128, 512], F32),
]
S4DEFS = [
    ("identf", [128, 128], F32), ("ones128", [1, 128], F32), ("onesbf", [128, 1], BF16),
    ("wrb", [128, 64], BF16), ("rbb", [128, 8], F32), ("xbf", [128, 8 * 1024], BF16),
    ("outacc", [128, 8 * 1024], F32),
    ("wa_0", [128, 4096], BF16), ("wa_1", [128, 4096], BF16),
    ("wg_0", [128, 4096], BF16), ("wg_1", [128, 4096], BF16),
    ("wb2_0", [128, 4096], BF16), ("wb2_1", [128, 4096], BF16),
    ("sa", [128, 2048], F32), ("gs", [128, 2048], F32), ("hid", [128, 4 * 512], BF16),
    ("ceb", [128, 1024], F32), ("lg", [8, 1024], F32), ("rtm", [128, 8], F32),
    ("srow", [128, 8], F32), ("ctm", [128, 64], F32), ("cfm", [8, 1024], F32),
    ("xt", [128, 4096], F32), ("y0", [128, 4096], F32), ("xout", [128, 4096], F32),
    ("lnrow", [1, 6 * 512], F32), ("mbr", [128, 512], F32), ("rbr", [128, 512], F32),
]


def prog_full(p):
    S = NS()
    for owner, name, step in (("gpsimd", "gP0", 16), ("gpsimd", "gP1", 16),
                              ("gpsimd", "gX", 16), ("gpsimd", "gC", 1),
                              ("sync", "sW", 16), ("sync", "sX", 16),
                              ("scalar", "vX", 16),
                              ("sync", "oX", 16), ("tensor", "tQ", 1),
                              ("tensor", "tV", 1), ("scalar", "cS", 1),
                              ("vector", "cV", 1), ("sync", "bS", 1),
                              ("gpsimd", "bG", 1)):
        S[name] = p.sem(owner, name, step)

    if not p.dry:
        nc, C = p.nc, p.ctx
        io = p.io
        io.x = nc.dram_tensor("x", [SPC * 1024, 4096], F32, kind="ExternalInput")
        io.wqkv = nc.dram_tensor("wqkv", [1024, 3072], F32, kind="ExternalInput")
        io.bqkv = nc.dram_tensor("bqkv_t", [128, 24], F32, kind="ExternalInput")
        io.bqv = nc.dram_tensor("bqv", [1, 1024], F32, kind="ExternalInput")
        io.wout = nc.dram_tensor("wout", [1024, 1024], F32, kind="ExternalInput")
        io.bout = nc.dram_tensor("bout_t", [128, 8], F32, kind="ExternalInput")
        io.omega = nc.dram_tensor("omega", [64, 256], F32, kind="ExternalInput")
        io.wr0 = nc.dram_tensor("wr0", [128, 64], F32, kind="ExternalInput")
        io.w1w2_0 = nc.dram_tensor("w1w2_0", [64 * 2048, 512], F32, kind="ExternalInput")
        io.ap = nc.dram_tensor("ap_t", [128, 8], F32, kind="ExternalInput")
        io.igw = nc.dram_tensor("igw", [1024, 128], F32, kind="ExternalInput")
        io.igb = nc.dram_tensor("igb_t", [128, 8], F32, kind="ExternalInput")
        io.agw = nc.dram_tensor("agw", [1024, 128], F32, kind="ExternalInput")
        io.agb = nc.dram_tensor("agb_t", [128, 8], F32, kind="ExternalInput")
        io.wr1 = nc.dram_tensor("wr1", [128, 64], F32, kind="ExternalInput")
        io.rb = nc.dram_tensor("rb", [1, 8], F32, kind="ExternalInput")
        io.w1w2_1 = nc.dram_tensor("w1w2_1", [64 * 3072, 512], F32, kind="ExternalInput")
        io.x4 = nc.dram_tensor("x4", [SPC * 1024, 4096], F32, kind="ExternalOutput")
        io.x1 = nc.dram_tensor("x1", [SPC * 1024, 4096], F32)
        io.x2 = nc.dram_tensor("x2", [SPC * 1024, 4096], F32)
        io.x3 = nc.dram_tensor("x3", [SPC * 1024, 4096], F32)
        io.wbf0 = nc.dram_tensor("wbf0", [66 * 2048, 512], BF16)
        io.wbf1 = nc.dram_tensor("wbf1", [66 * 3072, 512], BF16)
        io.cdr0 = nc.dram_tensor("cdr0", [8, 1024], F32)
        io.cdr1 = nc.dram_tensor("cdr1", [8, 1024], F32)
        io.bsc = nc.dram_tensor("bsc", [4, 4], F32)
        A = Arena(nc, C.st)
        p.sr = []
        for pref, defs in (("s1_", S1DEFS), ("s2_", S2DEFS),
                           ("s3_", S3DEFS), ("s4_", S4DEFS)):
            A.reset()
            d = NS()
            for nm, shape, dt in defs:
                d[nm] = A.sbuf(pref + nm, shape, dt)
            p.sr.append(d)
        A.allocate()
        for nm in ("pA", "pB", "pC", "pD", "pE", "pF", "pG", "pH"):
            h = C.psum(nm, [128, 512])
            for d in p.sr:
                d[nm] = h

    set_stage(p, 0)
    emit_l0attn(p, S)
    stage_barrier(p, S)
    set_stage(p, 1)
    emit_l0ffn(p, S)
    stage_barrier(p, S)
    set_stage(p, 2)
    emit_rglru(p, S)
    stage_barrier(p, S)
    set_stage(p, 3)
    emit_l1ffn(p, S)
    p.finish()
    p.wait("gpsimd", S.oX, S.oX.tok())


def host_full_inputs(inputs):
    d = {
        "wqkv": np.ascontiguousarray(inputs["L0_attn_Wqkv"]),
        "bqkv_t": np.ascontiguousarray(inputs["L0_attn_bqkv"].reshape(24, 128).T),
        "bqv": np.ascontiguousarray(inputs["L0_attn_bqkv"][2048:].reshape(1, 1024)),
        "wout": np.ascontiguousarray(inputs["L0_attn_Wout"]),
        "bout_t": np.ascontiguousarray(inputs["L0_attn_bout"].reshape(8, 128).T),
        "omega": np.ascontiguousarray(inputs["L0_attn_omega"]),
        "wr0": np.ascontiguousarray(
            inputs["L0_ffn_Wr"].reshape(8, 128, 8).transpose(1, 0, 2).reshape(128, 64)),
        "ap_t": np.ascontiguousarray(inputs["L1_attn_a_param"].reshape(8, 128).T),
        "igw": np.ascontiguousarray(inputs["L1_attn_ig_w"].reshape(1024, 128)),
        "igb_t": np.ascontiguousarray(inputs["L1_attn_ig_b"].reshape(8, 128).T),
        "agw": np.ascontiguousarray(inputs["L1_attn_ag_w"].reshape(1024, 128)),
        "agb_t": np.ascontiguousarray(inputs["L1_attn_ag_b"].reshape(8, 128).T),
        "wr1": np.ascontiguousarray(
            inputs["L1_ffn_router_w"].reshape(8, 128, 8).transpose(1, 0, 2).reshape(128, 64)),
        "rb": np.ascontiguousarray(inputs["L1_ffn_router_b"].reshape(1, 8)),
    }
    W1, W2 = inputs["L0_ffn_W1"], inputs["L0_ffn_W2"]
    chunks = []
    for e in range(8):
        for ncc in range(8):
            chunks.append(np.ascontiguousarray(W1[e][:, ncc * 512:(ncc + 1) * 512]))
            chunks.append(W2[e][ncc * 512:(ncc + 1) * 512, :].reshape(1024, 512))
    d["w1w2_0"] = np.ascontiguousarray(np.concatenate(chunks, axis=0))
    w1, w2 = inputs["L1_ffn_w1"], inputs["L1_ffn_w2"]
    chunks = []
    for e in range(8):
        for ncc in range(8):
            chunks.append(np.ascontiguousarray(w1[e][:, ncc * 512:(ncc + 1) * 512]))
            chunks.append(np.ascontiguousarray(w1[e][:, 4096 + ncc * 512: 4096 + (ncc + 1) * 512]))
            chunks.append(w2[e][ncc * 512:(ncc + 1) * 512, :].reshape(1024, 512))
    d["w1w2_1"] = np.ascontiguousarray(np.concatenate(chunks, axis=0))
    return d


_PROGS = {}


def _get_prog(name, fn):
    if name not in _PROGS:
        _PROGS[name] = build(fn)
    return _PROGS[name]


def kernel(**inputs):
    p = _get_prog("full", prog_full)
    cores = list(range(NCORE))
    shared = host_full_inputs(inputs)
    ims = []
    for c in cores:
        im = dict(shared)
        im["x"] = np.ascontiguousarray(
            inputs["L0_attn_x"][2 * c: 2 * c + 2].transpose(0, 2, 1)).reshape(SPC * 1024, 4096)
        ims.append(im)
    res = run_bass_kernel_spmd(p.nc, ims, core_ids=cores)

    out = np.empty((B, N, D), np.float32)
    for c in cores:
        x4 = np.asarray(res.results[c]["x4"]).reshape(SPC, 1024, 4096)
        out[2 * c: 2 * c + 2] = x4.transpose(0, 2, 1)
    return out


# version 06 (final)
# speedup vs baseline: 1.1331x; 1.0263x over previous
import contextlib
import numpy as np
import bass_rust
import concourse.bass as bass
import concourse.mybir as mybir
from concourse.bass_utils import run_bass_kernel_spmd

F32 = mybir.dt.float32
BF16 = mybir.dt.bfloat16
AF = mybir.ActivationFunctionType
ALU = mybir.AluOpType

B, N, D = 16, 4096, 1024
NCORE = 8
SPC = 2
TT = 512
NTILE = N // TT
NH, MF, DH = 16, 256, 64
LN_EPS = 1e-5
LN7 = -16.11809565095832
SEM_SHIFT = 0
C_TRIPS = (SPC, 4, 8, 4)
C_STAGE = 3
ENGS = ("sync", "tensor", "vector", "scalar", "gpsimd")


class LoopVar:
    __slots__ = ("lvl", "var")

    def __init__(self, lvl, var):
        self.lvl = lvl
        self.var = var


class Tok:
    __slots__ = ("c", "terms")

    def __init__(self, c, terms=()):
        self.c = c
        self.terms = tuple(terms)

    def __add__(self, k):
        return Tok(self.c + k, self.terms)


class Sem:
    def __init__(self, p, name, owner, step=1):
        self.p, self.name, self.owner, self.step = p, name, owner, step
        self.c = SEM_SHIFT
        self.stack = []
        if not p.dry:
            self.h = p.ctx.sem(name)
            if SEM_SHIFT:
                getattr(p.nc, owner).sem_inc(self.h, SEM_SHIFT)

    def tok(self, delta=0):
        return Tok(self.c + delta, [(lvl, pi) for (lvl, _s, pi) in self.stack if pi])


class P:
    MAXDEPTH = 4

    def __init__(self, dry, totals=None):
        self.dry = dry
        self.totals = totals if totals is not None else {}
        self.loop_seq = 0
        self.depth = 0
        self.sems = []
        self.r = NS()
        self.io = NS()
        self.regs = {}
        if not dry:
            self.ctx = Ctx()
            self.nc = self.ctx.nc
            for eng in ENGS:
                E = getattr(self.nc, eng)
                self.regs[eng] = {
                    "ri": [E.alloc_register(f"{eng}_ri{l}") for l in range(self.MAXDEPTH)],
                    "t0": E.alloc_register(f"{eng}_t0"),
                    "t1": E.alloc_register(f"{eng}_t1"),
                }

    def sem(self, owner, name, step=1):
        s = Sem(self, name, owner, step)
        self.sems.append(s)
        return s

    @contextlib.contextmanager
    def loop(self, n):
        idx = self.loop_seq
        self.loop_seq += 1
        lvl = self.depth
        self.depth += 1

        if self.dry:
            snap = [(s, s.c) for s in self.sems]
            for s in self.sems:
                s.stack.append([lvl, s.c, 0])
            yield LoopVar(lvl, 0)
            pis = {}
            for s, c0 in snap:
                pi = s.c - c0
                pis[s.name] = pi
                s.stack.pop()
                s.c = c0 + n * pi
            self.totals[idx] = pis
        else:
            pis = self.totals[idx]
            nc = self.nc
            loop_start = f"ploop{idx}_loop"
            loop_end = f"ploop{idx}_end"
            ris = bass_rust.RegisterHandles(
                tuple(self.regs[eng]["ri"][lvl] for eng in ENGS))
            for eng in ENGS:
                E = getattr(nc, eng)
                E.reg_mov(self.regs[eng]["ri"][lvl], -1)
            nc.br(loop_start)
            with nc.body(loop_start):
                for eng in ENGS:
                    E = getattr(nc, eng)
                    E.reg_add(self.regs[eng]["ri"][lvl],
                              self.regs[eng]["ri"][lvl], 1)
                snap = [(s, s.c) for s in self.sems]
                for s in self.sems:
                    s.stack.append([lvl, s.c, pis.get(s.name, 0)])
                yield LoopVar(lvl, None)
                for s, c0 in snap:
                    got = s.c - c0
                    assert got == pis.get(s.name, 0), \
                        f"loop {idx} sem {s.name}: emitted {got} != dry {pis.get(s.name, 0)}"
                    s.stack.pop()
                    s.c = c0 + n * got
                nc.br_lt(ris, n - 1, on_true=loop_start, on_false=loop_end)
            nc.switch_bb(loop_end)
        self.depth -= 1

    def _regval(self, eng, const, terms):
        E = getattr(self.nc, eng)
        rg = self.regs[eng]
        first = True
        for lvl, coef in terms:
            if coef == 0:
                continue
            dst = rg["t0"] if first else rg["t1"]
            E.reg_mul(dst, rg["ri"][lvl], coef)
            if not first:
                E.reg_add(rg["t0"], rg["t0"], rg["t1"])
            first = False
        assert not first
        E.reg_add(rg["t0"], rg["t0"], const)
        return rg["t0"]

    def wait(self, eng, sem, tok):
        if self.dry:
            return
        terms = [(l, c) for (l, c) in tok.terms if c]
        if not terms:
            assert tok.c >= 0
            getattr(self.nc, eng).wait_ge(sem.h, tok.c)
        else:
            reg = self._regval(eng, tok.c, terms)
            getattr(self.nc, eng).wait_ge(sem.h, reg)

    def off(self, eng, const, terms):
        E = getattr(self.nc, eng)
        rg = self.regs[eng]
        terms = [(lv, c) for (lv, c) in terms if c]
        if not terms:
            return const
        o = E.alloc_register()
        first = True
        for lv, coef in terms:
            if first:
                E.reg_mul(o, rg["ri"][lv.lvl], coef)
            else:
                E.reg_mul(rg["t1"], rg["ri"][lv.lvl], coef)
                E.reg_add(o, o, rg["t1"])
            first = False
        E.reg_add(o, o, const)
        return E.snap(o, donate=True)

    def op(self, eng, sem, fn, k=1):
        if not self.dry:
            ins = fn(getattr(self.nc, eng))
            ins.then_inc(sem.h, sem.step * k)
        sem.c += sem.step * k
        return sem.tok()

    def raw(self, eng, fn):
        if not self.dry:
            fn(getattr(self.nc, eng))

    def vchain(self, eng, sem, fns):
        t = None
        for fn in fns:
            t = self.op(eng, sem, fn)
            self.wait(eng, sem, t)
        return t

    def finish(self):
        for s in self.sems:
            if s.step == 16:
                self.wait(s.owner, s, s.tok())

    def barrier(self, sems=None):
        sems = sems if sems is not None else self.sems
        toks = [(s, s.tok()) for s in sems if s.step == 1]
        for eng in ENGS:
            for s, t in toks:
                if s.owner != eng:
                    self.wait(eng, s, t)
        return {s.name: t for s, t in toks}


class Ctx:
    def __init__(self):
        self.nc = bass.Bass("TRN2", target_bir_lowering=False, detect_race_conditions=False)
        self.st = contextlib.ExitStack()

    def sem(self, name):
        return self.st.enter_context(self.nc.semaphore(name))

    def sbuf(self, name, shape, dtype):
        return self.st.enter_context(self.nc.sbuf_tensor(name, shape, dtype))

    def psum(self, name, shape, dtype=F32):
        return self.st.enter_context(self.nc.psum_tensor(name, shape, dtype))


def build(program_fn, *args):
    p1 = P(dry=True)
    program_fn(p1, *args)
    p2 = P(dry=False, totals=p1.totals)
    program_fn(p2, *args)
    return p2


class NS(dict):
    __getattr__ = dict.__getitem__
    __setattr__ = dict.__setitem__


class Arena:

    def __init__(self, nc, st):
        self.nc, self.st = nc, st
        self.addr = 0
        self.size = 0
        self.tensors = []

    def sbuf(self, name, shape, dtype):
        per = 1
        for s in shape[1:]:
            per *= s
        per *= mybir.dt.size(dtype)
        a = self.addr
        self.addr = (a + per + 31) & ~31
        self.size = max(self.size, self.addr)
        h = self.nc.virtual_sbuf_tensor(name, shape, dtype)
        self.tensors.append((h, a))
        return h

    def reset(self):
        self.addr = 0

    def allocate(self):
        handle = self.st.enter_context(
            self.nc.sbuf_tensor("arena", [128, self.size], mybir.dt.uint8))
        off = self.nc.lookup_mloc(handle).addr
        for h, a in self.tensors:
            self.nc._tensor(h.name, h.shape, h.dtype, type="SB")
            m = self.nc.lookup_mloc(h)
            m.allocated = True
            m.addr = off + a


def set_stage(p, k):
    if not p.dry:
        p.r = p.sr[k]


def stage_barrier(p, S):
    for s in p.sems:
        if s.step == 16 and s.name not in ("gP0", "gP1"):
            p.wait(s.owner, s, s.tok())
    R = lambda n: p.r[n] if not p.dry else None
    io = p.io
    incs = []
    t = p.op("tensor", S.tQ, lambda e: e.matmul(
        R("pA")[0:1, 0:1], R("onesbf")[:, 0:1], R("onesbf")[:, 0:1],
        start=True, stop=True))
    incs.append(("tensor", S.tQ, t))
    t = p.op("vector", S.cV, lambda e: e.memset(R("lnrow")[0:1, 0:1], 0.0))
    incs.append(("vector", S.cV, t))
    t = p.op("scalar", S.cS, lambda e: e.copy(R("lnrow")[0:1, 2:3], R("lnrow")[0:1, 3:4]))
    incs.append(("scalar", S.cS, t))
    t = p.op("sync", S.oX, lambda e: e.dma_start(io.bsc[0:1, :], io.bsc[1:2, :]))
    incs.append(("sync", S.oX, t))
    t = p.op("gpsimd", S.gX, lambda e: e.dma_start(io.bsc[2:3, :], io.bsc[3:4, :]))
    incs.append(("gpsimd", S.gX, t))
    for eng in ENGS:
        for o, sem, tk in incs:
            if o != eng:
                p.wait(eng, sem, tk)


def emit_ln_store(p, sems, y0_toks, store_fn, store_sem, yb=("stk", 0), y2b=("stk", 4096)):
    tQ, cS, cV = sems["tQ"], sems["cS"], sems["cV"]
    R = lambda n: p.r[n] if not p.dry else None
    ybn, ybo = yb
    y2n, y2o = y2b
    cast_toks, sq_toks = [], []
    for jc in range(8):
        ct = p.op("vector", cV, lambda e, jc=jc: e.tensor_copy(
            R(ybn)[:, ybo + jc * 512: ybo + (jc + 1) * 512], R("y0")[:, jc * 512:(jc + 1) * 512]))
        p.wait("scalar", cV, y0_toks[jc])
        st = p.op("scalar", cS, lambda e, jc=jc: e.activation(
            R(y2n)[:, y2o + jc * 512: y2o + (jc + 1) * 512],
            R("y0")[:, jc * 512:(jc + 1) * 512], AF.Square))
        cast_toks.append(ct)
        sq_toks.append(st)
    p.wait("tensor", cV, cast_toks[-1])
    p.wait("tensor", cS, sq_toks[-1])
    p.op("tensor", tQ, lambda e: [
        e.matmul(R("pC")[0:1, :], R("onesbf")[:, :],
                 R(ybn)[:, ybo + kc * 512: ybo + (kc + 1) * 512],
                 start=(kc == 0), stop=(kc == 7)) for kc in range(8)][-1])
    t2 = p.op("tensor", tQ, lambda e: [
        e.matmul(R("pD")[0:1, :], R("onesbf")[:, :],
                 R(y2n)[:, y2o + kc * 512: y2o + (kc + 1) * 512],
                 start=(kc == 0), stop=(kc == 7)) for kc in range(8)][-1])
    p.wait("scalar", tQ, t2)
    p.op("scalar", cS, lambda e: e.mul(R("lnrow")[0:1, 0:512], R("pC")[0:1, :], 1.0 / 1024))
    p.op("scalar", cS, lambda e: e.mul(R("lnrow")[0:1, 512:1024], R("pD")[0:1, :], 1.0 / 1024))
    mt = p.op("scalar", cS, lambda e: e.activation(
        R("lnrow")[0:1, 1024:1536], R("lnrow")[0:1, 0:512], AF.Square))
    p.wait("vector", cS, mt)
    vt = p.op("vector", cV, lambda e: [
        e.tensor_sub(R("lnrow")[0:1, 1536:2048], R("lnrow")[0:1, 512:1024],
                     R("lnrow")[0:1, 1024:1536]),
        e.tensor_scalar(R("lnrow")[0:1, 1536:2048], R("lnrow")[0:1, 1536:2048],
                        LN_EPS, None, ALU.add)][-1])
    p.wait("scalar", cV, vt)
    sdt = p.op("scalar", cS, lambda e: e.activation(
        R("lnrow")[0:1, 2048:2560], R("lnrow")[0:1, 1536:2048], AF.Sqrt))
    p.wait("vector", cS, sdt)
    rst = p.op("vector", cV, lambda e: e.reciprocal(
        R("lnrow")[0:1, 2560:3072], R("lnrow")[0:1, 2048:2560]))
    p.wait("tensor", cV, rst)
    bt = p.op("tensor", tQ, lambda e: [
        e.matmul(R("pE")[:, :], R("ones128")[0:1, :], R("lnrow")[0:1, 0:512], start=True, stop=True),
        e.matmul(R("pF")[:, :], R("ones128")[0:1, :], R("lnrow")[0:1, 2560:3072], start=True, stop=True)][-1])
    p.wait("scalar", tQ, bt)
    cbt = p.op("scalar", cS, lambda e: [
        e.copy(R("mbr")[:, :], R("pE")[:, :]),
        e.copy(R("rbr")[:, :], R("pF")[:, :])][-1])
    p.wait("vector", cS, cbt)
    xt_ = None
    for jc in range(8):
        xt_ = p.op("vector", cV, lambda e, jc=jc: [
            e.tensor_sub(R("xout")[:, jc * 512:(jc + 1) * 512],
                         R("y0")[:, jc * 512:(jc + 1) * 512], R("mbr")[:, :]),
            e.tensor_mul(R("xout")[:, jc * 512:(jc + 1) * 512],
                         R("xout")[:, jc * 512:(jc + 1) * 512], R("rbr")[:, :])][-1])
    p.wait("scalar", cV, xt_)
    p.op("scalar", store_sem, lambda e: store_fn(e))


def emit_l0attn(p, S):
    gX, sX, oX, vX = S.gX, S.sX, S.oX, S.vX
    tQ, tV, cS, cV = S.tQ, S.tV, S.cS, S.cV
    sems = dict(tQ=tQ, cS=cS, cV=cV)

    R = lambda n: p.r[n] if not p.dry else None
    io = p.io

    for kc in range(8):
        p.op("gpsimd", gX, lambda e, kc=kc: e.dma_start(
            R("wqkvb")[:, kc * 3072:(kc + 1) * 3072], io.wqkv[kc * 128:(kc + 1) * 128, :]))
        p.op("gpsimd", gX, lambda e, kc=kc: e.dma_start(
            R("woutb")[:, kc * 1024:(kc + 1) * 1024], io.wout[kc * 128:(kc + 1) * 128, :]))
    p.op("sync", sX, lambda e: e.dma_start(R("bqkv")[:, :], io.bqkv[:, :]))
    p.op("sync", sX, lambda e: e.dma_start(R("bout")[:, :], io.bout[:, :]))
    p.op("sync", sX, lambda e: e.dma_start(R("bqr")[0:1, :], io.bqv[0:1, :]))
    p.raw("gpsimd", lambda e: e.memset(R("omega")[:, :], -0.5))
    p.op("gpsimd", gX, lambda e: e.dma_start(R("omega")[0:64, :], io.omega[:, :]))
    p.raw("gpsimd", lambda e: e.memset(R("vsb")[:, :], 1.0))
    ones_tok = p.op("vector", cV, lambda e: [
        e.memset(R("ones128")[:, :], 1.0),
        e.memset(R("onesbf")[:, :], 1.0)][-1])
    p.wait("tensor", gX, gX.tok())
    p.wait("tensor", sX, sX.tok())
    p.wait("tensor", cV, ones_tok)
    bt = p.op("tensor", tQ, lambda e: [
        e.matmul(R("pA")[:, :], R("ones128")[0:1, :], R("bqr")[0:1, 0:512], start=True, stop=True),
        e.matmul(R("pB")[:, :], R("ones128")[0:1, :], R("bqr")[0:1, 512:1024], start=True, stop=True)][-1])
    p.wait("scalar", tQ, bt)
    p.op("scalar", cS, lambda e: [
        e.copy(R("vb")[:, 0:512], R("pA")[:, :]),
        e.copy(R("vb")[:, 512:1024], R("pB")[:, :])][-1])

    with p.loop(SPC) as s_lv:
        p.barrier()
        p.op("vector", cV, lambda e: e.memset(R("kvacc")[:, :], 0.0))

        with p.loop(NTILE) as ti:
            p.barrier()
            p.op("gpsimd", gX, lambda e: e.dma_start(
                bass.AP(R("xbt"), 0, [[4096, 128], [512, 8], [1, TT]]),
                bass.AP(io.x, p.off("gpsimd", 0, [(s_lv, 1024 * 4096), (ti, TT)]),
                        [[4096, 128], [128 * 4096, 8], [1, TT]])))
            p.op("gpsimd", S.gP0, lambda e: e.dma_start(
                bass.AP(io.wbf0, p.off("gpsimd", 0,
                                       [(s_lv, 8 * 8192 * 512), (ti, 8192 * 512)]),
                        [[512, 8192], [1, 512]]),
                bass.AP(io.w1w2_0, p.off("gpsimd", 0,
                                         [(s_lv, 8 * 8192 * 512), (ti, 8192 * 512)]),
                        [[512, 8192], [1, 512]])))
            p.wait("tensor", gX, gX.tok())

            ev = []
            for j in range(8):
                ps = "pA" if j % 2 == 0 else "pB"
                if j >= 2:
                    p.wait("tensor", cS, ev[j - 2])
                mt = p.op("tensor", tQ, lambda e, j=j, ps=ps: [
                    e.matmul(R(ps)[:, :],
                             R("wqkvb")[:, kc * 3072 + 1024 + j * 128: kc * 3072 + 1024 + (j + 1) * 128],
                             R("xbt")[:, kc * 512:(kc + 1) * 512],
                             start=(kc == 0), stop=(kc == 7)) for kc in range(8)][-1])
                p.wait("scalar", tQ, mt)
                h0, h1 = 2 * j, 2 * j + 1
                et = p.op("scalar", cS, lambda e, j=j, ps=ps, h0=h0, h1=h1: [
                    e.activation(R("stk")[0:64, h0 * 512:(h0 + 1) * 512], R(ps)[0:64, :],
                                 AF.Identity, bias=R("bqkv")[0:64, 8 + j: 9 + j]),
                    e.activation(R("stk")[64:128, h0 * 512:(h0 + 1) * 512], R(ps)[0:64, :],
                                 AF.Square, bias=R("bqkv")[0:64, 8 + j: 9 + j]),
                    e.activation(R("stk")[0:64, h1 * 512:(h1 + 1) * 512], R(ps)[64:128, :],
                                 AF.Identity, bias=R("bqkv")[64:128, 8 + j: 9 + j]),
                    e.activation(R("stk")[64:128, h1 * 512:(h1 + 1) * 512], R(ps)[64:128, :],
                                 AF.Square, bias=R("bqkv")[64:128, 8 + j: 9 + j])][-1])
                ev.append(et)

            vev = []
            for idx in range(8):
                sc, grp = idx // 2, idx % 2
                ps = "pC" if idx % 2 == 0 else "pD"
                if idx >= 2:
                    p.wait("tensor", cV, vev[idx - 2])
                mt = p.op("tensor", tQ, lambda e, sc=sc, grp=grp, ps=ps: [
                    e.matmul(R(ps)[:, :],
                             R("xbt")[:, kc * 512 + sc * 128: kc * 512 + (sc + 1) * 128],
                             R("wqkvb")[:, kc * 3072 + 2048 + grp * 512: kc * 3072 + 2048 + (grp + 1) * 512],
                             start=(kc == 0), stop=(kc == 7)) for kc in range(8)][-1])
                p.wait("vector", tQ, mt)

                def vevac(e, sc=sc, grp=grp, ps=ps):
                    dst = bass.AP(R("vsb"), (sc * 16 + grp * 8) * 65,
                                  [[4 * 16 * 65, 128], [65, 8], [1, 64]])
                    src = bass.AP(R(ps), 0, [[512, 128], [64, 8], [1, 64]])
                    bia = bass.AP(R("vb"), grp * 512, [[1024, 128], [64, 8], [1, 64]])
                    return e.tensor_add(dst, src, bia)
                vev.append(p.op("vector", cV, vevac))
            p.wait("tensor", cV, vev[-1])

            kvadd = []
            exp_toks = {}
            for h in range(16):
                if h >= 1:
                    p.wait("tensor", cV, kvadd[h - 1])
                p.wait("tensor", cS, ev[h // 2])
                kt = None
                for sc in range(4):
                    pf = "pE" if sc % 2 == 0 else "pF"
                    if sc >= 2:
                        p.wait("tensor", cS, exp_toks[(h, sc - 2)])
                    elif h >= 1:
                        p.wait("tensor", cS, exp_toks[(h - 1, sc + 2)])
                    ft = p.op("tensor", tQ, lambda e, h=h, sc=sc, pf=pf: e.matmul(
                        R(pf)[:, 0:256], R("stk")[:, h * 512 + sc * 128: h * 512 + (sc + 1) * 128],
                        R("omega")[:, :], start=True, stop=True))
                    p.wait("scalar", tQ, ft)
                    p.wait("scalar", tV, tV.tok())
                    et = p.op("scalar", cS, lambda e, sc=sc, pf=pf: e.activation(
                        R("kpsb")[:, (sc % 2) * 512: (sc % 2) * 512 + 256], R(pf)[:, 0:256], AF.Exp))
                    exp_toks[(h, sc)] = et
                    p.wait("tensor", cS, et)
                    kt = p.op("tensor", tV, lambda e, h=h, sc=sc: [
                        e.matmul(R(("pG", "pH")[mc])[:, 0:65],
                                 R("kpsb")[:, (sc % 2) * 512 + mc * 128: (sc % 2) * 512 + mc * 128 + 128],
                                 R("vsb")[:, (sc * 16 + h) * 65: (sc * 16 + h + 1) * 65],
                                 start=(sc == 0), stop=(sc == 3)) for mc in range(2)][-1])
                p.wait("vector", tV, kt)
                kvadd.append(p.op("vector", cV, lambda e, h=h: [
                    e.tensor_add(R("kvacc")[:, h * 130: h * 130 + 65],
                                 R("kvacc")[:, h * 130: h * 130 + 65], R("pG")[:, 0:65]),
                    e.tensor_add(R("kvacc")[:, h * 130 + 65: h * 130 + 130],
                                 R("kvacc")[:, h * 130 + 65: h * 130 + 130], R("pH")[:, 0:65])][-1]))

        kvcast = p.op("vector", cV, lambda e: e.tensor_copy(R("kvbf")[:, :], R("kvacc")[:, :]))

        with p.loop(NTILE) as ti:
            p.barrier()
            p.wait("vector", oX, oX.tok())
            p.op("gpsimd", gX, lambda e: e.dma_start(
                bass.AP(R("xbt"), 0, [[4096, 128], [512, 8], [1, TT]]),
                bass.AP(io.x, p.off("gpsimd", 0, [(s_lv, 1024 * 4096), (ti, TT)]),
                        [[4096, 128], [128 * 4096, 8], [1, TT]])))
            p.op("gpsimd", S.gP1, lambda e: e.dma_start(
                bass.AP(io.wbf1, p.off("gpsimd", 0,
                                       [(s_lv, 8 * 8192 * 512), (ti, 8192 * 512)]),
                        [[512, 8192], [1, 512]]),
                bass.AP(io.w1w2_1, p.off("gpsimd", 0,
                                         [(s_lv, 8 * 8192 * 512), (ti, 8192 * 512)]),
                        [[512, 8192], [1, 512]])))
            p.op("scalar", vX, lambda e: e.dma_start(
                bass.AP(R("xt"), 0, [[4096, 128], [512, 8], [1, TT]]),
                bass.AP(io.x, p.off("scalar", 0, [(s_lv, 1024 * 4096), (ti, TT)]),
                        [[4096, 128], [128 * 4096, 8], [1, TT]])))
            xload32 = vX.tok()
            p.wait("tensor", gX, gX.tok())

            ev = []
            for j in range(8):
                ps = "pA" if j % 2 == 0 else "pB"
                if j >= 2:
                    p.wait("tensor", cS, ev[j - 2])
                mt = p.op("tensor", tQ, lambda e, j=j, ps=ps: [
                    e.matmul(R(ps)[:, :],
                             R("wqkvb")[:, kc * 3072 + j * 128: kc * 3072 + (j + 1) * 128],
                             R("xbt")[:, kc * 512:(kc + 1) * 512],
                             start=(kc == 0), stop=(kc == 7)) for kc in range(8)][-1])
                p.wait("scalar", tQ, mt)
                h0, h1 = 2 * j, 2 * j + 1
                et = p.op("scalar", cS, lambda e, j=j, ps=ps, h0=h0, h1=h1: [
                    e.activation(R("stk")[0:64, h0 * 512:(h0 + 1) * 512], R(ps)[0:64, :],
                                 AF.Identity, bias=R("bqkv")[0:64, j: j + 1]),
                    e.activation(R("stk")[64:128, h0 * 512:(h0 + 1) * 512], R(ps)[0:64, :],
                                 AF.Square, bias=R("bqkv")[0:64, j: j + 1]),
                    e.activation(R("stk")[0:64, h1 * 512:(h1 + 1) * 512], R(ps)[64:128, :],
                                 AF.Identity, bias=R("bqkv")[64:128, j: j + 1]),
                    e.activation(R("stk")[64:128, h1 * 512:(h1 + 1) * 512], R(ps)[64:128, :],
                                 AF.Square, bias=R("bqkv")[64:128, j: j + 1])][-1])
                ev.append(et)

            ho_toks, nd_toks, rt_toks = [], [], []
            expb = {}
            for h in range(16):
                if h >= 1:
                    p.wait("tensor", cS, expb[h - 1])
                p.wait("tensor", cS, ev[h // 2])
                qt = p.op("tensor", tQ, lambda e, h=h: [
                    e.matmul(R(("pC", "pD")[mc])[:, :], R("omega")[:, mc * 128:(mc + 1) * 128],
                             R("stk")[:, h * 512:(h + 1) * 512], start=True, stop=True)
                    for mc in range(2)][-1])
                p.wait("scalar", tQ, qt)
                if h >= 1:
                    p.wait("scalar", tV, nd_toks[h - 1])
                et = p.op("scalar", cS, lambda e: [
                    e.activation(R("kpsb")[:, mc * 512:(mc + 1) * 512], R(("pC", "pD")[mc])[:, :],
                                 AF.Exp) for mc in range(2)][-1])
                expb[h] = et
                p.wait("tensor", cS, et)
                if h == 0:
                    p.wait("tensor", cV, kvcast)
                if h >= 1:
                    p.wait("tensor", cV, ho_toks[h - 1])
                nt = p.op("tensor", tV, lambda e, h=h: [
                    e.matmul(R("pE")[0:65, :], R("kvbf")[:, h * 130 + mc * 65: h * 130 + (mc + 1) * 65],
                             R("kpsb")[:, mc * 512:(mc + 1) * 512],
                             start=(mc == 0), stop=(mc == 1)) for mc in range(2)][-1])
                nd_toks.append(nt)
                p.wait("vector", tV, nt)
                dt = p.op("vector", cV, lambda e: [
                    e.tensor_scalar(R("lnrow")[0:1, 0:512], R("pE")[64:65, :], 1e-6, None, ALU.add),
                    e.reciprocal(R("lnrow")[0:1, 512:1024], R("lnrow")[0:1, 0:512])][-1])
                p.wait("tensor", cV, dt)
                if h >= 1:
                    p.wait("tensor", cS, rt_toks[h - 1])
                bt2 = p.op("tensor", tQ, lambda e: e.matmul(
                    R("pF")[0:64, :], R("ones128")[0:1, 0:64], R("lnrow")[0:1, 512:1024],
                    start=True, stop=True))
                p.wait("scalar", tQ, bt2)
                if h >= 1:
                    p.wait("scalar", cV, ho_toks[h - 1])
                rt = p.op("scalar", cS, lambda e: e.copy(R("rbsb")[:, :], R("pF")[0:64, :]))
                rt_toks.append(rt)
                p.wait("vector", cS, rt)
                ho_toks.append(p.op("vector", cV, lambda e, h=h: e.tensor_mul(
                    R("hosb")[(h % 2) * 64:(h % 2) * 64 + 64, (h // 2) * 512:(h // 2 + 1) * 512],
                    R("pE")[0:64, :], R("rbsb")[:, :])))

            p.wait("tensor", cV, ho_toks[-1])
            y0_toks, wtoks = [], []
            for j in range(8):
                ps = "pG" if j % 2 == 0 else "pH"
                wo = "wo0" if j % 2 == 0 else "wo1"
                if j >= 2:
                    p.wait("tensor", cS, wtoks[j - 2])
                mt = p.op("tensor", tQ, lambda e, j=j, ps=ps: [
                    e.matmul(R(ps)[:, :],
                             R("woutb")[:, kc * 1024 + j * 128: kc * 1024 + (j + 1) * 128],
                             R("hosb")[:, kc * 512:(kc + 1) * 512],
                             start=(kc == 0), stop=(kc == 7)) for kc in range(8)][-1])
                p.wait("scalar", tQ, mt)
                wt = p.op("scalar", cS, lambda e, j=j, ps=ps, wo=wo: e.activation(
                    R(wo)[:, :], R(ps)[:, :], AF.Identity, bias=R("bout")[:, j: j + 1]))
                wtoks.append(wt)
                p.wait("vector", cS, wt)
                if j == 0:
                    p.wait("vector", vX, xload32)
                y0_toks.append(p.op("vector", cV, lambda e, j=j, wo=wo: e.tensor_add(
                    R("y0")[:, j * 512:(j + 1) * 512], R(wo)[:, :], R("xt")[:, j * 512:(j + 1) * 512])))

            def store_fn(e, s_lv=s_lv, ti=ti):
                return e.dma_start(
                    bass.AP(io.x1, p.off("scalar", 0, [(s_lv, 1024 * 4096), (ti, TT)]),
                            [[4096, 128], [128 * 4096, 8], [1, TT]]),
                    bass.AP(R("xout"), 0, [[4096, 128], [512, 8], [1, TT]]))

            emit_ln_store(p, sems, y0_toks, store_fn, oX)


def emit_l0ffn(p, S):
    gX, gC, sW, sX, oX, vX = S.gX, S.gC, S.sW, S.sX, S.oX, S.vX
    gP0 = S.gP0
    tQ, cS, cV = S.tQ, S.cS, S.cV
    sems = dict(tQ=tQ, cS=cS, cV=cV)
    CSZ = 2048 * 512
    TR_S, TR_TG, TR_E, TR_NC = C_TRIPS

    R = lambda n: p.r[n] if not p.dry else None
    io = p.io

    pc0 = gP0.tok()

    p.op("gpsimd", gC, lambda e: e.iota(R("it1")[:, :], [[1, 128]], channel_multiplier=0,
                                        allow_small_or_imprecise_dtypes=True))
    it = p.op("gpsimd", gC, lambda e: e.iota(R("it2")[:, :], [[0, 128]], channel_multiplier=1,
                                             allow_small_or_imprecise_dtypes=True))
    p.wait("vector", gC, it)
    ones_tok = p.op("vector", cV, lambda e: [
        e.tensor_tensor(R("identf")[:, :], R("it1")[:, :], R("it2")[:, :], ALU.is_equal),
        e.memset(R("ones128")[:, :], 1.0),
        e.memset(R("onesbf")[:, :], 1.0)][-1])
    p.op("gpsimd", gX, lambda e: e.dma_start(R("wrb")[:, :], io.wr0[:, :]))

    with p.loop(TR_S) as s_lv:
        with p.loop(TR_TG) as tg_lv:
            p.barrier()
            p.wait("vector", oX, oX.tok())
            p.wait("scalar", oX, oX.tok())
            p.wait("sync", gP0, pc0)
            p.wait("sync", sW, sW.tok())
            for par in range(2):
                p.op("sync", sW, lambda e, par=par: e.dma_start(
                    bass.AP(R("wb1_" + str(par)), 0, [[4096, 128], [512, 8], [1, 512]]),
                    bass.AP(io.wbf0, par * CSZ, [[512, 128], [128 * 512, 8], [1, 512]])))
                p.op("sync", sW, lambda e, par=par: e.dma_start(
                    bass.AP(R("wb2_" + str(par)), 0, [[4096, 128], [1024, 4], [1, 1024]]),
                    bass.AP(io.wbf0, par * CSZ + 1024 * 512, [[1024, 128], [128 * 1024, 4], [1, 1024]])))
            p.op("gpsimd", gX, lambda e: e.dma_start(
                bass.AP(R("xbf"), 0, [[8192, 128], [1024, 8], [1, 1024]]),
                bass.AP(io.x1, p.off("gpsimd", 0, [(s_lv, 1024 * 4096), (tg_lv, 1024)]),
                        [[4096, 128], [128 * 4096, 8], [1, 1024]])))
            xbf_tok = gX.tok()
            p.op("vector", cV, lambda e: e.memset(R("outacc")[:, :], 0.0))

            ROUTER = C_STAGE >= 1
            EXPERTS = C_STAGE >= 2
            p.wait("tensor", gX, xbf_tok)
            p.wait("tensor", cV, ones_tok)
            lg_toks = []
            for t2 in range(2 if ROUTER else 0):
                plg = ("pA", "pB")[t2]
                mt = p.op("tensor", tQ, lambda e, t2=t2, plg=plg: [
                    e.matmul(R(plg)[0:8, :], R("wrb")[:, kc * 8:(kc + 1) * 8],
                             R("xbf")[:, kc * 1024 + t2 * 512: kc * 1024 + t2 * 512 + 512],
                             start=(kc == 0), stop=(kc == 7)) for kc in range(8)][-1])
                p.wait("scalar", tQ, mt)
                lg_toks.append(p.op("scalar", cS, lambda e, t2=t2, plg=plg: e.copy(
                    R("lg")[0:8, t2 * 512:(t2 + 1) * 512], R(plg)[0:8, :])))
            c_toks = []
            rtm_toks = []
            for sub in range(8 if ROUTER else 0):
                p.wait("tensor", cS, lg_toks[1])
                if sub >= 1:
                    p.wait("tensor", cS, rtm_toks[sub - 1])
                tt = p.op("tensor", tQ, lambda e, sub=sub: e.transpose(
                    R("pA")[0:128, 0:8], R("lg")[0:8, sub * 128:(sub + 1) * 128],
                    R("identf")[0:8, 0:8]))
                p.wait("scalar", tQ, tt)
                if sub >= 1:
                    p.wait("scalar", cV, c_toks[sub - 1])
                rt = p.op("scalar", cS, lambda e: e.copy(R("rtm")[:, :], R("pA")[0:128, 0:8]))
                rtm_toks.append(rt)
                p.wait("vector", cS, rt)
                p.wait("scalar" if False else "vector", cS, rt)
                t = p.op("vector", cV, lambda e: e.reduce_max(
                    R("srow")[:, 0:1], R("rtm")[:, :], mybir.AxisListType.X))
                p.wait("vector", cV, t)
                ct = p.op("vector", cV, lambda e, sub=sub: e.tensor_scalar(
                    R("ctm")[:, sub * 8:(sub + 1) * 8], R("rtm")[:, :],
                    R("srow")[:, 0:1], None, ALU.subtract))
                p.wait("scalar", cV, ct)
                et = p.op("scalar", cS, lambda e, sub=sub: e.activation(
                    R("ctm")[:, sub * 8:(sub + 1) * 8], R("ctm")[:, sub * 8:(sub + 1) * 8], AF.Exp))
                p.wait("vector", cS, et)
                c_toks.append(p.vchain("vector", cV, [
                    lambda e, sub=sub: e.reduce_max(R("srow")[:, 3:4],
                                                    R("ctm")[:, sub * 8:(sub + 1) * 8],
                                                    mybir.AxisListType.X),
                    lambda e, sub=sub: e.tensor_scalar(R("rtm")[:, :],
                                                       R("ctm")[:, sub * 8:(sub + 1) * 8],
                                                       R("srow")[:, 3:4], None, ALU.is_lt),
                    lambda e, sub=sub: e.tensor_mul(R("rtm")[:, :], R("rtm")[:, :],
                                                    R("ctm")[:, sub * 8:(sub + 1) * 8]),
                    lambda e: e.reduce_max(R("srow")[:, 4:5], R("rtm")[:, :],
                                           mybir.AxisListType.X),
                    lambda e: e.tensor_add(R("srow")[:, 5:6], R("srow")[:, 3:4],
                                           R("srow")[:, 4:5]),
                    lambda e: e.reciprocal(R("srow")[:, 6:7], R("srow")[:, 5:6]),
                    lambda e, sub=sub: e.tensor_scalar(R("rtm")[:, :],
                                                       R("ctm")[:, sub * 8:(sub + 1) * 8],
                                                       R("srow")[:, 4:5], None, ALU.is_ge),
                    lambda e, sub=sub: e.tensor_mul(R("rtm")[:, :], R("rtm")[:, :],
                                                    R("ctm")[:, sub * 8:(sub + 1) * 8]),
                    lambda e, sub=sub: e.tensor_scalar(R("ctm")[:, sub * 8:(sub + 1) * 8],
                                                       R("rtm")[:, :],
                                                       R("srow")[:, 6:7], None, ALU.mult),
                ]))
            cf_toks = []
            for sub in range(8 if ROUTER else 0):
                p.wait("tensor", cV, c_toks[sub])
                if sub >= 1:
                    p.wait("tensor", cS, cf_toks[sub - 1])
                tt = p.op("tensor", tQ, lambda e, sub=sub: e.transpose(
                    R("pB")[0:8, 0:128], R("ctm")[:, sub * 8:(sub + 1) * 8], R("identf")[:, :]))
                p.wait("scalar", tQ, tt)
                cf_toks.append(p.op("scalar", cS, lambda e, sub=sub: e.copy(
                    R("cfm")[0:8, sub * 128:(sub + 1) * 128], R("pB")[0:8, 0:128])))
            if ROUTER:
                p.wait("sync", cS, cf_toks[-1])
                cdr_tok = p.op("sync", oX, lambda e: e.dma_start(io.cdr0[:, :], R("cfm")[:, :]))
            else:
                cdr_tok = oX.tok()

            if EXPERTS:
                with p.loop(TR_E) as e_lv:
                    cv_start = cV.tok()
                    cs_start = cS.tok()
                    sw_start = sW.tok()
                    p.wait("gpsimd", oX, cdr_tok)
                    p.wait("gpsimd", cV, cv_start)
                    ceb_tok = p.op("gpsimd", gX, lambda e: e.dma_start(
                        R("ceb")[:, :],
                        bass.AP(io.cdr0, p.off("gpsimd", 0, [(e_lv, 1024)]), [[0, 128], [1, 1024]])))
                    p.wait("vector", gX, ceb_tok)
                    xs_toks = []
                    for kc in range(8):
                        for t2 in range(2):
                            xs_toks.append(p.op("vector", cV, lambda e, kc=kc, t2=t2: e.tensor_mul(
                                R("xs")[:, kc * 1024 + t2 * 512: kc * 1024 + t2 * 512 + 512],
                                R("xbf")[:, kc * 1024 + t2 * 512: kc * 1024 + t2 * 512 + 512],
                                R("ceb")[:, t2 * 512:(t2 + 1) * 512])))
                    if C_STAGE >= 3:
                        with p.loop(TR_NC) as nc_lv:
                            sw_it = sW.tok()
                            cs_it = cS.tok()
                            cv_it = cV.tok()
                            for par in range(2):
                                wb1 = "wb1_" + str(par)
                                wb2 = "wb2_" + str(par)
                                p.wait("tensor", sW, sw_it)
                                p.wait("tensor", cV, xs_toks[-1])
                                relu_toks, add_toks = [], []
                                mm1_tok = mm2_tok = None
                                for t2 in range(2):
                                    p.wait("tensor", cS, cs_it if t2 == 0 else relu_toks[-1])
                                    mm1a = p.op("tensor", tQ, lambda e, wb1=wb1, t2=t2: [
                                        e.matmul(R(("pC", "pD")[oc])[:, :],
                                                 R(wb1)[:, kc * 512 + oc * 128: kc * 512 + (oc + 1) * 128],
                                                 R("xs")[:, kc * 1024 + t2 * 512: kc * 1024 + t2 * 512 + 512],
                                                 start=(kc == 0), stop=(kc == 7))
                                        for oc in range(2) for kc in range(8)][-1])
                                    mm1_tok = p.op("tensor", tQ, lambda e, wb1=wb1, t2=t2: [
                                        e.matmul(R(("pE", "pF")[oc - 2])[:, :],
                                                 R(wb1)[:, kc * 512 + oc * 128: kc * 512 + (oc + 1) * 128],
                                                 R("xs")[:, kc * 1024 + t2 * 512: kc * 1024 + t2 * 512 + 512],
                                                 start=(kc == 0), stop=(kc == 7))
                                        for oc in (2, 3) for kc in range(8)][-1])
                                    p.wait("scalar", tQ, mm1a)
                                    p.op("scalar", cS, lambda e: [
                                        e.activation(R("hid")[:, oc * 512:(oc + 1) * 512],
                                                     R(("pC", "pD")[oc])[:, :], AF.Relu)
                                        for oc in range(2)][-1])
                                    p.wait("scalar", tQ, mm1_tok)
                                    rt = p.op("scalar", cS, lambda e: [
                                        e.activation(R("hid")[:, oc * 512:(oc + 1) * 512],
                                                     R(("pE", "pF")[oc - 2])[:, :], AF.Relu)
                                        for oc in (2, 3)][-1])
                                    relu_toks.append(rt)
                                    p.wait("tensor", cS, rt)
                                    for jc in range(8):
                                        pw = ("pG", "pH")[jc % 2]
                                        a_idx = t2 * 8 + jc
                                        if a_idx >= 2:
                                            p.wait("tensor", cV, add_toks[a_idx - 2])
                                        else:
                                            p.wait("tensor", cV, cv_it)
                                        mm2_tok = p.op("tensor", tQ, lambda e, wb2=wb2, jc=jc, pw=pw: [
                                            e.matmul(R(pw)[:, :],
                                                     R(wb2)[:, kc2 * 1024 + jc * 128: kc2 * 1024 + (jc + 1) * 128],
                                                     R("hid")[:, kc2 * 512:(kc2 + 1) * 512],
                                                     start=(kc2 == 0), stop=(kc2 == 3)) for kc2 in range(4)][-1])
                                        p.wait("vector", tQ, mm2_tok)
                                        at = p.op("vector", cV, lambda e, jc=jc, t2=t2, pw=pw: e.tensor_add(
                                            R("outacc")[:, jc * 1024 + t2 * 512: jc * 1024 + t2 * 512 + 512],
                                            R("outacc")[:, jc * 1024 + t2 * 512: jc * 1024 + t2 * 512 + 512],
                                            R(pw)[:, :]))
                                        add_toks = add_toks + [at] if False else add_toks
                                        add_toks.append(at)
                                p.wait("sync", tQ, mm1_tok)
                                p.wait("sync", tQ, mm2_tok)
                                p.wait("sync", gP0, pc0)
                                p.op("sync", sW, lambda e, par=par, wb1=wb1: e.dma_start(
                                    bass.AP(R(wb1), 0, [[4096, 128], [512, 8], [1, 512]]),
                                    bass.AP(io.wbf0, p.off("sync", (par + 2) * CSZ,
                                                           [(e_lv, 8 * CSZ), (nc_lv, 2 * CSZ)]),
                                            [[512, 128], [128 * 512, 8], [1, 512]])))
                                p.op("sync", sW, lambda e, par=par, wb2=wb2: e.dma_start(
                                    bass.AP(R(wb2), 0, [[4096, 128], [1024, 4], [1, 1024]]),
                                    bass.AP(io.wbf0, p.off("sync", (par + 2) * CSZ + 1024 * 512,
                                                           [(e_lv, 8 * CSZ), (nc_lv, 2 * CSZ)]),
                                            [[1024, 128], [128 * 1024, 4], [1, 1024]])))

            for t2 in range(2):
                p.wait("vector", cV, cV.tok())
                xt_tok = p.op("scalar", vX, lambda e, t2=t2: e.dma_start(
                    bass.AP(R("xt"), 0, [[4096, 128], [512, 8], [1, 512]]),
                    bass.AP(io.x1, p.off("scalar", t2 * 512, [(s_lv, 1024 * 4096), (tg_lv, 1024)]),
                            [[4096, 128], [128 * 4096, 8], [1, 512]])))
                p.wait("vector", vX, xt_tok)
                y0_toks = []
                for jc in range(8):
                    y0_toks.append(p.op("vector", cV, lambda e, jc=jc, t2=t2: e.tensor_add(
                        R("y0")[:, jc * 512:(jc + 1) * 512],
                        R("outacc")[:, jc * 1024 + t2 * 512: jc * 1024 + t2 * 512 + 512],
                        R("xt")[:, jc * 512:(jc + 1) * 512])))

                def store_fn(e, t2=t2):
                    return e.dma_start(
                        bass.AP(io.x2, p.off("scalar", t2 * 512, [(s_lv, 1024 * 4096), (tg_lv, 1024)]),
                                [[4096, 128], [128 * 4096, 8], [1, 512]]),
                        bass.AP(R("xout"), 0, [[4096, 128], [512, 8], [1, 512]]))

                emit_ln_store(p, sems, y0_toks, store_fn, oX)


def emit_rglru(p, S):
    gX, sX, oX, vX = S.gX, S.sX, S.oX, S.vX
    tQ, cS, cV = S.tQ, S.cS, S.cV
    sems = dict(tQ=tQ, cS=cS, cV=cV)

    R = lambda n: p.r[n] if not p.dry else None
    io = p.io

    p.op("gpsimd", gX, lambda e: e.dma_start(
        bass.AP(R("igwb"), 0, [[1024, 128], [128, 8], [1, 128]]),
        bass.AP(io.igw, 0, [[128, 128], [128 * 128, 8], [1, 128]])))
    p.op("gpsimd", gX, lambda e: e.dma_start(
        bass.AP(R("agwb"), 0, [[1024, 128], [128, 8], [1, 128]]),
        bass.AP(io.agw, 0, [[128, 128], [128 * 128, 8], [1, 128]])))
    p.op("sync", sX, lambda e: e.dma_start(R("igb")[:, :], io.igb[:, :]))
    p.op("sync", sX, lambda e: e.dma_start(R("agb")[:, :], io.agb[:, :]))
    apt = p.op("sync", sX, lambda e: e.dma_start(R("apv")[:, :], io.ap[:, :]))
    ones_tok = p.op("vector", cV, lambda e: [
        e.memset(R("ones128")[:, :], 1.0),
        e.memset(R("onesbf")[:, :], 1.0),
        e.memset(R("onesf")[:, :], 1.0)][-1])
    p.wait("scalar", sX, apt)
    t = p.op("scalar", cS, lambda e: e.activation(R("sp8")[:, 0:8], R("apv")[:, :], AF.Abs))
    p.wait("scalar", cS, t)
    t = p.op("scalar", cS, lambda e: e.activation(R("sp8")[:, 8:16], R("sp8")[:, 0:8], AF.Exp, scale=-1.0))
    p.wait("vector", cS, t)
    t = p.vchain("vector", cV, [
        lambda e: e.tensor_scalar(R("sp8")[:, 16:24], R("sp8")[:, 8:16], 1.0, None, ALU.add),
    ])
    p.wait("scalar", cV, t)
    t = p.op("scalar", cS, lambda e: e.activation(R("sp8")[:, 24:32], R("sp8")[:, 16:24], AF.Ln))
    p.wait("vector", cS, t)
    p.vchain("vector", cV, [
        lambda e: e.tensor_scalar(R("cvec")[:, :], R("apv")[:, :], 0.0, None, ALU.max),
        lambda e: e.tensor_add(R("cvec")[:, :], R("cvec")[:, :], R("sp8")[:, 24:32]),
        lambda e: e.tensor_scalar(R("cvec")[:, :], R("cvec")[:, :], -8.0, None, ALU.mult),
    ])

    with p.loop(SPC) as s_lv:
        p.barrier()
        p.op("vector", cV, lambda e: [
            e.memset(R("lc")[:, :], 0.0),
            e.memset(R("sc")[:, :], 0.0)][-1])
        with p.loop(NTILE) as ti:
            p.barrier()
            p.wait("vector", oX, oX.tok())
            p.op("gpsimd", gX, lambda e: e.dma_start(
                bass.AP(R("xbt"), 0, [[4096, 128], [512, 8], [1, TT]]),
                bass.AP(io.x2, p.off("gpsimd", 0, [(s_lv, 1024 * 4096), (ti, TT)]),
                        [[4096, 128], [128 * 4096, 8], [1, TT]])))
            xbt_tok = gX.tok()
            p.op("gpsimd", S.gP1, lambda e: e.dma_start(
                bass.AP(io.wbf1, p.off("gpsimd", 131072 * 512,
                                       [(s_lv, 8 * 4096 * 512), (ti, 4096 * 512)]),
                        [[512, 4096], [1, 512]]),
                bass.AP(io.w1w2_1, p.off("gpsimd", 131072 * 512,
                                         [(s_lv, 8 * 4096 * 512), (ti, 4096 * 512)]),
                        [[512, 4096], [1, 512]])))
            p.op("scalar", vX, lambda e: e.dma_start(
                bass.AP(R("xt"), 0, [[4096, 128], [512, 8], [1, TT]]),
                bass.AP(io.x2, p.off("scalar", 0, [(s_lv, 1024 * 4096), (ti, TT)]),
                        [[4096, 128], [128 * 4096, 8], [1, TT]])))
            xt_tok = vX.tok()
            p.wait("tensor", gX, xbt_tok)
            p.wait("tensor", cV, ones_tok)

            y0_toks = []
            sig_toks = []
            for h in range(8):
                pig = ("pA", "pB")[h % 2]
                pag = ("pC", "pD")[h % 2]
                if h >= 2:
                    p.wait("tensor", cS, sig_toks[h - 2])
                mt = p.op("tensor", tQ, lambda e, h=h, pig=pig, pag=pag: [
                    e.matmul(R(pig)[:, :], R("igwb")[:, h * 128:(h + 1) * 128],
                             R("xbt")[:, h * 512:(h + 1) * 512], start=True, stop=True),
                    e.matmul(R(pag)[:, :], R("agwb")[:, h * 128:(h + 1) * 128],
                             R("xbt")[:, h * 512:(h + 1) * 512], start=True, stop=True)][-1])
                p.wait("scalar", tQ, mt)
                if h >= 1:
                    p.wait("scalar", cV, y0_toks[h - 1])
                st = p.op("scalar", cS, lambda e, h=h, pig=pig, pag=pag: [
                    e.activation(R("gx")[:, :], R(pig)[:, :], AF.Sigmoid,
                                 bias=R("igb")[:, h: h + 1]),
                    e.activation(R("ga")[:, :], R(pag)[:, :], AF.Sigmoid,
                                 bias=R("agb")[:, h: h + 1])][-1])
                sig_toks.append(st)
                p.wait("vector", cS, st)
                t = p.op("vector", cV, lambda e, h=h: e.tensor_scalar(
                    R("la")[:, :], R("ga")[:, :], R("cvec")[:, h: h + 1], None, ALU.mult))
                p.wait("vector", cV, t)
                t = p.op("vector", cV, lambda e, h=h: e.tensor_tensor_scan(
                    R("li")[:, :], R("onesf")[:, :], R("la")[:, :],
                    R("lc")[:, h: h + 1], ALU.mult, ALU.add))
                p.wait("vector", cV, t)
                lvt = p.op("vector", cV, lambda e, h=h: [
                    e.tensor_copy(R("lc")[:, h: h + 1], R("li")[:, 511:512]),
                    e.tensor_sub(R("lx")[:, :], R("li")[:, :], R("la")[:, :]),
                    e.tensor_scalar(R("lx")[:, :], R("lx")[:, :], LN7, None, ALU.max)][-1])
                p.wait("scalar", cV, lvt)
                et = p.op("scalar", cS, lambda e: [
                    e.activation(R("pp")[:, :], R("lx")[:, :], AF.Exp),
                    e.activation(R("ppi")[:, :], R("lx")[:, :], AF.Exp, scale=-1.0),
                    e.activation(R("a2")[:, :], R("la")[:, :], AF.Exp, scale=2.0)][-1])
                p.wait("vector", cS, et)
                t = p.op("vector", cV, lambda e: [
                    e.tensor_scalar(R("ml")[:, :], R("a2")[:, :], -1.0, 1.0, ALU.mult, ALU.add),
                    e.tensor_scalar(R("ml")[:, :], R("ml")[:, :], 1e-7, None, ALU.max)][-1])
                p.wait("scalar", cV, t)
                sqt = p.op("scalar", cS, lambda e: e.activation(R("ml")[:, :], R("ml")[:, :], AF.Sqrt))
                p.wait("vector", cS, sqt)
                if h == 0:
                    p.wait("vector", vX, xt_tok)
                t = p.op("vector", cV, lambda e, h=h: [
                    e.tensor_mul(R("nx")[:, :], R("xt")[:, h * 512:(h + 1) * 512], R("gx")[:, :]),
                    e.tensor_mul(R("nx")[:, :], R("nx")[:, :], R("ml")[:, :]),
                    e.tensor_mul(R("nx")[:, :], R("nx")[:, :], R("ppi")[:, :])][-1])
                p.wait("vector", cV, t)
                t = p.op("vector", cV, lambda e, h=h: e.tensor_tensor_scan(
                    R("ss")[:, :], R("onesf")[:, :], R("nx")[:, :],
                    R("sc")[:, h: h + 1], ALU.mult, ALU.add))
                p.wait("vector", cV, t)
                y0_toks.append(p.op("vector", cV, lambda e, h=h: [
                    e.tensor_copy(R("sc")[:, h: h + 1], R("ss")[:, 511:512]),
                    e.tensor_mul(R("ss")[:, :], R("pp")[:, :], R("ss")[:, :]),
                    e.tensor_add(R("y0")[:, h * 512:(h + 1) * 512],
                                 R("xt")[:, h * 512:(h + 1) * 512], R("ss")[:, :])][-1]))

            def store_fn(e, s_lv=s_lv, ti=ti):
                return e.dma_start(
                    bass.AP(io.x3, p.off("scalar", 0, [(s_lv, 1024 * 4096), (ti, TT)]),
                            [[4096, 128], [128 * 4096, 8], [1, TT]]),
                    bass.AP(R("xout"), 0, [[4096, 128], [512, 8], [1, TT]]))

            emit_ln_store(p, sems, y0_toks, store_fn, oX)


def emit_l1ffn(p, S):
    gX, gC, sW, sX, oX, vX = S.gX, S.gC, S.sW, S.sX, S.oX, S.vX
    gP1 = S.gP1
    tQ, cS, cV = S.tQ, S.cS, S.cV
    sems = dict(tQ=tQ, cS=cS, cV=cV)
    CSZ = 3072 * 512
    TR_S, TR_TG, TR_E, TR_NC = C_TRIPS

    R = lambda n: p.r[n] if not p.dry else None
    io = p.io

    pc1 = gP1.tok()

    p.op("gpsimd", gC, lambda e: e.iota(R("xt")[:, 0:128], [[1, 128]], channel_multiplier=0,
                                        allow_small_or_imprecise_dtypes=True))
    it = p.op("gpsimd", gC, lambda e: e.iota(R("y0")[:, 0:128], [[0, 128]], channel_multiplier=1,
                                             allow_small_or_imprecise_dtypes=True))
    p.wait("vector", gC, it)
    ones_tok = p.op("vector", cV, lambda e: [
        e.tensor_tensor(R("identf")[:, :], R("xt")[:, 0:128], R("y0")[:, 0:128], ALU.is_equal),
        e.memset(R("ones128")[:, :], 1.0),
        e.memset(R("onesbf")[:, :], 1.0)][-1])
    p.op("gpsimd", gX, lambda e: e.dma_start(R("wrb")[:, :], io.wr1[:, :]))
    p.op("gpsimd", gX, lambda e: e.dma_start(
        R("rbb")[:, :], bass.AP(io.rb, 0, [[0, 128], [1, 8]])))

    with p.loop(TR_S) as s_lv:
        with p.loop(TR_TG) as tg_lv:
            p.barrier()
            p.wait("vector", oX, oX.tok())
            p.wait("scalar", oX, oX.tok())
            p.wait("sync", gP1, pc1)
            p.wait("sync", sW, sW.tok())
            for par in range(2):
                for ci, wdst in ((0, "wa_"), (1, "wg_"), (2, "wb2_")):
                    pat = [[4096, 128], [512, 8], [1, 512]] if ci < 2 else [[4096, 128], [1024, 4], [1, 1024]]
                    spat = [[512, 128], [128 * 512, 8], [1, 512]] if ci < 2 else [[1024, 128], [128 * 1024, 4], [1, 1024]]
                    p.op("sync", sW, lambda e, par=par, ci=ci, wdst=wdst, pat=pat, spat=spat: e.dma_start(
                        bass.AP(R(wdst + str(par)), 0, pat),
                        bass.AP(io.wbf1, par * CSZ + ci * 1024 * 512, spat)))
            p.op("gpsimd", gX, lambda e: e.dma_start(
                bass.AP(R("xbf"), 0, [[8192, 128], [1024, 8], [1, 1024]]),
                bass.AP(io.x3, p.off("gpsimd", 0, [(s_lv, 1024 * 4096), (tg_lv, 1024)]),
                        [[4096, 128], [128 * 4096, 8], [1, 1024]])))
            xbf_tok = gX.tok()
            p.op("vector", cV, lambda e: e.memset(R("outacc")[:, :], 0.0))

            p.wait("tensor", gX, xbf_tok)
            p.wait("tensor", cV, ones_tok)
            lg_toks = []
            for t2 in range(2):
                plg = ("pA", "pB")[t2]
                mt = p.op("tensor", tQ, lambda e, t2=t2, plg=plg: [
                    e.matmul(R(plg)[0:8, :], R("wrb")[:, kc * 8:(kc + 1) * 8],
                             R("xbf")[:, kc * 1024 + t2 * 512: kc * 1024 + t2 * 512 + 512],
                             start=(kc == 0), stop=(kc == 7)) for kc in range(8)][-1])
                p.wait("scalar", tQ, mt)
                lg_toks.append(p.op("scalar", cS, lambda e, t2=t2, plg=plg: e.copy(
                    R("lg")[0:8, t2 * 512:(t2 + 1) * 512], R(plg)[0:8, :])))
            c_toks = []
            rtm_toks = []
            for sub in range(8):
                p.wait("tensor", cS, lg_toks[1])
                if sub >= 1:
                    p.wait("tensor", cS, rtm_toks[sub - 1])
                tt = p.op("tensor", tQ, lambda e, sub=sub: e.transpose(
                    R("pA")[0:128, 0:8], R("lg")[0:8, sub * 128:(sub + 1) * 128],
                    R("identf")[0:8, 0:8]))
                p.wait("scalar", tQ, tt)
                if sub >= 1:
                    p.wait("scalar", cV, c_toks[sub - 1])
                rt = p.op("scalar", cS, lambda e: e.copy(R("rtm")[:, :], R("pA")[0:128, 0:8]))
                rtm_toks.append(rt)
                p.wait("vector", cS, rt)
                t = p.vchain("vector", cV, [
                    lambda e: e.tensor_add(R("rtm")[:, :], R("rtm")[:, :], R("rbb")[:, :]),
                    lambda e: e.reduce_max(R("srow")[:, 0:1], R("rtm")[:, :], mybir.AxisListType.X),
                    lambda e, sub=sub: e.tensor_scalar(R("ctm")[:, sub * 8:(sub + 1) * 8],
                                                       R("rtm")[:, :], R("srow")[:, 0:1],
                                                       None, ALU.subtract),
                ])
                p.wait("scalar", cV, t)
                et = p.op("scalar", cS, lambda e, sub=sub: e.activation(
                    R("ctm")[:, sub * 8:(sub + 1) * 8], R("ctm")[:, sub * 8:(sub + 1) * 8], AF.Exp))
                p.wait("vector", cS, et)
                c_toks.append(p.vchain("vector", cV, [
                    lambda e, sub=sub: e.reduce_sum(R("srow")[:, 1:2],
                                                    R("ctm")[:, sub * 8:(sub + 1) * 8],
                                                    mybir.AxisListType.X),
                    lambda e: e.reciprocal(R("srow")[:, 2:3], R("srow")[:, 1:2]),
                    lambda e, sub=sub: e.reduce_max(R("srow")[:, 3:4],
                                                    R("ctm")[:, sub * 8:(sub + 1) * 8],
                                                    mybir.AxisListType.X),
                    lambda e: e.tensor_mul(R("srow")[:, 4:5], R("srow")[:, 3:4], R("srow")[:, 2:3]),
                    lambda e, sub=sub: e.tensor_scalar(R("rtm")[:, :],
                                                       R("ctm")[:, sub * 8:(sub + 1) * 8],
                                                       R("srow")[:, 3:4], None, ALU.is_ge),
                    lambda e, sub=sub: e.tensor_scalar(R("ctm")[:, sub * 8:(sub + 1) * 8],
                                                       R("rtm")[:, :],
                                                       R("srow")[:, 4:5], None, ALU.mult),
                ]))
            cf_toks = []
            for sub in range(8):
                p.wait("tensor", cV, c_toks[sub])
                if sub >= 1:
                    p.wait("tensor", cS, cf_toks[sub - 1])
                tt = p.op("tensor", tQ, lambda e, sub=sub: e.transpose(
                    R("pB")[0:8, 0:128], R("ctm")[:, sub * 8:(sub + 1) * 8], R("identf")[:, :]))
                p.wait("scalar", tQ, tt)
                cf_toks.append(p.op("scalar", cS, lambda e, sub=sub: e.copy(
                    R("cfm")[0:8, sub * 128:(sub + 1) * 128], R("pB")[0:8, 0:128])))
            p.wait("sync", cS, cf_toks[-1])
            cdr_tok = p.op("sync", oX, lambda e: e.dma_start(io.cdr1[:, :], R("cfm")[:, :]))

            with p.loop(TR_E) as e_lv:
                cv_start = cV.tok()
                cs_start = cS.tok()
                p.wait("gpsimd", oX, cdr_tok)
                p.wait("gpsimd", cV, cv_start)
                ceb_tok = p.op("gpsimd", gX, lambda e: e.dma_start(
                    R("ceb")[:, :],
                    bass.AP(io.cdr1, p.off("gpsimd", 0, [(e_lv, 1024)]), [[0, 128], [1, 1024]])))
                with p.loop(TR_NC) as nc_lv:
                    sw_it = sW.tok()
                    cs_it = cS.tok()
                    cv_it = cV.tok()
                    for par in range(2):
                        wa, wg, wb2 = f"wa_{par}", f"wg_{par}", f"wb2_{par}"
                        p.wait("tensor", sW, sw_it)
                        p.wait("tensor", gX, ceb_tok) if False else None
                        add_toks = []
                        mm_last = None
                        for t2 in range(2):
                            p.wait("tensor", cS, cs_it if t2 == 0 else sil_tok)
                            if t2 == 0:
                                p.wait("tensor", cV, cv_it)
                            else:
                                p.wait("tensor", cV, hid_tok)
                            mm_a = p.op("tensor", tQ, lambda e, wa=wa, t2=t2: [
                                e.matmul(R(("pC", "pD", "pE", "pF")[oc])[:, :],
                                         R(wa)[:, kc * 512 + oc * 128: kc * 512 + (oc + 1) * 128],
                                         R("xbf")[:, kc * 1024 + t2 * 512: kc * 1024 + t2 * 512 + 512],
                                         start=(kc == 0), stop=(kc == 7))
                                for oc in range(4) for kc in range(8)][-1])
                            mm_g01 = p.op("tensor", tQ, lambda e, wg=wg, t2=t2: [
                                e.matmul(R(("pA", "pB")[oc])[:, :],
                                         R(wg)[:, kc * 512 + oc * 128: kc * 512 + (oc + 1) * 128],
                                         R("xbf")[:, kc * 1024 + t2 * 512: kc * 1024 + t2 * 512 + 512],
                                         start=(kc == 0), stop=(kc == 7))
                                for oc in range(2) for kc in range(8)][-1])
                            p.wait("scalar", tQ, mm_a)
                            sil_tok = p.op("scalar", cS, lambda e: [
                                e.activation(R("sa")[:, oc * 512:(oc + 1) * 512],
                                             R(("pC", "pD", "pE", "pF")[oc])[:, :], AF.Silu)
                                for oc in range(4)][-1])
                            p.wait("vector", tQ, mm_g01)
                            p.wait("vector", gX, ceb_tok)
                            gs01 = p.op("vector", cV, lambda e: [
                                e.tensor_scalar(R("gs")[:, oc * 512:(oc + 1) * 512],
                                                R(("pA", "pB")[oc])[:, :],
                                                1.0, None, ALU.mult)
                                for oc in range(2)][-1])
                            p.wait("tensor", cS, sil_tok)
                            mm_g23 = p.op("tensor", tQ, lambda e, wg=wg, t2=t2: [
                                e.matmul(R(("pC", "pD")[oc - 2])[:, :],
                                         R(wg)[:, kc * 512 + oc * 128: kc * 512 + (oc + 1) * 128],
                                         R("xbf")[:, kc * 1024 + t2 * 512: kc * 1024 + t2 * 512 + 512],
                                         start=(kc == 0), stop=(kc == 7))
                                for oc in (2, 3) for kc in range(8)][-1])
                            p.wait("vector", tQ, mm_g23)
                            gs_tok = p.op("vector", cV, lambda e: [
                                e.tensor_scalar(R("gs")[:, oc * 512:(oc + 1) * 512],
                                                R(("pC", "pD")[oc - 2])[:, :],
                                                1.0, None, ALU.mult)
                                for oc in (2, 3)][-1])
                            p.wait("vector", cS, sil_tok)
                            hid_tok = p.op("vector", cV, lambda e, t2=t2: [
                                [e.tensor_mul(R("gs")[:, oc * 512:(oc + 1) * 512],
                                              R("gs")[:, oc * 512:(oc + 1) * 512],
                                              R("ceb")[:, t2 * 512:(t2 + 1) * 512]),
                                 e.tensor_mul(R("hid")[:, oc * 512:(oc + 1) * 512],
                                              R("gs")[:, oc * 512:(oc + 1) * 512],
                                              R("sa")[:, oc * 512:(oc + 1) * 512])][-1]
                                for oc in range(4)][-1])
                            p.wait("tensor", cV, hid_tok)
                            for jc in range(8):
                                pw = ("pG", "pH")[jc % 2]
                                a_idx = t2 * 8 + jc
                                if a_idx >= 2:
                                    p.wait("tensor", cV, add_toks[a_idx - 2])
                                else:
                                    p.wait("tensor", cV, cv_it)
                                mm_last = p.op("tensor", tQ, lambda e, wb2=wb2, jc=jc, pw=pw: [
                                    e.matmul(R(pw)[:, :],
                                             R(wb2)[:, kc2 * 1024 + jc * 128: kc2 * 1024 + (jc + 1) * 128],
                                             R("hid")[:, kc2 * 512:(kc2 + 1) * 512],
                                             start=(kc2 == 0), stop=(kc2 == 3)) for kc2 in range(4)][-1])
                                p.wait("vector", tQ, mm_last)
                                add_toks.append(p.op("vector", cV, lambda e, jc=jc, t2=t2, pw=pw: e.tensor_add(
                                    R("outacc")[:, jc * 1024 + t2 * 512: jc * 1024 + t2 * 512 + 512],
                                    R("outacc")[:, jc * 1024 + t2 * 512: jc * 1024 + t2 * 512 + 512],
                                    R(pw)[:, :])))
                        p.wait("sync", tQ, mm_last)
                        for ci, wdst in ((0, "wa_"), (1, "wg_"), (2, "wb2_")):
                            pat = [[4096, 128], [512, 8], [1, 512]] if ci < 2 else [[4096, 128], [1024, 4], [1, 1024]]
                            spat = [[512, 128], [128 * 512, 8], [1, 512]] if ci < 2 else [[1024, 128], [128 * 1024, 4], [1, 1024]]
                            p.op("sync", sW, lambda e, par=par, ci=ci, wdst=wdst, pat=pat, spat=spat: e.dma_start(
                                bass.AP(R(wdst + str(par)), 0, pat),
                                bass.AP(io.wbf1, p.off("sync", (par + 2) * CSZ + ci * 1024 * 512,
                                                      [(e_lv, 8 * CSZ), (nc_lv, 2 * CSZ)]),
                                        spat)))

            p.wait("vector", sW, sW.tok())
            for t2 in range(2):
                p.wait("vector", cV, cV.tok())
                xt_tok = p.op("scalar", vX, lambda e, t2=t2: e.dma_start(
                    bass.AP(R("xt"), 0, [[4096, 128], [512, 8], [1, 512]]),
                    bass.AP(io.x3, p.off("scalar", t2 * 512, [(s_lv, 1024 * 4096), (tg_lv, 1024)]),
                            [[4096, 128], [128 * 4096, 8], [1, 512]])))
                p.wait("vector", vX, xt_tok)
                y0_toks = []
                for jc in range(8):
                    y0_toks.append(p.op("vector", cV, lambda e, jc=jc, t2=t2: e.tensor_add(
                        R("y0")[:, jc * 512:(jc + 1) * 512],
                        R("outacc")[:, jc * 1024 + t2 * 512: jc * 1024 + t2 * 512 + 512],
                        R("xt")[:, jc * 512:(jc + 1) * 512])))

                def store_fn(e, t2=t2):
                    return e.dma_start(
                        bass.AP(io.x4, p.off("scalar", t2 * 512, [(s_lv, 1024 * 4096), (tg_lv, 1024)]),
                                [[4096, 128], [128 * 4096, 8], [1, 512]]),
                        bass.AP(R("xout"), 0, [[4096, 128], [512, 8], [1, 512]]))

                emit_ln_store(p, sems, y0_toks, store_fn, oX,
                              yb=("wa_0", 0), y2b=("wg_0", 0))


S1DEFS = [
    ("wqkvb", [128, 8 * 3072], BF16), ("woutb", [128, 8 * 1024], BF16),
    ("bqkv", [128, 24], F32), ("bout", [128, 8], F32), ("omega", [128, 256], BF16),
    ("ones128", [1, 128], F32), ("onesbf", [128, 1], BF16), ("vb", [128, 1024], F32),
    ("bqr", [1, 1024], F32), ("xbt", [128, 4096], BF16), ("xt", [128, 4096], F32),
    ("stk", [128, 16 * 512], BF16), ("vsb", [128, 4 * 16 * 65], BF16),
    ("kvacc", [128, 16 * 130], F32), ("kvbf", [128, 16 * 130], BF16),
    ("kpsb", [128, 2 * 512], BF16), ("rbsb", [64, 512], F32), ("hosb", [128, 8 * 512], BF16),
    ("wo0", [128, 512], F32), ("wo1", [128, 512], F32), ("y0", [128, 4096], F32),
    ("xout", [128, 4096], F32), ("lnrow", [1, 6 * 512], F32), ("mbr", [128, 512], F32),
    ("rbr", [128, 512], F32),
]
S2DEFS = [
    ("identf", [128, 128], F32), ("it1", [128, 128], F32), ("it2", [128, 128], F32),
    ("ones128", [1, 128], F32), ("onesbf", [128, 1], BF16), ("wrb", [128, 64], BF16),
    ("xbf", [128, 8 * 1024], BF16), ("outacc", [128, 8 * 1024], F32),
    ("xs", [128, 8 * 1024], BF16),
    ("wb1_0", [128, 4096], BF16), ("wb1_1", [128, 4096], BF16),
    ("wb2_0", [128, 4096], BF16), ("wb2_1", [128, 4096], BF16),
    ("hid", [128, 4 * 512], BF16), ("ceb", [128, 1024], BF16), ("lg", [8, 1024], F32),
    ("rtm", [128, 8], F32), ("srow", [128, 8], F32), ("ctm", [128, 64], F32),
    ("cfm", [8, 1024], F32), ("xt", [128, 4096], F32), ("y0", [128, 4096], F32),
    ("xout", [128, 4096], F32), ("stk", [128, 8192], BF16), ("lnrow", [1, 6 * 512], F32),
    ("mbr", [128, 512], F32), ("rbr", [128, 512], F32),
]
S3DEFS = [
    ("igwb", [128, 8 * 128], BF16), ("agwb", [128, 8 * 128], BF16),
    ("igb", [128, 8], F32), ("agb", [128, 8], F32), ("apv", [128, 8], F32),
    ("cvec", [128, 8], F32), ("sp8", [128, 4 * 8], F32),
    ("ones128", [1, 128], F32), ("onesbf", [128, 1], BF16), ("onesf", [128, 512], F32),
    ("xt", [128, 4096], F32), ("xbt", [128, 4096], BF16),
    ("gx", [128, 512], F32), ("ga", [128, 512], F32), ("la", [128, 512], F32),
    ("li", [128, 512], F32), ("lx", [128, 512], F32), ("pp", [128, 512], F32),
    ("ppi", [128, 512], F32), ("a2", [128, 512], F32), ("ml", [128, 512], F32),
    ("nx", [128, 512], F32), ("ss", [128, 512], F32),
    ("lc", [128, 8], F32), ("sc", [128, 8], F32), ("y0", [128, 4096], F32),
    ("xout", [128, 4096], F32), ("stk", [128, 8192], BF16), ("lnrow", [1, 6 * 512], F32),
    ("mbr", [128, 512], F32), ("rbr", [128, 512], F32),
]
S4DEFS = [
    ("identf", [128, 128], F32), ("ones128", [1, 128], F32), ("onesbf", [128, 1], BF16),
    ("wrb", [128, 64], BF16), ("rbb", [128, 8], F32), ("xbf", [128, 8 * 1024], BF16),
    ("outacc", [128, 8 * 1024], F32),
    ("wa_0", [128, 4096], BF16), ("wa_1", [128, 4096], BF16),
    ("wg_0", [128, 4096], BF16), ("wg_1", [128, 4096], BF16),
    ("wb2_0", [128, 4096], BF16), ("wb2_1", [128, 4096], BF16),
    ("sa", [128, 2048], F32), ("gs", [128, 2048], F32), ("hid", [128, 4 * 512], BF16),
    ("ceb", [128, 1024], F32), ("lg", [8, 1024], F32), ("rtm", [128, 8], F32),
    ("srow", [128, 8], F32), ("ctm", [128, 64], F32), ("cfm", [8, 1024], F32),
    ("xt", [128, 4096], F32), ("y0", [128, 4096], F32), ("xout", [128, 4096], F32),
    ("lnrow", [1, 6 * 512], F32), ("mbr", [128, 512], F32), ("rbr", [128, 512], F32),
]


def prog_full(p):
    S = NS()
    for owner, name, step in (("gpsimd", "gP0", 16), ("gpsimd", "gP1", 16),
                              ("gpsimd", "gX", 16), ("gpsimd", "gC", 1),
                              ("sync", "sW", 16), ("sync", "sX", 16),
                              ("scalar", "vX", 16),
                              ("sync", "oX", 16), ("tensor", "tQ", 1),
                              ("tensor", "tV", 1), ("scalar", "cS", 1),
                              ("vector", "cV", 1), ("sync", "bS", 1),
                              ("gpsimd", "bG", 1)):
        S[name] = p.sem(owner, name, step)

    if not p.dry:
        nc, C = p.nc, p.ctx
        io = p.io
        io.x = nc.dram_tensor("x", [SPC * 1024, 4096], F32, kind="ExternalInput")
        io.wqkv = nc.dram_tensor("wqkv", [1024, 3072], F32, kind="ExternalInput")
        io.bqkv = nc.dram_tensor("bqkv_t", [128, 24], F32, kind="ExternalInput")
        io.bqv = nc.dram_tensor("bqv", [1, 1024], F32, kind="ExternalInput")
        io.wout = nc.dram_tensor("wout", [1024, 1024], F32, kind="ExternalInput")
        io.bout = nc.dram_tensor("bout_t", [128, 8], F32, kind="ExternalInput")
        io.omega = nc.dram_tensor("omega", [64, 256], F32, kind="ExternalInput")
        io.wr0 = nc.dram_tensor("wr0", [128, 64], F32, kind="ExternalInput")
        io.w1w2_0 = nc.dram_tensor("w1w2_0", [64 * 2048, 512], F32, kind="ExternalInput")
        io.ap = nc.dram_tensor("ap_t", [128, 8], F32, kind="ExternalInput")
        io.igw = nc.dram_tensor("igw", [1024, 128], F32, kind="ExternalInput")
        io.igb = nc.dram_tensor("igb_t", [128, 8], F32, kind="ExternalInput")
        io.agw = nc.dram_tensor("agw", [1024, 128], F32, kind="ExternalInput")
        io.agb = nc.dram_tensor("agb_t", [128, 8], F32, kind="ExternalInput")
        io.wr1 = nc.dram_tensor("wr1", [128, 64], F32, kind="ExternalInput")
        io.rb = nc.dram_tensor("rb", [1, 8], F32, kind="ExternalInput")
        io.w1w2_1 = nc.dram_tensor("w1w2_1", [64 * 3072, 512], F32, kind="ExternalInput")
        io.x4 = nc.dram_tensor("x4", [SPC * 1024, 4096], F32, kind="ExternalOutput")
        io.x1 = nc.dram_tensor("x1", [SPC * 1024, 4096], F32)
        io.x2 = nc.dram_tensor("x2", [SPC * 1024, 4096], F32)
        io.x3 = nc.dram_tensor("x3", [SPC * 1024, 4096], F32)
        io.wbf0 = nc.dram_tensor("wbf0", [66 * 2048, 512], BF16)
        io.wbf1 = nc.dram_tensor("wbf1", [66 * 3072, 512], BF16)
        io.cdr0 = nc.dram_tensor("cdr0", [8, 1024], F32)
        io.cdr1 = nc.dram_tensor("cdr1", [8, 1024], F32)
        io.bsc = nc.dram_tensor("bsc", [4, 4], F32)
        A = Arena(nc, C.st)
        p.sr = []
        for pref, defs in (("s1_", S1DEFS), ("s2_", S2DEFS),
                           ("s3_", S3DEFS), ("s4_", S4DEFS)):
            A.reset()
            d = NS()
            for nm, shape, dt in defs:
                d[nm] = A.sbuf(pref + nm, shape, dt)
            p.sr.append(d)
        A.allocate()
        for nm in ("pA", "pB", "pC", "pD", "pE", "pF", "pG", "pH"):
            h = C.psum(nm, [128, 512])
            for d in p.sr:
                d[nm] = h

    set_stage(p, 0)
    emit_l0attn(p, S)
    stage_barrier(p, S)
    set_stage(p, 1)
    emit_l0ffn(p, S)
    stage_barrier(p, S)
    set_stage(p, 2)
    emit_rglru(p, S)
    stage_barrier(p, S)
    set_stage(p, 3)
    emit_l1ffn(p, S)
    p.finish()
    p.wait("gpsimd", S.oX, S.oX.tok())


def host_full_inputs(inputs):
    d = {
        "wqkv": np.ascontiguousarray(inputs["L0_attn_Wqkv"]),
        "bqkv_t": np.ascontiguousarray(inputs["L0_attn_bqkv"].reshape(24, 128).T),
        "bqv": np.ascontiguousarray(inputs["L0_attn_bqkv"][2048:].reshape(1, 1024)),
        "wout": np.ascontiguousarray(inputs["L0_attn_Wout"]),
        "bout_t": np.ascontiguousarray(inputs["L0_attn_bout"].reshape(8, 128).T),
        "omega": np.ascontiguousarray(inputs["L0_attn_omega"]),
        "wr0": np.ascontiguousarray(
            inputs["L0_ffn_Wr"].reshape(8, 128, 8).transpose(1, 0, 2).reshape(128, 64)),
        "ap_t": np.ascontiguousarray(inputs["L1_attn_a_param"].reshape(8, 128).T),
        "igw": np.ascontiguousarray(inputs["L1_attn_ig_w"].reshape(1024, 128)),
        "igb_t": np.ascontiguousarray(inputs["L1_attn_ig_b"].reshape(8, 128).T),
        "agw": np.ascontiguousarray(inputs["L1_attn_ag_w"].reshape(1024, 128)),
        "agb_t": np.ascontiguousarray(inputs["L1_attn_ag_b"].reshape(8, 128).T),
        "wr1": np.ascontiguousarray(
            inputs["L1_ffn_router_w"].reshape(8, 128, 8).transpose(1, 0, 2).reshape(128, 64)),
        "rb": np.ascontiguousarray(inputs["L1_ffn_router_b"].reshape(1, 8)),
    }
    W1, W2 = inputs["L0_ffn_W1"], inputs["L0_ffn_W2"]
    chunks = []
    for e in range(8):
        for ncc in range(8):
            chunks.append(np.ascontiguousarray(W1[e][:, ncc * 512:(ncc + 1) * 512]))
            chunks.append(W2[e][ncc * 512:(ncc + 1) * 512, :].reshape(1024, 512))
    d["w1w2_0"] = np.ascontiguousarray(np.concatenate(chunks, axis=0))
    w1, w2 = inputs["L1_ffn_w1"], inputs["L1_ffn_w2"]
    chunks = []
    for e in range(8):
        for ncc in range(8):
            chunks.append(np.ascontiguousarray(w1[e][:, ncc * 512:(ncc + 1) * 512]))
            chunks.append(np.ascontiguousarray(w1[e][:, 4096 + ncc * 512: 4096 + (ncc + 1) * 512]))
            chunks.append(w2[e][ncc * 512:(ncc + 1) * 512, :].reshape(1024, 512))
    d["w1w2_1"] = np.ascontiguousarray(np.concatenate(chunks, axis=0))
    return d


_PROGS = {}


def _get_prog(name, fn):
    if name not in _PROGS:
        _PROGS[name] = build(fn)
    return _PROGS[name]


def kernel(**inputs):
    p = _get_prog("full", prog_full)
    cores = list(range(NCORE))
    shared = host_full_inputs(inputs)
    ims = []
    for c in cores:
        im = dict(shared)
        im["x"] = np.ascontiguousarray(
            inputs["L0_attn_x"][2 * c: 2 * c + 2].transpose(0, 2, 1)).reshape(SPC * 1024, 4096)
        ims.append(im)
    res = run_bass_kernel_spmd(p.nc, ims, core_ids=cores)

    out = np.empty((B, N, D), np.float32)
    for c in cores:
        x4 = np.asarray(res.results[c]["x4"]).reshape(SPC, 1024, 4096)
        out[2 * c: 2 * c + 2] = x4.transpose(0, 2, 1)
    return out
